# version 01
# baseline (speedup 1.0000x reference)
import numpy as np
import ml_dtypes
import concourse.bass as bass
import concourse.mybir as mybir
from concourse.bass_utils import run_bass_kernel_spmd

F32 = mybir.dt.float32
BF16 = mybir.dt.bfloat16
I32 = mybir.dt.int32
AF = mybir.ActivationFunctionType
ALU = mybir.AluOpType

NCORE = 8
P = 128
D = 2048
TLOC = 2048
KC = D // P
NT = 512
NTT = TLOC // NT
EPS_LN = 1e-5
EPS_ATT = 1e-6
NSEQ = 4096
DP = 4096
INTER = 5632
NH = 16
DK = 128

BF = ml_dtypes.bfloat16

WCONV = [
    ("wqT", D, D, "f32"), ("wkT", D, D, "f32"), ("wvT", D, D, "f32"),
    ("woT", D, D, "f32"),
    ("g0T", D, INTER, "f32"), ("u0T", D, INTER, "f32"), ("d0T", INTER, D, "f32"),
    ("win", D, 2 * DP, "f32"), ("wx", DP, 160, "f32"), ("wdt", P, DP, "f32"),
    ("wout", DP, D, "f32"),
    ("qgT", D, INTER, "i32"), ("quT", D, INTER, "i32"), ("qdT", INTER, D, "i32"),
]


class Prog:
    ENGINES = ("sync", "tensor", "vector", "scalar", "gpsimd")

    def __init__(self, nc, ctx):
        self.nc = nc
        self.ctx = ctx
        self.sems = {}
        self.ops = {e: [] for e in self.ENGINES}
        self.sem_count = {}
        self.stream_idx = {}
        self._uniq = 0
        self._last_ev = {}

    def _sem(self, key):
        if key not in self.sems:
            self.sems[key] = self.ctx.enter_context(self.nc.semaphore(key))
            self.sem_count[key] = 0
        return key

    def op(self, eng, fn, deps=(), sem=None, inc=0):
        deps = [d for d in deps if d is not None]
        if eng in ("vector", "scalar") and eng in self._last_ev:
            deps.append(self._last_ev[eng])
        ev = None
        if sem is not None:
            self._sem(sem)
            self.sem_count[sem] += inc
            ev = (sem, self.sem_count[sem])
        self.ops[eng].append((fn, tuple(deps), sem, inc))
        if eng in ("vector", "scalar") and ev is not None:
            self._last_ev[eng] = ev
        return ev

    def dma(self, fn, deps=(), stream=None, eng="sync", n=1):
        if stream is None:
            self._uniq += 1
            key = f"dU{self._uniq}"
        else:
            i = self.stream_idx.get(stream, 0)
            self.stream_idx[stream] = i + 1
            key = f"d_{stream}_{i % 2}"
        self._sem(key)
        if n == 1:
            return self.op(eng, fn, deps, key, 16)
        h = self.sems[key]
        return self.op(eng, lambda e, fn=fn, h=h: fn(e, h), deps, key, 16 * n)

    def barrier_ev(self, stream):
        out = []
        for par in (0, 1):
            key = f"d_{stream}_{par}"
            if key in self.sem_count and self.sem_count[key] > 0:
                out.append((key, self.sem_count[key]))
        return out

    def emit(self):
        sems = self.sems

        def body(eng_name):
            def f(eng):
                waited = {}
                for (fn, deps, sem, inc) in self.ops[eng_name]:
                    need = {}
                    for (dsem, dval) in deps:
                        if dval > need.get(dsem, -1):
                            need[dsem] = dval
                    for dsem, dval in need.items():
                        if waited.get(dsem, 0) >= dval:
                            continue
                        eng.wait_ge(sems[dsem], dval)
                        waited[dsem] = dval
                    inst = fn(eng)
                    if sem is not None and inst is not None:
                        inst.then_inc(sems[sem], inc)
            return f

        with self.nc.Block() as block:
            for e in self.ENGINES:
                if self.ops[e]:
                    getattr(block, e)(body(e))


def emit_ln(p, nc, ctx, tag, x_hbm, ncols, psums, out_fn, deps_start=()):
    sb = lambda n, sh, dt: ctx.enter_context(nc.sbuf_tensor(tag + n, sh, dt))
    ntt = ncols // NT if ncols % NT == 0 else None
    assert ntt is not None
    xt = sb("xt", [P, ncols], F32)
    sq = sb("sq", [P, ncols], F32)
    ones_col = sb("ones", [P, 1], F32)
    mrow = sb("mrow", [1, ncols], F32)
    vrow = sb("vrow", [1, ncols], F32)
    mbc = sb("mbc", [P, ncols], F32)
    rbc = sb("rbc", [P, ncols], F32)
    t1 = sb("t1", [P, ncols], F32)
    m2row = t1[0:1, 0:ncols]
    rrow = vrow
    srow = nc.dram_tensor(tag + "srow", [2, ncols], F32, kind="Internal").ap()

    ones_ev = p.op("vector", lambda v: v.memset(ones_col[:], 1.0), sem="VE", inc=1)
    xld, sqd, mm_done = [], [], []
    for k in range(KC):
        deps = list(deps_start)
        if k >= 1:
            deps += [mm_done[k - 1], sqd[k - 1]]
        b = 0
        xld.append(p.dma(lambda s, k=k, b=b: s.dma_start(
            out=xt[:, b:b + ncols], in_=x_hbm[k * P:(k + 1) * P, :]),
            deps=deps, stream=tag + "x"))
        sqd.append(p.op("scalar", lambda s, b=b: s.activation(
            sq[:, b:b + ncols], xt[:, b:b + ncols], AF.Square),
            deps=[xld[k]], sem="SE", inc=1))

        def mk_mm(k=k, b=b):
            def f(t):
                last = None
                for tt in range(ntt):
                    t.matmul(psums[tt][0:1, :], ones_col[:, 0:1],
                             xt[:, b + tt * NT:b + (tt + 1) * NT],
                             start=(k == 0), stop=(k == KC - 1))
                    last = t.matmul(psums[4 + tt][0:1, :], ones_col[:, 0:1],
                                    sq[:, b + tt * NT:b + (tt + 1) * NT],
                                    start=(k == 0), stop=(k == KC - 1))
                return last
            return f
        mm_done.append(p.op("tensor", mk_mm(), deps=[xld[k], sqd[k], ones_ev],
                            sem="TE", inc=1))

    def evac_stats(v):
        for tt in range(ntt):
            v.tensor_copy(mrow[0:1, tt * NT:(tt + 1) * NT], psums[tt][0:1, :])
        for tt in range(ntt - 1):
            v.tensor_copy(vrow[0:1, tt * NT:(tt + 1) * NT], psums[4 + tt][0:1, :])
        return v.tensor_copy(vrow[0:1, (ntt - 1) * NT:ntt * NT], psums[4 + ntt - 1][0:1, :])
    ev_ev = p.op("vector", evac_stats, deps=[mm_done[KC - 1]], sem="VE", inc=1)
    ev_mr = p.op("vector", lambda v: v.tensor_scalar_mul(mrow[:], mrow[:], 1.0 / D),
                 deps=[ev_ev], sem="VE", inc=1)
    ev_st = p.op("vector", lambda v: v.tensor_scalar_mul(vrow[:], vrow[:], 1.0 / D),
                 deps=[ev_ev], sem="VE", inc=1)
    ev_m2 = p.op("scalar", lambda s: s.activation(m2row, mrow[:], AF.Square),
                 deps=[ev_mr], sem="SE", inc=1)
    ev_v1 = p.op("vector", lambda v: v.tensor_sub(vrow[:], vrow[:], m2row),
                 deps=[ev_m2, ev_st], sem="VE", inc=1)
    ev_var = p.op("vector", lambda v: v.tensor_scalar_add(vrow[:], vrow[:], EPS_LN),
                  deps=[ev_v1], sem="VE", inc=1)
    ev_ln = p.op("scalar", lambda s: s.activation(vrow[:], vrow[:], AF.Ln),
                 deps=[ev_var], sem="SE", inc=1)
    ev_lh = p.op("vector", lambda v: v.tensor_scalar_mul(vrow[:], vrow[:], -0.5),
                 deps=[ev_ln], sem="VE", inc=1)
    ev_rstd = p.op("scalar", lambda s: s.activation(vrow[:], vrow[:], AF.Exp),
                   deps=[ev_lh], sem="SE", inc=1)
    ev_sr0 = p.dma(lambda s: s.dma_start(out=srow[0:1, :], in_=mrow[:]), deps=[ev_mr])
    ev_sr1 = p.dma(lambda s: s.dma_start(out=srow[1:2, :], in_=vrow[:]), deps=[ev_rstd])
    ev_mbc = p.dma(lambda s: s.dma_start(out=mbc[:], in_=srow[0:1, :].partition_broadcast(P)),
                   deps=[ev_sr0])
    ev_rbc = p.dma(lambda s: s.dma_start(out=rbc[:], in_=srow[1:2, :].partition_broadcast(P)),
                   deps=[ev_sr1])

    xh_evs, xwr = [], []
    for k in range(KC):
        b = 0
        deps = [mm_done[KC - 1], sqd[KC - 1]]
        if k >= 1:
            deps.append(ev_t1)
        ld2 = p.dma(lambda s, k=k, b=b: s.dma_start(
            out=xt[:, b:b + ncols], in_=x_hbm[k * P:(k + 1) * P, :]),
            deps=deps, stream=tag + "x2")
        out_sb, out_hbm = out_fn(k)

        deps2 = [ld2, ev_mbc]
        if k >= 1:
            deps2.append(ev_x)
        else:
            deps2.append(ev_v1)
        ev_t1 = p.op("vector", lambda v, b=b: v.tensor_sub(t1[:], xt[:, b:b + ncols], mbc[:]),
                     deps=deps2, sem="VE", inc=1)
        deps3 = [ev_t1, ev_rbc]
        if out_hbm is not None and k >= 2:
            deps3.append(xwr[k - 2])
        ev_x = p.op("vector", lambda v, out_sb=out_sb: v.tensor_mul(out_sb, t1[:], rbc[:]),
                    deps=deps3, sem="VE", inc=1)
        if out_hbm is not None:
            xwr.append(p.dma(lambda s, out_hbm=out_hbm, out_sb=out_sb: s.dma_start(
                out=out_hbm, in_=out_sb), deps=[ev_x], stream=tag + "xw"))
            xh_evs.append(xwr[-1])
        else:
            xh_evs.append(ev_x)
    return xh_evs


def build_A():
    nc = bass.Bass("TRN2", target_bir_lowering=False)
    x = nc.dram_tensor("x", [D, TLOC], F32, kind="ExternalInput").ap()
    xhat = nc.dram_tensor("xhat1", [D, TLOC], BF16, kind="ExternalOutput").ap()

    wc_in, wc_out = {}, {}
    for (nm, rows, cols, kd) in WCONV:
        rs = rows // NCORE
        wc_in[nm] = nc.dram_tensor(nm + "_s", [rs, cols], I32 if kd == "i32" else F32,
                                   kind="ExternalInput").ap()
        wc_out[nm] = nc.dram_tensor(nm + "_b", [rs, cols], BF16, kind="ExternalOutput").ap()
    CT = 2048
    conv_tiles = []
    for (nm, rows, cols, kd) in WCONV:
        rs = rows // NCORE
        for r0 in range(0, rs, P):
            nr = min(P, rs - r0)
            for c0 in range(0, cols, CT):
                conv_tiles.append((nm, r0, nr, c0, min(CT, cols - c0), kd))

    import contextlib
    with contextlib.ExitStack() as ctx:
        sb = lambda n, sh, dt: ctx.enter_context(nc.sbuf_tensor(n, sh, dt))
        xh = sb("xh", [P, 2 * TLOC], BF16)
        cvf = sb("cvf", [P, 2 * CT], F32)
        cvi = sb("cvi", [P, 2 * CT], I32)
        cvo = sb("cvo", [P, 2 * CT], BF16)
        psums = [ctx.enter_context(nc.psum_tensor(f"ps{i}", [P, NT], F32)) for i in range(8)]
        p = Prog(nc, ctx)
        wr_evs = []

        def out_fn(k):
            b = (k % 2) * TLOC
            return xh[:, b:b + TLOC], xhat[k * P:(k + 1) * P, :]
        xh_evs = emit_ln(p, nc, ctx, "l1", x, TLOC, psums, out_fn)
        wr_evs += xh_evs

        cvd, cwr = [], []
        for i, (nm, r0, nr, c0, ncol, kd) in enumerate(conv_tiles):
            b = (i % 2) * CT
            deps = []
            if i >= 2:
                deps.append(cvd[i - 2])
            src_ap = wc_in[nm][r0:r0 + nr, c0:c0 + ncol]
            dst = (cvi if kd == "i32" else cvf)[0:nr, b:b + ncol]
            ld = p.dma(lambda s, src_ap=src_ap, dst=dst: s.dma_start(out=dst, in_=src_ap),
                       deps=deps, stream="cv")

            def mk_cv(nm=nm, nr=nr, ncol=ncol, b=b, kd=kd):
                def f(s):
                    if kd == "i32":
                        return s.activation(cvo[0:nr, b:b + ncol], cvi[0:nr, b:b + ncol],
                                            AF.Copy, bias=-128.0)
                    return s.activation(cvo[0:nr, b:b + ncol], cvf[0:nr, b:b + ncol], AF.Copy)
                return f
            deps_c = [ld]
            if i >= 2:
                deps_c.append(cwr[i - 2])
            cvd.append(p.op("scalar", mk_cv(), deps=deps_c, sem="SE", inc=1))
            cwr.append(p.dma(lambda s, nm=nm, r0=r0, nr=nr, c0=c0, ncol=ncol, b=b: s.dma_start(
                out=wc_out[nm][r0:r0 + nr, c0:c0 + ncol], in_=cvo[0:nr, b:b + ncol]),
                deps=[cvd[i]], stream="cw"))
        wr_evs += cwr

        p.op("sync", lambda s: None, deps=wr_evs)
        p.emit()
    return nc


_NC_CACHE = {}


def _get(name, builder):
    if name not in _NC_CACHE:
        _NC_CACHE[name] = builder()
    return _NC_CACHE[name]


def _shard_x(xfull):
    out = []
    for c in range(NCORE):
        b, h = c // 2, c % 2
        sl = xfull[b, h * TLOC:(h + 1) * TLOC, :]
        out.append(np.ascontiguousarray(sl.T, dtype=np.float32))
    return out


def run_A(inp, xcore):
    nc = _get("A", build_A)
    wq = np.ascontiguousarray(inp["L0_attn_Wq"].T)
    wk = np.ascontiguousarray(inp["L0_attn_Wk"].T)
    wv = np.ascontiguousarray(inp["L0_attn_Wv"].T)
    wo = np.ascontiguousarray(inp["L0_attn_Wo"].T)
    g0 = np.ascontiguousarray(inp["L0_ffn_gate_w"].T)
    u0 = np.ascontiguousarray(inp["L0_ffn_up_w"].T)
    d0 = np.ascontiguousarray(inp["L0_ffn_down_w"].T)
    win = np.ascontiguousarray(inp["L1_attn_W_in"])
    wx = np.ascontiguousarray(inp["L1_attn_W_x"])
    wdt = np.ascontiguousarray(inp["L1_attn_W_dt"])
    wout = np.ascontiguousarray(inp["L1_attn_W_out"])
    qg = np.ascontiguousarray(inp["L1_ffn_wq_gate"].T)
    qu = np.ascontiguousarray(inp["L1_ffn_wq_up"].T)
    qd = np.ascontiguousarray(inp["L1_ffn_wq_down"].T)
    full = {"wqT": wq, "wkT": wk, "wvT": wv, "woT": wo, "g0T": g0, "u0T": u0,
            "d0T": d0, "win": win, "wx": wx, "wdt": wdt, "wout": wout,
            "qgT": qg, "quT": qu, "qdT": qd}
    in_maps = []
    for c in range(NCORE):
        m = {"x": xcore[c]}
        for (nm, rows, cols, kd) in WCONV:
            rs = rows // NCORE
            m[nm + "_s"] = np.ascontiguousarray(full[nm][c * rs:(c + 1) * rs])
        in_maps.append(m)
    res = run_bass_kernel_spmd(nc, in_maps, core_ids=list(range(NCORE)))
    wb = {}
    for (nm, rows, cols, kd) in WCONV:
        wb[nm] = np.concatenate([res.results[c][nm + "_b"] for c in range(NCORE)], axis=0)
    xhat1 = [res.results[c]["xhat1"] for c in range(NCORE)]
    return xhat1, wb


def kernel(**inp):
    xcore = _shard_x(np.asarray(inp["L0_attn_x"], dtype=np.float32))
    xhat1, wb = run_A(inp, xcore)
    raise NotImplementedError("launches B..F pending")


def build_B():
    nc = bass.Bass("TRN2", target_bir_lowering=False)
    xhat = nc.dram_tensor("xhat1", [D, TLOC], BF16, kind="ExternalInput").ap()
    wq_arr = nc.dram_tensor("wq_arr", [D, D], BF16, kind="ExternalInput").ap()
    wkT = nc.dram_tensor("wkT", [D, D], BF16, kind="ExternalInput").ap()
    wvT = nc.dram_tensor("wvT", [D, D], BF16, kind="ExternalInput").ap()
    bq = nc.dram_tensor("bq", [1, D], F32, kind="ExternalInput").ap()
    bk = nc.dram_tensor("bk", [1, D], F32, kind="ExternalInput").ap()
    bv = nc.dram_tensor("bv", [1, D], F32, kind="ExternalInput").ap()
    pos_row = nc.dram_tensor("pos_row", [1, TLOC], F32, kind="ExternalInput").ap()
    pos_col = nc.dram_tensor("pos_col", [P, KC], F32, kind="ExternalInput").ap()
    qout = nc.dram_tensor("q", [D, TLOC], BF16, kind="ExternalOutput").ap()
    kvp = nc.dram_tensor("kvp", [P, NH * 129], F32, kind="ExternalOutput").ap()
    drow_h = nc.dram_tensor("drow_h", [1, TLOC], F32, kind="Internal").ap()

    import contextlib
    with contextlib.ExitStack() as ctx:
        sbt = lambda n, sh, dt: ctx.enter_context(nc.sbuf_tensor(n, sh, dt))
        arena = sbt("arena", [P, 2 * KC * TLOC], BF16)
        wkb = arena[:, 0:KC * TLOC]
        wvb = arena[:, KC * TLOC:2 * KC * TLOC]
        xh_tc = sbt("xh_tc", [P, 2 * D], BF16)
        ktile = sbt("ktile", [P, D], BF16)
        VW = NH * 129
        vtile = sbt("vtile", [P, VW], BF16)
        kvsb = sbt("kvsb", [P, NH * 129], F32)
        wq_st = sbt("wq_st", [P, 2 * D], BF16)
        qtile = sbt("qtile", [P, TLOC], BF16)
        brow = sbt("brow", [1, 3 * D], BF16)
        brow_f = sbt("brow_f", [1, D], F32)
        ones_row = sbt("ones_row", [1, NT], BF16)
        posc = sbt("posc", [P, KC], F32)
        drow = sbt("drow", [1, TLOC], F32)
        dcols = sbt("dcols", [P, KC], F32)
        dbc = sbt("dbc", [P, TLOC], F32)
        ps = [ctx.enter_context(nc.psum_tensor(f"ps{i}", [P, NT], F32)) for i in range(8)]
        p = Prog(nc, ctx)
        wr_evs = []

        ev_ones = p.op("vector", lambda v: v.memset(ones_row[:], 1.0), sem="VE", inc=1)
        ev_vones = p.op("vector", lambda v: v.memset(vtile[:], 1.0), sem="VE", inc=1)

        ev_bb = None
        for i, bsrc in enumerate((bq, bk, bv)):
            ld_bi = p.dma(lambda s, bsrc=bsrc: s.dma_start(out=brow_f[:], in_=bsrc[:]),
                          deps=[ev_bb], stream="bia")
            ev_bb = p.op("scalar", lambda s, i=i: s.activation(
                brow[0:1, i * D:(i + 1) * D], brow_f[:], AF.Copy),
                deps=[ld_bi], sem="SE", inc=1)

        ld_pr = p.dma(lambda s: s.dma_start(out=drow[:], in_=pos_row[:]))
        ld_pc = p.dma(lambda s: s.dma_start(out=posc[:], in_=pos_col[:]))
        ev_dr0 = p.op("vector", lambda v: v.tensor_scalar_mul(drow[:], drow[:], -1.0 / NSEQ),
                      deps=[ld_pr], sem="VE", inc=1)
        ev_dr = p.op("scalar", lambda s: s.activation(drow[:], drow[:], AF.Exp),
                     deps=[ev_dr0], sem="SE", inc=1)
        ev_dc0 = p.op("vector", lambda v: v.tensor_scalar_mul(dcols[:], posc[:], -1.0 / NSEQ),
                      deps=[ld_pc], sem="VE", inc=1)
        ev_dc = p.op("scalar", lambda s: s.activation(dcols[:], dcols[:], AF.Exp),
                     deps=[ev_dc0], sem="SE", inc=1)
        ev_dwr = p.dma(lambda s: s.dma_start(out=drow_h[:], in_=drow[:]), deps=[ev_dr])
        ev_dbc = p.dma(lambda s: s.dma_start(
            out=dbc[:], in_=drow_h.partition_broadcast(P)), deps=[ev_dwr])

        wkv_evs = []
        for k in range(KC):
            wkv_evs.append(p.dma(lambda s, k=k: s.dma_start(
                out=arena[:, k * TLOC:(k + 1) * TLOC], in_=wkT[k * P:(k + 1) * P, :]),
                stream="wkv"))
            wkv_evs.append(p.dma(lambda s, k=k: s.dma_start(
                out=arena[:, KC * TLOC + k * TLOC:KC * TLOC + (k + 1) * TLOC],
                in_=wvT[k * P:(k + 1) * P, :]), stream="wkv"))

        NB = 129
        kv_mm, xld = [], []
        ev_evK, ev_evV, kv_add = {}, {}, {}
        for tc in range(KC):
            bx = (tc % 2) * D
            deps = [] if tc < 2 else [kv_mm[tc - 2]]

            def mk_ldx(tc=tc, bx=bx):
                def f(s, h):
                    for k in range(KC):
                        s.dma_start(out=xh_tc[:, bx + k * P:bx + (k + 1) * P],
                                    in_=xhat[k * P:(k + 1) * P, tc * P:(tc + 1) * P]
                                    ).then_inc(h, 16)
                    return None
                return f
            xld.append(p.dma(mk_ldx(), deps=deps, stream="xh", n=KC))

            for dtile in range(4):
                def mk_kv(tc=tc, bx=bx, dtile=dtile):
                    def f(t):
                        t.matmul(ps[0][:, :], ones_row[0:1, 0:P],
                                 brow[0:1, D + dtile * NT:D + (dtile + 1) * NT],
                                 start=True, stop=False)
                        t.matmul(ps[1][:, :], ones_row[0:1, 0:P],
                                 brow[0:1, 2 * D + dtile * NT:2 * D + (dtile + 1) * NT],
                                 start=True, stop=False)
                        last = None
                        for k in range(KC):
                            lhs = xh_tc[:, bx + k * P:bx + (k + 1) * P]
                            t.matmul(ps[0][:, :], lhs,
                                     wkb[:, k * TLOC + dtile * NT:k * TLOC + (dtile + 1) * NT],
                                     start=False, stop=(k == KC - 1))
                            last = t.matmul(ps[1][:, :], lhs,
                                            wvb[:, k * TLOC + dtile * NT:k * TLOC + (dtile + 1) * NT],
                                            start=False, stop=(k == KC - 1))
                        return last
                    return f
                deps_mm = [xld[tc], ev_bb, ev_ones] + wkv_evs
                if dtile > 0:
                    deps_mm += [ev_evK[(tc, dtile - 1)], ev_evV[(tc, dtile - 1)]]
                elif tc > 0:
                    deps_mm += [ev_evK[(tc - 1, 3)], ev_evV[(tc - 1, 3)]]
                if tc == 0 and dtile == 0:
                    deps_mm.append(ev_vones)
                ev_mm = p.op("tensor", mk_kv(), deps=deps_mm, sem="TE", inc=1)

                def mk_evK(tc=tc, dtile=dtile):
                    def f(v):
                        return v.tensor_scalar(
                            ktile[:, dtile * NT:(dtile + 1) * NT], ps[0][:, :],
                            0.0, dcols[:, tc:tc + 1], ALU.max, ALU.mult)
                    return f
                deps_ek = [ev_mm, ev_dc]
                if tc > 0:
                    deps_ek.append(kv_mm[tc - 1])
                ev_evK[(tc, dtile)] = p.op("vector", mk_evK(), deps=deps_ek,
                                           sem="VE", inc=1)

                def mk_evV(tc=tc, dtile=dtile):
                    def f(v):
                        h0 = dtile * 4
                        out = bass.AP(vtile, h0 * NB, [[VW, P], [NB, 4], [1, P]])
                        return v.tensor_copy(out, ps[1][:, :])
                    return f
                deps_ev = [ev_mm]
                if tc > 0:
                    deps_ev.append(kv_mm[tc - 1])
                ev_evV[(tc, dtile)] = p.op("vector", mk_evV(), deps=deps_ev,
                                           sem="VE", inc=1)

            for h in range(NH):
                pb = 2 + (h % 2)

                def mk_kvh(tc=tc, h=h, pb=pb):
                    def f(t):
                        return t.matmul(ps[pb][:, 0:NB], ktile[:, h * P:(h + 1) * P],
                                        vtile[:, h * NB:(h + 1) * NB],
                                        start=True, stop=True)
                    return f
                deps_h = [ev_evK[(tc, 3)], ev_evV[(tc, 3)]]
                if h >= 2:
                    deps_h.append(kv_add[(tc, h - 2)])
                elif tc > 0:
                    deps_h.append(kv_add[(tc - 1, NH - 2 + h)])
                kv_h = p.op("tensor", mk_kvh(), deps=deps_h, sem="TE", inc=1)

                def mk_add(tc=tc, h=h, pb=pb):
                    def f(v):
                        if tc == 0:
                            return v.tensor_copy(kvsb[:, h * NB:(h + 1) * NB], ps[pb][:, 0:NB])
                        return v.tensor_add(kvsb[:, h * NB:(h + 1) * NB],
                                            kvsb[:, h * NB:(h + 1) * NB], ps[pb][:, 0:NB])
                    return f
                kv_add[(tc, h)] = p.op("vector", mk_add(), deps=[kv_h], sem="VE", inc=1)
            kv_mm.append(kv_add[(tc, NH - 1)])

        ev_evkv = kv_add[(KC - 1, NH - 1)]
        kv_wrs = [p.dma(lambda s: s.dma_start(out=kvp[:], in_=kvsb[:]),
                        deps=[ev_evkv], stream="kvw")]
        wr_evs += kv_wrs

        xres_evs = []
        for k in range(KC):
            xres_evs.append(p.dma(lambda s, k=k: s.dma_start(
                out=arena[:, k * TLOC:(k + 1) * TLOC], in_=xhat[k * P:(k + 1) * P, :]),
                deps=[kv_mm[KC - 1]], stream="xr"))
        q_wr, qmm, ev_evq = [], [], []
        for g in range(KC):
            bs = (g % 2) * D
            bq_t = 0
            deps_st = [] if g < 2 else [qmm[g - 2]]
            st_ld = p.dma(lambda s, g=g, bs=bs: s.dma_start(
                out=wq_st[:, bs:bs + D], in_=wq_arr[g * P:(g + 1) * P, :]),
                deps=deps_st, stream="qst")

            def mk_qmm(g=g, bs=bs):
                def f(t):
                    last = None
                    for tt in range(NTT):
                        t.matmul(ps[tt][:, :], brow[0:1, g * P:(g + 1) * P],
                                 ones_row[0:1, 0:NT], start=True, stop=False)
                    for k in range(KC):
                        for tt in range(NTT):
                            last = t.matmul(ps[tt][:, :], wq_st[:, bs + k * P:bs + (k + 1) * P],
                                            arena[:, k * TLOC + tt * NT:k * TLOC + (tt + 1) * NT],
                                            start=False, stop=(k == KC - 1))
                    return last
                return f
            deps_q = [st_ld, ev_bb] + xres_evs
            if g >= 1:
                deps_q.append(ev_evq[g - 1])
            if g == 0:
                deps_q.append(ev_evkv)
            qmm.append(p.op("tensor", mk_qmm(), deps=deps_q, sem="TE", inc=1))

            def mk_evq(g=g, bq_t=bq_t):
                def f(v):
                    last = None
                    for tt in range(NTT):
                        last = v.scalar_tensor_tensor(
                            qtile[:, bq_t + tt * NT:bq_t + (tt + 1) * NT],
                            ps[tt][:, :], 0.0, dbc[:, tt * NT:(tt + 1) * NT],
                            ALU.max, ALU.mult)
                    return last
                return f
            deps_eq = [qmm[g], ev_dbc]
            if g >= 1:
                deps_eq.append(q_wr[g - 1])
            ev_evq.append(p.op("vector", mk_evq(), deps=deps_eq, sem="VE", inc=1))
            q_wr.append(p.dma(lambda s, g=g, bq_t=bq_t: s.dma_start(
                out=qout[g * P:(g + 1) * P, :], in_=qtile[:, bq_t:bq_t + TLOC]),
                deps=[ev_evq[g]], stream="qw"))
        wr_evs += q_wr

        p.op("sync", lambda s: None, deps=wr_evs)
        p.emit()
    return nc


def build_C1():
    nc = bass.Bass("TRN2", target_bir_lowering=False)
    x = nc.dram_tensor("x", [D, TLOC], F32, kind="ExternalInput").ap()
    q = nc.dram_tensor("q", [D, TLOC], BF16, kind="ExternalInput").ap()
    kvp_s = nc.dram_tensor("kvp_s", [P, NH * 129], F32, kind="ExternalInput").ap()
    kvp_p = nc.dram_tensor("kvp_p", [P, NH * 129], F32, kind="ExternalInput").ap()
    wo_arr = nc.dram_tensor("wo_arr", [D, D], BF16, kind="ExternalInput").ap()
    bo = nc.dram_tensor("bo", [1, D], F32, kind="ExternalInput").ap()
    x2o = nc.dram_tensor("x2", [D, TLOC], F32, kind="ExternalOutput").ap()
    zh = nc.dram_tensor("zh", [NH, TLOC], F32, kind="Internal").ap()
    NB = 129

    import contextlib
    with contextlib.ExitStack() as ctx:
        sbt = lambda n, sh, dt: ctx.enter_context(nc.sbuf_tensor(n, sh, dt))
        arena = sbt("arena", [P, KC * TLOC], BF16)
        kv_f = sbt("kv_f", [P, NH * NB], F32)
        kv2 = sbt("kv2", [P, NH * NB], F32)
        kvb = sbt("kvb", [P, NH * NB], BF16)
        zb = sbt("zb", [P, 2 * TLOC], F32)
        dnrow = sbt("dnrow", [1, TLOC], F32)
        wst = sbt("wst", [P, 2 * D], BF16)
        xrt = sbt("xrt", [P, 2 * TLOC], F32)
        x2t = sbt("x2t", [P, 2 * NT], F32)
        brow_f = sbt("brow_f", [1, D], F32)
        bo_b = sbt("bo_b", [1, D], BF16)
        ones_row = sbt("ones_row", [1, NT], BF16)
        ps = [ctx.enter_context(nc.psum_tensor(f"ps{i}", [P, NT], F32)) for i in range(6)]
        p = Prog(nc, ctx)
        wr_evs = []

        ev_ones = p.op("vector", lambda v: v.memset(ones_row[:], 1.0), sem="VE", inc=1)
        ld_bo = p.dma(lambda s: s.dma_start(out=brow_f[:], in_=bo[:]))
        ev_bo = p.op("scalar", lambda s: s.activation(bo_b[:], brow_f[:], AF.Copy),
                     deps=[ld_bo], sem="SE", inc=1)

        ld_kv1 = p.dma(lambda s: s.dma_start(out=kv_f[:], in_=kvp_s[:]))
        ld_kv2 = p.dma(lambda s: s.dma_start(out=kv2[:], in_=kvp_p[:]))
        ev_kvs = p.op("vector", lambda v: v.tensor_add(kv_f[:], kv_f[:], kv2[:]),
                      deps=[ld_kv1, ld_kv2], sem="VE", inc=1)
        ev_kvb = p.op("scalar", lambda s: s.activation(kvb[:], kv_f[:], AF.Copy),
                      deps=[ev_kvs], sem="SE", inc=1)

        q_evs = [p.dma(lambda s, k=k: s.dma_start(
            out=arena[:, k * TLOC:(k + 1) * TLOC], in_=q[k * P:(k + 1) * P, :]),
            stream="q") for k in range(KC)]

        z_wr = []
        ev_dn_prev = None
        dn_ev_evac = {}
        for h in range(NH):
            ev_tt_mm = []
            for tt in range(NTT):
                pb = 4 + (tt % 2)

                def mk_dn1(h=h, tt=tt, pb=pb):
                    def f(t):
                        return t.matmul(ps[pb][0:1, :],
                                        kvb[:, h * NB + 128:h * NB + NB],
                                        arena[:, h * TLOC + tt * NT:h * TLOC + (tt + 1) * NT],
                                        start=True, stop=True)
                    return f
                deps_d = [ev_kvb] + q_evs
                key_prev = (h, tt - 2) if tt >= 2 else ((h - 1, tt + 2) if h > 0 else None)
                if key_prev is not None and key_prev in dn_ev_evac:
                    deps_d.append(dn_ev_evac[key_prev])
                ev_mm = p.op("tensor", mk_dn1(), deps=deps_d, sem="TE", inc=1)
                ev_tt_mm.append(ev_mm)
                deps_ev = [ev_mm]
                if h > 0:
                    deps_ev.append(z_wr[h - 1])
                dn_ev_evac[(h, tt)] = p.op("vector", lambda v, tt=tt, pb=pb: v.tensor_copy(
                    dnrow[0:1, tt * NT:(tt + 1) * NT], ps[pb][0:1, :]),
                    deps=deps_ev, sem="VE", inc=1)

            deps_z0 = [dn_ev_evac[(h, tt)] for tt in range(NTT)]
            if h > 0:
                deps_z0.append(z_wr[h - 1])
            ev_z0 = p.op("vector", lambda v: v.tensor_scalar_add(dnrow[:], dnrow[:], EPS_ATT),
                         deps=deps_z0, sem="VE", inc=1)
            ev_z = p.op("vector", lambda v: v.reciprocal(dnrow[:], dnrow[:]),
                        deps=[ev_z0], sem="VE", inc=1)
            z_wr.append(p.dma(lambda s, h=h: s.dma_start(
                out=zh[h:h + 1, :], in_=dnrow[:]), deps=[ev_z], stream="zw"))
            ev_dn_prev = ev_z
        ev_dn_all = ev_dn_prev

        att_evs = []
        zld = []
        for h in range(NH):
            bz = (h % 2) * TLOC
            deps_z = [z_wr[h]]
            if h >= 2:
                deps_z.append(att_evs[h - 2])
            zld.append(p.dma(lambda s, h=h, bz=bz: s.dma_start(
                out=zb[:, bz:bz + TLOC], in_=zh[h:h + 1, :].partition_broadcast(P)),
                deps=deps_z, stream="zb"))
            ev_prev_tt = None
            for tt in range(NTT):
                pb = tt % 2

                def mk_am(h=h, tt=tt, pb=pb):
                    def f(t):
                        return t.matmul(ps[pb][:, :], kvb[:, h * NB:h * NB + 128],
                                        arena[:, h * TLOC + tt * NT:h * TLOC + (tt + 1) * NT],
                                        start=True, stop=True)
                    return f
                deps_am = [ev_kvb, ev_dn_all]
                if ev_prev_tt is not None:
                    deps_am.append(ev_prev_tt)
                if tt >= 2:
                    deps_am.append(ev_ev_tt[tt - 2])
                elif h > 0:
                    deps_am.append(prev_ev_ev[tt + 2])
                ev_am = p.op("tensor", mk_am(), deps=[d for d in deps_am if d],
                             sem="TE", inc=1)

                def mk_ea(h=h, tt=tt, pb=pb, bz=bz):
                    def f(v):
                        return v.tensor_mul(
                            arena[:, h * TLOC + tt * NT:h * TLOC + (tt + 1) * NT],
                            ps[pb][:, :], zb[:, bz + tt * NT:bz + (tt + 1) * NT])
                    return f
                if tt == 0:
                    ev_ev_tt = [None] * NTT
                ev_ev_tt[tt] = p.op("vector", mk_ea(), deps=[ev_am, zld[h]],
                                    sem="VE", inc=1)
                ev_prev_tt = ev_am
            prev_ev_ev = list(ev_ev_tt)
            att_evs.append(ev_ev_tt[NTT - 1])
        att_all = [e for e in prev_ev_ev] + att_evs

        x2_wr, womm, ev_evo = [], [], []
        xr_ld = []
        for g in range(KC):
            bs = (g % 2) * D
            bx = (g % 2) * TLOC
            deps_st = [] if g < 2 else [womm[g - 2]]
            st_ld = p.dma(lambda s, g=g, bs=bs: s.dma_start(
                out=wst[:, bs:bs + D], in_=wo_arr[g * P:(g + 1) * P, :]),
                deps=deps_st, stream="wst")
            deps_xr = [] if g < 2 else [ev_evo[g - 2]]
            xr_ld.append(p.dma(lambda s, g=g, bx=bx: s.dma_start(
                out=xrt[:, bx:bx + TLOC], in_=x[g * P:(g + 1) * P, :]),
                deps=deps_xr, stream="xr"))

            ev_o_tts = []
            for tt in range(NTT):
                pb = 2 + (tt % 2)

                def mk_wog(g=g, bs=bs, tt=tt, pb=pb):
                    def f(t):
                        t.matmul(ps[pb][:, :], bo_b[0:1, g * P:(g + 1) * P],
                                 ones_row[0:1, 0:NT], start=True, stop=False)
                        last = None
                        for k in range(KC):
                            last = t.matmul(ps[pb][:, :], wst[:, bs + k * P:bs + (k + 1) * P],
                                            arena[:, k * TLOC + tt * NT:k * TLOC + (tt + 1) * NT],
                                            start=False, stop=(k == KC - 1))
                        return last
                    return f
                deps_wo = [st_ld, ev_bo, ev_ones] + att_all
                if tt >= 2:
                    deps_wo.append(ev_o_tts[tt - 2])
                elif g > 0:
                    deps_wo.append(prev_o_tts[tt])
                mm_ev = p.op("tensor", mk_wog(), deps=deps_wo, sem="TE", inc=1)
                if tt == NTT - 1:
                    womm.append(mm_ev)

                def mk_eo(g=g, tt=tt, pb=pb, bx=bx):
                    def f(v):
                        bt2 = (tt % 2) * NT
                        return v.tensor_add(x2t[:, bt2:bt2 + NT], ps[pb][:, :],
                                            xrt[:, bx + tt * NT:bx + (tt + 1) * NT])
                    return f
                deps_eo = [mm_ev, xr_ld[g]]
                if len(x2_wr) >= 2:
                    deps_eo.append(x2_wr[-2])
                ev_e = p.op("vector", mk_eo(), deps=deps_eo, sem="VE", inc=1)
                ev_o_tts.append(ev_e)
                x2_wr.append(p.dma(lambda s, g=g, tt=tt: s.dma_start(
                    out=x2o[g * P:(g + 1) * P, tt * NT:(tt + 1) * NT],
                    in_=x2t[:, (tt % 2) * NT:(tt % 2) * NT + NT]),
                    deps=[ev_e], stream="x2w"))
            prev_o_tts = ev_o_tts
            ev_evo.append(ev_o_tts[NTT - 1])
        wr_evs += x2_wr

        p.op("sync", lambda s: None, deps=wr_evs)
        p.emit()
    return nc


def build_C2():
    nc = bass.Bass("TRN2", target_bir_lowering=False)
    x2 = nc.dram_tensor("x2", [D, TLOC], F32, kind="ExternalInput").ap()
    g_arr = nc.dram_tensor("g_arr", [INTER, D], BF16, kind="ExternalInput").ap()
    u_arr = nc.dram_tensor("u_arr", [INTER, D], BF16, kind="ExternalInput").ap()
    d_arr = nc.dram_tensor("d_arr", [D, INTER], BF16, kind="ExternalInput").ap()
    gs_c = nc.dram_tensor("gs_c", [P, 44], F32, kind="ExternalInput").ap()
    us_c = nc.dram_tensor("us_c", [P, 44], F32, kind="ExternalInput").ap()
    ds_c = nc.dram_tensor("ds_c", [P, KC], F32, kind="ExternalInput").ap()
    x3o = nc.dram_tensor("x3", [D, TLOC], F32, kind="ExternalOutput").ap()
    NI = INTER // P

    import contextlib
    with contextlib.ExitStack() as ctx:
        sbt = lambda n, sh, dt: ctx.enter_context(nc.sbuf_tensor(n, sh, dt))
        arena = sbt("arena", [P, KC * TLOC], BF16)
        hdn = sbt("hdn", [P, NI * NT], BF16)
        wpool = sbt("wpool", [P, 2 * INTER], BF16)
        gst = wpool[:, 0:2 * D]
        ust = wpool[:, 2 * D:4 * D]
        dst = wpool[:, 0:2 * INTER]
        gt = sbt("gt", [P, NT], BF16)
        ut = sbt("ut", [P, NT], BF16)
        x2r = sbt("x2r", [P, 2 * NT], F32)
        x3t = sbt("x3t", [P, 2 * NT], F32)
        gsc = sbt("gsc", [P, NI], F32)
        usc = sbt("usc", [P, NI], F32)
        dsc = sbt("dsc", [P, KC], F32)
        ps = [ctx.enter_context(nc.psum_tensor(f"ps{i}", [P, NT], F32)) for i in range(8)]
        p = Prog(nc, ctx)
        wr_evs = []

        ld_gs = p.dma(lambda s: s.dma_start(out=gsc[:], in_=gs_c[:]))
        ld_us = p.dma(lambda s: s.dma_start(out=usc[:], in_=us_c[:]))
        ld_ds = p.dma(lambda s: s.dma_start(out=dsc[:], in_=ds_c[:]))

        def out_fn(k):
            return arena[:, k * TLOC:(k + 1) * TLOC], None
        xh_evs = emit_ln(p, nc, ctx, "l2", x2, TLOC, ps, out_fn)

        x3_wr = []
        last_down_of_tt = {}
        hd_evs = {}
        for tt in range(NTT):
            for i in range(NI):
                bgs = (i % 2) * D

                def mk_ldg(i=i, bgs=bgs):
                    def f(s):
                        return s.dma_start(out=gst[:, bgs:bgs + D],
                                           in_=g_arr[i * P:(i + 1) * P, :])
                    return f

                def mk_ldu(i=i, bgs=bgs):
                    def f(s):
                        return s.dma_start(out=ust[:, bgs:bgs + D],
                                           in_=u_arr[i * P:(i + 1) * P, :])
                    return f
                idx = tt * NI + i
                deps_l = [] if idx < 2 else [gmm_evs[idx - 2]]
                if tt > 0 and i < 2:
                    deps_l = deps_l + [dmm_evs[tt * KC - 1]]
                ld_g = p.dma(mk_ldg(), deps=deps_l, stream="gst")
                ld_u = p.dma(mk_ldu(), deps=deps_l, stream="ust")

                def mk_gu(i=i, tt=tt, bgs=bgs):
                    def f(t):
                        last = None
                        for k in range(KC):
                            t.matmul(ps[0][:, :], gst[:, bgs + k * P:bgs + (k + 1) * P],
                                     arena[:, k * TLOC + tt * NT:k * TLOC + (tt + 1) * NT],
                                     start=(k == 0), stop=(k == KC - 1))
                            last = t.matmul(ps[1][:, :], ust[:, bgs + k * P:bgs + (k + 1) * P],
                                            arena[:, k * TLOC + tt * NT:k * TLOC + (tt + 1) * NT],
                                            start=(k == 0), stop=(k == KC - 1))
                        return last
                    return f
                deps_g = [ld_g, ld_u] + xh_evs
                if idx == 0:
                    gmm_evs = []
                if idx >= 1:
                    deps_g.append(ev_hd)
                gmm_evs.append(p.op("tensor", mk_gu(), deps=deps_g, sem="TE", inc=1))

                ev_g = p.op("scalar", lambda s, i=i: s.activation(
                    gt[:], ps[0][:, :], AF.Silu, scale=gsc[:, i:i + 1]),
                    deps=[gmm_evs[idx], ld_gs], sem="SE", inc=1)
                ev_u = p.op("vector", lambda v, i=i: v.tensor_scalar_mul(
                    ut[:], ps[1][:, :], usc[:, i:i + 1]),
                    deps=[gmm_evs[idx], ld_us], sem="VE", inc=1)

                def mk_hd(i=i, tt=tt):
                    def f(v):
                        return v.tensor_mul(hdn[:, i * NT:(i + 1) * NT], gt[:], ut[:])
                    return f
                deps_h = [ev_g, ev_u]
                if tt > 0:
                    deps_h.append(last_down_of_tt[tt - 1])
                ev_hd = p.op("vector", mk_hd(), deps=deps_h, sem="VE", inc=1)
                hd_evs[(tt, i)] = ev_hd

            for g in range(KC):
                bds = (g % 2) * INTER
                idx2 = tt * KC + g
                deps_ld = [] if idx2 < 2 else [dmm_evs[idx2 - 2]]
                if g < 2:
                    deps_ld = deps_ld + [gmm_evs[tt * NI + NI - 1],
                                         hd_evs[(tt, NI - 1)], hd_evs[(tt, NI - 2)]]
                ld_d = p.dma(lambda s, g=g, bds=bds: s.dma_start(
                    out=dst[:, bds:bds + INTER], in_=d_arr[g * P:(g + 1) * P, :]),
                    deps=deps_ld, stream="dst")
                deps_x2 = [] if idx2 < 2 else [x3_wr[idx2 - 2]]
                ld_x2 = p.dma(lambda s, g=g, tt=tt, bds=bds: s.dma_start(
                    out=x2r[:, (g % 2) * NT:(g % 2) * NT + NT],
                    in_=x2[g * P:(g + 1) * P, tt * NT:(tt + 1) * NT]),
                    deps=deps_x2, stream="x2r")

                def mk_dn(g=g, tt=tt, bds=bds):
                    def f(t):
                        pb = 2 + (g % 2)
                        last = None
                        for i in range(NI):
                            last = t.matmul(ps[pb][:, :], dst[:, bds + i * P:bds + (i + 1) * P],
                                            hdn[:, i * NT:(i + 1) * NT],
                                            start=(i == 0), stop=(i == NI - 1))
                        return last
                    return f
                deps_d = [ld_d] + [hd_evs[(tt, i)] for i in range(NI)]
                if idx2 == 0:
                    dmm_evs = []
                if idx2 >= 2:
                    deps_d.append(ev_x3s[idx2 - 2])
                dmm_evs.append(p.op("tensor", mk_dn(), deps=deps_d, sem="TE", inc=1))

                def mk_x3(g=g, tt=tt):
                    def f(v):
                        pb = 2 + (g % 2)
                        bo2 = (g % 2) * NT
                        return v.scalar_tensor_tensor(
                            x3t[:, bo2:bo2 + NT], ps[pb][:, :], dsc[:, g:g + 1],
                            x2r[:, bo2:bo2 + NT], ALU.mult, ALU.add)
                    return f
                if idx2 == 0:
                    ev_x3s = []
                ev_x3 = p.op("vector", mk_x3(), deps=[dmm_evs[idx2], ld_x2, ld_ds],
                             sem="VE", inc=1)
                ev_x3s.append(ev_x3)
                x3_wr.append(p.dma(lambda s, g=g, tt=tt: s.dma_start(
                    out=x3o[g * P:(g + 1) * P, tt * NT:(tt + 1) * NT],
                    in_=x3t[:, (g % 2) * NT:(g % 2) * NT + NT]),
                    deps=[ev_x3], stream="x3w"))
                last_down_of_tt[tt] = dmm_evs[idx2]
        wr_evs += x3_wr

        p.op("sync", lambda s: None, deps=wr_evs)
        p.emit()
    return nc


def build_D1():
    nc = bass.Bass("TRN2", target_bir_lowering=False)
    x3 = nc.dram_tensor("x3", [D, TLOC], F32, kind="ExternalInput").ap()
    win_arr = nc.dram_tensor("win_arr", [2 * DP, D], BF16, kind="ExternalInput").ap()
    xpo = nc.dram_tensor("xp", [DP, TLOC], F32, kind="ExternalOutput").ap()
    szo = nc.dram_tensor("sz", [DP, TLOC], BF16, kind="ExternalOutput").ap()
    NG = 2 * DP // P

    import contextlib
    with contextlib.ExitStack() as ctx:
        sbt = lambda n, sh, dt: ctx.enter_context(nc.sbuf_tensor(n, sh, dt))
        arena = sbt("arena", [P, KC * TLOC], BF16)
        wst = sbt("wst", [P, 2 * D], BF16)
        xpt = sbt("xpt", [P, 2 * NT], F32)
        szt = sbt("szt", [P, 2 * NT], BF16)
        ps = [ctx.enter_context(nc.psum_tensor(f"ps{i}", [P, NT], F32)) for i in range(8)]
        p = Prog(nc, ctx)
        wr_evs = []

        def out_fn(k):
            return arena[:, k * TLOC:(k + 1) * TLOC], None
        xh_evs = emit_ln(p, nc, ctx, "l3", x3, TLOC, ps, out_fn)

        out_wr = []
        for g in range(NG):
            bs = (g % 2) * D
            idx0 = g * NTT
            deps_st = [] if g < 2 else [mm_evs[(g - 2) * NTT + NTT - 1]]
            st_ld = p.dma(lambda s, g=g, bs=bs: s.dma_start(
                out=wst[:, bs:bs + D], in_=win_arr[g * P:(g + 1) * P, :]),
                deps=deps_st, stream="wst")
            for tt in range(NTT):
                idx = idx0 + tt
                pb = idx % 8

                def mk_mm(g=g, bs=bs, tt=tt, pb=pb):
                    def f(t):
                        last = None
                        for k in range(KC):
                            last = t.matmul(ps[pb][:, :], wst[:, bs + k * P:bs + (k + 1) * P],
                                            arena[:, k * TLOC + tt * NT:k * TLOC + (tt + 1) * NT],
                                            start=(k == 0), stop=(k == KC - 1))
                        return last
                    return f
                deps_m = [st_ld] + xh_evs
                if idx == 0:
                    mm_evs, ev_evs = [], []
                if idx >= 8:
                    deps_m.append(ev_evs[idx - 8])
                mm_evs.append(p.op("tensor", mk_mm(), deps=deps_m, sem="TE", inc=1))

                if g < NG // 2:
                    def mk_ev(g=g, tt=tt, pb=pb):
                        def f(v):
                            bo = (idx % 2) * NT if False else (tt % 2) * NT
                            return v.tensor_copy(xpt[:, (tt % 2) * NT:(tt % 2) * NT + NT],
                                                 ps[pb][:, :])
                        return f
                    deps_e = [mm_evs[idx]]
                    if len(out_wr) >= 2:
                        deps_e.append(out_wr[-2])
                    ev_e = p.op("vector", mk_ev(), deps=deps_e, sem="VE", inc=1)
                    out_wr.append(p.dma(lambda s, g=g, tt=tt: s.dma_start(
                        out=xpo[g * P:(g + 1) * P, tt * NT:(tt + 1) * NT],
                        in_=xpt[:, (tt % 2) * NT:(tt % 2) * NT + NT]),
                        deps=[ev_e], stream="xw"))
                else:
                    def mk_ev(g=g, tt=tt, pb=pb):
                        def f(s):
                            return s.activation(szt[:, (tt % 2) * NT:(tt % 2) * NT + NT],
                                                ps[pb][:, :], AF.Silu)
                        return f
                    deps_e = [mm_evs[idx]]
                    if len(out_wr) >= 2:
                        deps_e.append(out_wr[-2])
                    ev_e = p.op("scalar", mk_ev(), deps=deps_e, sem="SE", inc=1)
                    g2 = g - NG // 2
                    out_wr.append(p.dma(lambda s, g2=g2, tt=tt: s.dma_start(
                        out=szo[g2 * P:(g2 + 1) * P, tt * NT:(tt + 1) * NT],
                        in_=szt[:, (tt % 2) * NT:(tt % 2) * NT + NT]),
                        deps=[ev_e], stream="zw"))
                ev_evs.append(ev_e)
        wr_evs += out_wr

        p.op("sync", lambda s: None, deps=wr_evs)
        p.emit()
    return nc


D2_STAGE = 3
def build_D2():
    nc = bass.Bass("TRN2", target_bir_lowering=False)
    xpe = nc.dram_tensor("xpe", [DP, TLOC + 2], F32, kind="ExternalInput").ap()
    cw_c = nc.dram_tensor("cw_c", [P, 3 * 32], F32, kind="ExternalInput").ap()
    cb_c = nc.dram_tensor("cb_c", [P, 32], F32, kind="ExternalInput").ap()
    wx_b = nc.dram_tensor("wx_b", [DP, 160], BF16, kind="ExternalInput").ap()
    wdt_b = nc.dram_tensor("wdt_b", [P, DP], BF16, kind="ExternalInput").ap()
    bdt_c = nc.dram_tensor("bdt_c", [P, 32], F32, kind="ExternalInput").ap()
    xco = nc.dram_tensor("xc", [DP, TLOC], F32, kind="ExternalOutput").ap()
    dto = nc.dram_tensor("dt", [DP, TLOC], F32, kind="ExternalOutput").ap()
    bco = nc.dram_tensor("bc", [32, TLOC], F32, kind="ExternalOutput").ap()
    NG = DP // P
    TE = TLOC + 2

    import contextlib
    with contextlib.ExitStack() as ctx:
        sbt = lambda n, sh, dt: ctx.enter_context(nc.sbuf_tensor(n, sh, dt))
        xpt = sbt("xpt", [P, 2 * TE], F32)
        cwc = sbt("cwc", [P, 3 * 32], F32)
        cbc = sbt("cbc", [P, 32], F32)
        bdt = sbt("bdt", [P, 32], F32)
        c1 = sbt("c1", [P, TLOC], F32)
        c2t = sbt("c2t", [P, TLOC], F32)
        xct = sbt("xct", [P, 2 * TLOC], F32)
        xcb = sbt("xcb", [P, 2 * TLOC], BF16)
        wxt = sbt("wxt", [P, 2 * 160], BF16)
        xdd = sbt("xdd", [P, TLOC], F32)
        xdb = sbt("xdb", [32, TLOC], F32)
        xdbB = sbt("xdbB", [P, TLOC], BF16)
        wdt_t = sbt("wdt_t", [P, DP], BF16)
        mt = sbt("mt", [P, 2 * NT], F32)
        ut = sbt("ut", [P, 2 * NT], F32)
        et = sbt("et", [P, 2 * NT], F32)
        dtt = sbt("dtt", [P, 2 * NT], F32)
        ps = [ctx.enter_context(nc.psum_tensor(f"ps{i}", [P, NT], F32)) for i in range(8)]
        p = Prog(nc, ctx)
        wr_evs = []

        ld_cw = p.dma(lambda s: s.dma_start(out=cwc[:], in_=cw_c[:]))
        ld_cb = p.dma(lambda s: s.dma_start(out=cbc[:], in_=cb_c[:]))
        ld_bdt = p.dma(lambda s: s.dma_start(out=bdt[:], in_=bdt_c[:]))
        ld_wdt = p.dma(lambda s: s.dma_start(out=wdt_t[:], in_=wdt_b[:]))

        xc_wr = []
        xdbl_done = []
        for g in range(NG):
            bx = (g % 2) * TE
            bc_ = (g % 2) * TLOC
            deps_x = [] if g < 2 else [xcb_ev[g - 2]]
            ld_x = p.dma(lambda s, g=g, bx=bx: s.dma_start(
                out=xpt[:, bx:bx + TE], in_=xpe[g * P:(g + 1) * P, :]),
                deps=deps_x, stream="xp")
            deps_w = [] if g < 2 else [xdbl_done[g - 2]]
            ld_w = p.dma(lambda s, g=g: s.dma_start(
                out=wxt[:, (g % 2) * 160:(g % 2) * 160 + 160],
                in_=wx_b[g * P:(g + 1) * P, :]),
                deps=deps_w, stream="wx")

            deps_c = [ld_x, ld_cw]
            if g >= 1:
                deps_c.append(ev_c2[0] if False else ev_c3)
            ev_c1 = p.op("vector", lambda v, bx=bx, g=g: v.tensor_scalar_mul(
                c1[:], xpt[:, bx:bx + TLOC], cwc[:, g:g + 1]),
                deps=[ld_x, ld_cw] + ([ev_c3] if g >= 1 else []), sem="VE", inc=1)
            ev_c2 = p.op("vector", lambda v, bx=bx, g=g: v.scalar_tensor_tensor(
                c2t[:], xpt[:, bx + 1:bx + 1 + TLOC], cwc[:, 32 + g:32 + g + 1], c1[:],
                ALU.mult, ALU.add), deps=[ev_c1], sem="VE", inc=1)
            ev_c3 = p.op("vector", lambda v, bx=bx, g=g, bc_=bc_: v.scalar_tensor_tensor(
                xct[:, bc_:bc_ + TLOC], xpt[:, bx + 2:bx + 2 + TLOC],
                cwc[:, 64 + g:64 + g + 1], c2t[:], ALU.mult, ALU.add),
                deps=[ev_c2] + ([xc_wr[g - 2]] if g >= 2 else []), sem="VE", inc=1)
            ev_xc = p.op("scalar", lambda s, g=g, bc_=bc_: s.activation(
                xct[:, bc_:bc_ + TLOC], xct[:, bc_:bc_ + TLOC], AF.Silu,
                bias=cbc[:, g:g + 1]), deps=[ev_c3, ld_cb], sem="SE", inc=1)
            xc_wr.append(p.dma(lambda s, g=g, bc_=bc_: s.dma_start(
                out=xco[g * P:(g + 1) * P, :], in_=xct[:, bc_:bc_ + TLOC]),
                deps=[ev_xc], stream="xcw"))
            if g == 0:
                xcb_ev = []
            deps_cb = [ev_xc]
            if g >= 2:
                deps_cb.append(xdbl_done[g - 2])
            xcb_ev.append(p.op("vector", lambda v, g=g, bc_=bc_: v.tensor_copy(
                xcb[:, bc_:bc_ + TLOC], xct[:, bc_:bc_ + TLOC]),
                deps=deps_cb, sem="VE", inc=1))

            def mk_xd(g=g, bc_=bc_):
                def f(t):
                    last = None
                    for tt in range(NTT):
                        t.matmul(ps[tt][:, :],
                                 wxt[:, (g % 2) * 160:(g % 2) * 160 + 128],
                                 xcb[:, bc_ + tt * NT:bc_ + (tt + 1) * NT],
                                 start=(g == 0), stop=(g == NG - 1))
                        last = t.matmul(ps[4 + tt][0:32, :],
                                        wxt[:, (g % 2) * 160 + 128:(g % 2) * 160 + 160],
                                        xcb[:, bc_ + tt * NT:bc_ + (tt + 1) * NT],
                                        start=(g == 0), stop=(g == NG - 1))
                    return last
                return f
            xdbl_done.append(p.op("tensor", mk_xd(), deps=[xcb_ev[g], ld_w],
                                  sem="TE", inc=1))

        if D2_STAGE == 1:
            wr_evs += xc_wr
            p.op("sync", lambda s: None, deps=wr_evs)
            p.emit()
            return nc
        def mk_exd(v):
            for tt in range(NTT):
                v.tensor_copy(xdd[:, tt * NT:(tt + 1) * NT], ps[tt][:, :])
            for tt in range(NTT - 1):
                v.tensor_copy(xdb[0:32, tt * NT:(tt + 1) * NT], ps[4 + tt][0:32, :])
            return v.tensor_copy(xdb[0:32, (NTT - 1) * NT:NTT * NT], ps[7][0:32, :])
        ev_exd = p.op("vector", mk_exd, deps=[xdbl_done[NG - 1]], sem="VE", inc=1)
        wr_evs.append(p.dma(lambda s: s.dma_start(out=bco[:], in_=xdb[0:32, :]),
                            deps=[ev_exd]))
        ev_xdbB = p.op("scalar", lambda s: s.activation(xdbB[:], xdd[:], AF.Copy),
                       deps=[ev_exd], sem="SE", inc=1)

        if D2_STAGE == 2:
            wr_evs += xc_wr
            p.op("sync", lambda s: None, deps=wr_evs + [ev_xdbB])
            p.emit()
            return nc
        dt_wr = []
        for j in range(NG):
            for tt in range(NTT):
                idx = j * NTT + tt
                pb = idx % 2
                bo = pb * NT

                def mk_dm(j=j, tt=tt, pb=pb):
                    def f(t):
                        return t.matmul(ps[pb][:, :], wdt_t[:, j * P:(j + 1) * P],
                                        xdbB[:, tt * NT:(tt + 1) * NT],
                                        start=True, stop=True)
                    return f
                deps_dm = [ev_xdbB, ld_wdt, ev_exd]
                if idx >= 2:
                    deps_dm.append(ev_m[idx - 2])
                    if ev_u[idx - 2] is not None:
                        deps_dm.append(ev_u[idx - 2])
                mm = p.op("tensor", mk_dm(), deps=deps_dm, sem="TE", inc=1)
                if idx == 0:
                    ev_m, ev_u, ev_dt = [], [], []
                ev_m.append(p.op("vector", lambda v, j=j, pb=pb, bo=bo: v.tensor_scalar(
                    mt[:, bo:bo + NT], ps[pb][:, :], bdt[:, j:j + 1], 0.0,
                    ALU.add, ALU.max), deps=[mm, ld_bdt] +
                    ([ev_dt[idx - 2]] if idx >= 2 else []), sem="VE", inc=1))
                if D2_STAGE == 4:
                    ev_u.append(None)
                    ev_dt.append(p.op("vector", lambda v, bo=bo: v.tensor_copy(
                        dtt[:, bo:bo + NT], mt[:, bo:bo + NT]),
                        deps=[ev_m[idx]] +
                        ([dt_wr[idx - 2]] if idx >= 2 else []), sem="VE", inc=1))
                else:
                    ev_u.append(p.op("scalar", lambda s, j=j, pb=pb, bo=bo: s.activation(
                        ut[:, bo:bo + NT], ps[pb][:, :], AF.Abs, bias=bdt[:, j:j + 1]),
                        deps=[mm, ld_bdt, ev_m[idx]] +
                        ([ev_dt[idx - 2]] if idx >= 2 else []),
                        sem="SE", inc=1))
                    ev_e = p.op("scalar", lambda s, bo=bo: s.activation(
                        et[:, bo:bo + NT], ut[:, bo:bo + NT], AF.Exp, scale=-1.0),
                        deps=[ev_u[idx]], sem="SE", inc=1)
                    ev_l = p.op("scalar", lambda s, bo=bo: s.activation(
                        et[:, bo:bo + NT], et[:, bo:bo + NT], AF.Ln, bias=1.0),
                        deps=[ev_e], sem="SE", inc=1)
                    ev_dt.append(p.op("vector", lambda v, bo=bo: v.tensor_add(
                        dtt[:, bo:bo + NT], mt[:, bo:bo + NT], et[:, bo:bo + NT]),
                        deps=[ev_l, ev_m[idx]] +
                        ([dt_wr[idx - 2]] if idx >= 2 else []), sem="VE", inc=1))
                dt_wr.append(p.dma(lambda s, j=j, tt=tt, bo=bo: s.dma_start(
                    out=dto[j * P:(j + 1) * P, tt * NT:(tt + 1) * NT],
                    in_=dtt[:, bo:bo + NT]), deps=[ev_dt[idx]], stream="dtw"))
        wr_evs += dt_wr + xc_wr

        p.op("sync", lambda s: None, deps=wr_evs)
        p.emit()
    return nc


def build_E():
    nc = bass.Bass("TRN2", target_bir_lowering=False)
    CH = 2048
    NGC = CH // P
    NS = 16
    NTAU = NSEQ // NT
    xc = nc.dram_tensor("xc", [CH, NSEQ], F32, kind="ExternalInput").ap()
    dt = nc.dram_tensor("dt", [CH, NSEQ], F32, kind="ExternalInput").ap()
    sz = nc.dram_tensor("sz", [CH, NSEQ], BF16, kind="ExternalInput").ap()
    bc = nc.dram_tensor("bc", [32, NSEQ], F32, kind="ExternalInput").ap()
    A_arr = nc.dram_tensor("A_arr", [P, NGC * NS], F32, kind="ExternalInput").ap()
    D_c = nc.dram_tensor("D_c", [P, NGC], F32, kind="ExternalInput").ap()
    yzo = nc.dram_tensor("yz", [CH, NSEQ], BF16, kind="ExternalOutput").ap()

    import contextlib
    with contextlib.ExitStack() as ctx:
        sbt = lambda n, sh, dt_: ctx.enter_context(nc.sbuf_tensor(n, sh, dt_))
        xct = sbt("xct", [P, 2 * NT], F32)
        dtt = sbt("dtt", [P, 2 * NT], F32)
        szt = sbt("szt", [P, 2 * NT], BF16)
        dut = sbt("dut", [P, NT], F32)
        dat = sbt("dat", [P, NT], F32)
        wt = sbt("wt", [P, NT], F32)
        ht = sbt("ht", [P, NT], F32)
        yac = sbt("yac", [P, NT], F32)
        ypt = sbt("ypt", [P, NT], F32)
        yzt = sbt("yzt", [P, 2 * NT], BF16)
        Bbc = sbt("Bbc", [P, NS * NT], F32)
        Cbc = sbt("Cbc", [P, NS * NT], F32)
        Aar = sbt("Aar", [P, NGC * NS], F32)
        Am = sbt("Am", [P, NGC * NS], F32)
        Dc = sbt("Dc", [P, NGC], F32)
        hst = sbt("hst", [P, NGC * NS], F32)
        p = Prog(nc, ctx)
        wr_evs = []

        ld_A = p.dma(lambda s: s.dma_start(out=Aar[:], in_=A_arr[:]))
        ld_D = p.dma(lambda s: s.dma_start(out=Dc[:], in_=D_c[:]))
        ev_A1 = p.op("scalar", lambda s: s.activation(Am[:], Aar[:], AF.Exp),
                     deps=[ld_A], sem="SE", inc=1)
        ev_A = p.op("vector", lambda v: v.tensor_scalar_mul(Am[:], Am[:], -1.0),
                    deps=[ev_A1], sem="VE", inc=1)
        ev_h0 = p.op("vector", lambda v: v.memset(hst[:], 0.0), sem="VE", inc=1)

        yz_wr = []
        prev_tau_done = {}
        for tau in range(NTAU):
            t0 = tau * NT
            bb_evs = []
            for s_ in range(NS):
                deps_bb = []
                if tau > 0:
                    deps_bb = [prev_tau_done[(tau - 1, g) if False else tau - 1]]
                bb_evs.append(p.dma(lambda s, s_=s_, t0=t0: s.dma_start(
                    out=Bbc[:, s_ * NT:(s_ + 1) * NT],
                    in_=bc[s_:s_ + 1, t0:t0 + NT].partition_broadcast(P)),
                    deps=deps_bb, stream="bb"))
                bb_evs.append(p.dma(lambda s, s_=s_, t0=t0: s.dma_start(
                    out=Cbc[:, s_ * NT:(s_ + 1) * NT],
                    in_=bc[16 + s_:16 + s_ + 1, t0:t0 + NT].partition_broadcast(P)),
                    deps=deps_bb, stream="cb"))
            for g in range(NGC):
                bx = (g % 2) * NT
                idx = tau * NGC + g
                deps_ld = []
                if idx >= 2:
                    deps_ld.append(dtu_evs[idx - 2])
                ld_xc = p.dma(lambda s, g=g, t0=t0, bx=bx: s.dma_start(
                    out=xct[:, bx:bx + NT], in_=xc[g * P:(g + 1) * P, t0:t0 + NT]),
                    deps=deps_ld, stream="xc")
                ld_dt = p.dma(lambda s, g=g, t0=t0, bx=bx: s.dma_start(
                    out=dtt[:, bx:bx + NT], in_=dt[g * P:(g + 1) * P, t0:t0 + NT]),
                    deps=deps_ld, stream="dt")
                deps_ldz = []
                if idx >= 2:
                    deps_ldz.append(yz_evs[idx - 2])
                ld_sz = p.dma(lambda s, g=g, t0=t0, bx=bx: s.dma_start(
                    out=szt[:, bx:bx + NT], in_=sz[g * P:(g + 1) * P, t0:t0 + NT]),
                    deps=deps_ldz, stream="sz")

                if idx == 0:
                    dtu_evs, yz_evs = [], []
                deps_du = [ld_xc, ld_dt]
                if idx >= 1:
                    deps_du.append(w_ev)
                dtu_evs.append(p.op("vector", lambda v, bx=bx: v.tensor_mul(
                    dut[:], dtt[:, bx:bx + NT], xct[:, bx:bx + NT]),
                    deps=deps_du, sem="VE", inc=1))

                prev_scan = None
                prev_hup = None
                prev_yp = None
                for s_ in range(NS):
                    col = g * NS + s_
                    deps_da = [ld_dt, ev_A]
                    if prev_scan is not None:
                        deps_da.append(prev_scan)
                    da_ev = p.op("scalar", lambda s, bx=bx, col=col: s.activation(
                        dat[:], dtt[:, bx:bx + NT], AF.Exp, scale=Am[:, col:col + 1]),
                        deps=deps_da, sem="SE", inc=1)
                    deps_w = [dtu_evs[idx], bb_evs[2 * s_]]
                    if prev_scan is not None:
                        deps_w.append(prev_scan)
                    w_ev = p.op("vector", lambda v, s_=s_: v.tensor_mul(
                        wt[:], dut[:], Bbc[:, s_ * NT:(s_ + 1) * NT]),
                        deps=deps_w, sem="VE", inc=1)
                    init = 0.0 if tau == 0 else hst[:, col:col + 1]

                    def mk_scan(init=init):
                        def f(v):
                            return v.tensor_tensor_scan(ht[:], dat[:], wt[:], init,
                                                        ALU.mult, ALU.add)
                        return f
                    deps_sc = [da_ev, w_ev]
                    if tau > 0:
                        deps_sc.append(hup_evs[(tau - 1, col)])
                    if prev_hup is not None:
                        deps_sc.append(prev_hup)
                    if prev_yp is not None:
                        deps_sc.append(prev_yp)
                    scan_ev = p.op("vector", mk_scan(), deps=deps_sc, sem="VE", inc=1)
                    if tau == 0 and g == 0 and s_ == 0:
                        hup_evs = {}
                    deps_h = [scan_ev, ev_h0]
                    if tau > 0:
                        deps_h.append(hup_evs[(tau - 1, col)])
                    hup_evs[(tau, col)] = p.op("vector", lambda v, col=col: v.tensor_copy(
                        hst[:, col:col + 1], ht[:, NT - 1:NT]),
                        deps=deps_h, sem="VE", inc=1)
                    if s_ == 0:
                        yp_ev = p.op("vector", lambda v, s_=s_: v.tensor_mul(
                            yac[:], ht[:], Cbc[:, s_ * NT:(s_ + 1) * NT]),
                            deps=[scan_ev, bb_evs[2 * s_ + 1]], sem="VE", inc=1)
                    else:
                        ev_p = p.op("vector", lambda v, s_=s_: v.tensor_mul(
                            ypt[:], ht[:], Cbc[:, s_ * NT:(s_ + 1) * NT]),
                            deps=[scan_ev, bb_evs[2 * s_ + 1]], sem="VE", inc=1)
                        yp_ev = p.op("vector", lambda v: v.tensor_add(
                            yac[:], yac[:], ypt[:]), deps=[ev_p, yp_ev], sem="VE", inc=1)
                        prev_yp = ev_p
                    prev_scan = scan_ev
                    prev_hup = hup_evs[(tau, col)]
                ev_y2 = p.op("vector", lambda v, g=g, bx=bx: v.scalar_tensor_tensor(
                    yac[:], xct[:, bx:bx + NT], Dc[:, g:g + 1], yac[:],
                    ALU.mult, ALU.add), deps=[yp_ev, ld_D], sem="VE", inc=1)
                dtu_evs[idx] = ev_y2
                yz_evs.append(p.op("vector", lambda v, bx=bx: v.tensor_mul(
                    yzt[:, bx:bx + NT], yac[:], szt[:, bx:bx + NT]),
                    deps=[ev_y2, ld_sz] + ([yz_wr[idx - 2]] if idx >= 2 else []),
                    sem="VE", inc=1))
                yz_wr.append(p.dma(lambda s, g=g, t0=t0, bx=bx: s.dma_start(
                    out=yzo[g * P:(g + 1) * P, t0:t0 + NT], in_=yzt[:, bx:bx + NT]),
                    deps=[yz_evs[idx]], stream="yzw"))
            prev_tau_done[tau] = yz_evs[tau * NGC + NGC - 1]
        wr_evs += yz_wr

        p.op("sync", lambda s: None, deps=wr_evs)
        p.emit()
    return nc


def build_F1():
    nc = bass.Bass("TRN2", target_bir_lowering=False)
    x3 = nc.dram_tensor("x3", [D, TLOC], F32, kind="ExternalInput").ap()
    yz = nc.dram_tensor("yz", [DP, TLOC], BF16, kind="ExternalInput").ap()
    wout_arr = nc.dram_tensor("wout_arr", [D, DP], BF16, kind="ExternalInput").ap()
    x4o = nc.dram_tensor("x4", [D, TLOC], F32, kind="ExternalOutput").ap()
    KD = DP // P

    import contextlib
    with contextlib.ExitStack() as ctx:
        sbt = lambda n, sh, dt: ctx.enter_context(nc.sbuf_tensor(n, sh, dt))
        yzb = sbt("yzb", [P, KD * NT], BF16)
        wst = sbt("wst", [P, 2 * DP], BF16)
        xrt = sbt("xrt", [P, 2 * NT], F32)
        x4t = sbt("x4t", [P, 2 * NT], F32)
        ps = [ctx.enter_context(nc.psum_tensor(f"ps{i}", [P, NT], F32)) for i in range(4)]
        p = Prog(nc, ctx)
        wr_evs = []

        x4_wr = []
        for tt in range(NTT):
            def mk_ldyz(tt=tt):
                def f(s, h):
                    for k in range(KD):
                        s.dma_start(out=yzb[:, k * NT:(k + 1) * NT],
                                    in_=yz[k * P:(k + 1) * P, tt * NT:(tt + 1) * NT]
                                    ).then_inc(h, 16)
                    return None
                return f
            deps_yz = [] if tt == 0 else [mm_last]
            yz_ev = p.dma(mk_ldyz(), deps=deps_yz, stream="yzl", n=KD)
            for g in range(KC):
                idx = tt * KC + g
                pb = idx % 4
                deps_st = [] if idx < 2 else [mm_evs2[idx - 2]]
                st_ld = p.dma(lambda s, g=g, idx=idx: s.dma_start(
                    out=wst[:, (idx % 2) * DP:(idx % 2) * DP + DP],
                    in_=wout_arr[g * P:(g + 1) * P, :]), deps=deps_st, stream="wst")
                deps_xr = [] if idx < 2 else [ev_x4[idx - 2]]
                xr_ld = p.dma(lambda s, g=g, tt=tt, idx=idx: s.dma_start(
                    out=xrt[:, (idx % 2) * NT:(idx % 2) * NT + NT],
                    in_=x3[g * P:(g + 1) * P, tt * NT:(tt + 1) * NT]),
                    deps=deps_xr, stream="xr")

                def mk_mm(g=g, tt=tt, idx=idx, pb=pb):
                    def f(t):
                        last = None
                        for k in range(KD):
                            last = t.matmul(ps[pb][:, :],
                                            wst[:, (idx % 2) * DP + k * P:(idx % 2) * DP + (k + 1) * P],
                                            yzb[:, k * NT:(k + 1) * NT],
                                            start=(k == 0), stop=(k == KD - 1))
                        return last
                    return f
                deps_m = [st_ld, yz_ev]
                if idx == 0:
                    mm_evs2, ev_x4 = [], []
                if idx >= 4:
                    deps_m.append(ev_x4[idx - 4])
                mm_evs2.append(p.op("tensor", mk_mm(), deps=deps_m, sem="TE", inc=1))

                def mk_e(idx=idx, pb=pb):
                    def f(v):
                        return v.tensor_add(x4t[:, (idx % 2) * NT:(idx % 2) * NT + NT],
                                            ps[pb][:, :],
                                            xrt[:, (idx % 2) * NT:(idx % 2) * NT + NT])
                    return f
                deps_e = [mm_evs2[idx], xr_ld]
                if idx >= 2:
                    deps_e.append(x4_wr[idx - 2])
                ev_x4.append(p.op("vector", mk_e(), deps=deps_e, sem="VE", inc=1))
                x4_wr.append(p.dma(lambda s, g=g, tt=tt, idx=idx: s.dma_start(
                    out=x4o[g * P:(g + 1) * P, tt * NT:(tt + 1) * NT],
                    in_=x4t[:, (idx % 2) * NT:(idx % 2) * NT + NT]),
                    deps=[ev_x4[idx]], stream="x4w"))
            mm_last = mm_evs2[tt * KC + KC - 1]
        wr_evs += x4_wr

        p.op("sync", lambda s: None, deps=wr_evs)
        p.emit()
    return nc


def run_B(inp, wb, xhat1):
    nc = _get("B", build_B)
    wq_arr = np.ascontiguousarray(
        wb["wqT"].reshape(KC, P, KC, P).transpose(2, 1, 0, 3).reshape(D, D))
    in_maps = []
    for c in range(NCORE):
        n0 = (c % 2) * TLOC
        pos = n0 + np.arange(TLOC, dtype=np.float32)
        pos_col = (n0 + np.arange(TLOC, dtype=np.float32)).reshape(KC, P).T
        in_maps.append({
            "xhat1": xhat1[c], "wq_arr": wq_arr, "wkT": wb["wkT"], "wvT": wb["wvT"],
            "bq": inp["L0_attn_bq"].reshape(1, D).astype(np.float32),
            "bk": inp["L0_attn_bk"].reshape(1, D).astype(np.float32),
            "bv": inp["L0_attn_bv"].reshape(1, D).astype(np.float32),
            "pos_row": pos.reshape(1, TLOC),
            "pos_col": np.ascontiguousarray(pos_col),
        })
    res = run_bass_kernel_spmd(nc, in_maps, core_ids=list(range(NCORE)))
    q = [res.results[c]["q"] for c in range(NCORE)]
    kvp = [res.results[c]["kvp"] for c in range(NCORE)]
    return q, kvp


def run_C(inp, wb, xcore, q, kvp):
    ncc1 = _get("C1", build_C1)
    wo_arr = np.ascontiguousarray(
        wb["woT"].reshape(KC, P, KC, P).transpose(2, 1, 0, 3).reshape(D, D))
    in_maps = []
    for c in range(NCORE):
        in_maps.append({
            "x": xcore[c], "q": q[c], "kvp_s": kvp[c], "kvp_p": kvp[c ^ 1],
            "wo_arr": wo_arr,
            "bo": inp["L0_attn_bo"].reshape(1, D).astype(np.float32),
        })
    res = run_bass_kernel_spmd(ncc1, in_maps, core_ids=list(range(NCORE)))
    x2 = [res.results[c]["x2"] for c in range(NCORE)]

    ncc2 = _get("C2", build_C2)
    NI = INTER // P
    g_arr = np.ascontiguousarray(
        wb["g0T"].reshape(KC, P, NI, P).transpose(2, 1, 0, 3).reshape(INTER, D))
    u_arr = np.ascontiguousarray(
        wb["u0T"].reshape(KC, P, NI, P).transpose(2, 1, 0, 3).reshape(INTER, D))
    d_arr = np.ascontiguousarray(
        wb["d0T"].reshape(NI, P, KC, P).transpose(2, 1, 0, 3).reshape(D, INTER))
    gs = np.ascontiguousarray(inp["L0_ffn_gate_s"].reshape(NI, P).T.astype(np.float32))
    us = np.ascontiguousarray(inp["L0_ffn_up_s"].reshape(NI, P).T.astype(np.float32))
    ds = np.ascontiguousarray(inp["L0_ffn_down_s"].reshape(KC, P).T.astype(np.float32))
    in_maps = []
    for c in range(NCORE):
        in_maps.append({"x2": x2[c], "g_arr": g_arr, "u_arr": u_arr, "d_arr": d_arr,
                        "gs_c": gs, "us_c": us, "ds_c": ds})
    res = run_bass_kernel_spmd(ncc2, in_maps, core_ids=list(range(NCORE)))
    x3 = [res.results[c]["x3"] for c in range(NCORE)]
    return x2, x3


def run_D(inp, wb, x3):
    ncd1 = _get("D1", build_D1)
    win_arr = np.ascontiguousarray(
        wb["win"].reshape(KC, P, 64, P).transpose(2, 1, 0, 3).reshape(2 * DP, D))
    in_maps = [{"x3": x3[c], "win_arr": win_arr} for c in range(NCORE)]
    res = run_bass_kernel_spmd(ncd1, in_maps, core_ids=list(range(NCORE)))
    xp = [res.results[c]["xp"] for c in range(NCORE)]
    sz = [res.results[c]["sz"] for c in range(NCORE)]

    ncd2 = _get("D2", build_D2)
    cw = np.asarray(inp["L1_attn_conv_w"], dtype=np.float32).reshape(DP, 3)
    cw_c = np.concatenate([cw[:, k].reshape(32, P).T for k in range(3)], axis=1)
    cb_c = np.asarray(inp["L1_attn_conv_b"], dtype=np.float32).reshape(32, P).T
    bdt_c = np.asarray(inp["L1_attn_b_dt"], dtype=np.float32).reshape(32, P).T
    zcol = np.zeros((DP, 1), np.float32)
    in_maps = []
    for c in range(NCORE):
        b, h = c // 2, c % 2
        left = zcol if h == 0 else xp[2 * b][:, TLOC - 1:TLOC]
        right = zcol if h == 1 else xp[2 * b + 1][:, 0:1]
        xpe = np.ascontiguousarray(np.concatenate([left, xp[c], right], axis=1))
        in_maps.append({"xpe": xpe, "cw_c": np.ascontiguousarray(cw_c),
                        "cb_c": np.ascontiguousarray(cb_c),
                        "wx_b": wb["wx"], "wdt_b": wb["wdt"],
                        "bdt_c": np.ascontiguousarray(bdt_c)})
    res = run_bass_kernel_spmd(ncd2, in_maps, core_ids=list(range(NCORE)))
    xc = [res.results[c]["xc"] for c in range(NCORE)]
    dt = [res.results[c]["dt"] for c in range(NCORE)]
    bc = [res.results[c]["bc"] for c in range(NCORE)]
    return xp, sz, xc, dt, bc


def run_E(inp, xc, dt, sz, bc):
    nce = _get("E", build_E)
    A_log = np.asarray(inp["L1_attn_A_log"], dtype=np.float32)
    Dv = np.asarray(inp["L1_attn_D"], dtype=np.float32)
    in_maps = []
    for c in range(NCORE):
        b, h = c // 2, c % 2
        xcf = np.concatenate([xc[2 * b], xc[2 * b + 1]], axis=1)[h * 2048:(h + 1) * 2048]
        dtf = np.concatenate([dt[2 * b], dt[2 * b + 1]], axis=1)[h * 2048:(h + 1) * 2048]
        szf = np.concatenate([sz[2 * b], sz[2 * b + 1]], axis=1)[h * 2048:(h + 1) * 2048]
        bcf = np.concatenate([bc[2 * b], bc[2 * b + 1]], axis=1)
        Asl = A_log[h * 2048:(h + 1) * 2048].reshape(16, P, 16).transpose(1, 0, 2).reshape(P, 256)
        Dsl = Dv[h * 2048:(h + 1) * 2048].reshape(16, P).T
        in_maps.append({"xc": np.ascontiguousarray(xcf), "dt": np.ascontiguousarray(dtf),
                        "sz": np.ascontiguousarray(szf), "bc": np.ascontiguousarray(bcf),
                        "A_arr": np.ascontiguousarray(Asl), "D_c": np.ascontiguousarray(Dsl)})
    res = run_bass_kernel_spmd(nce, in_maps, core_ids=list(range(NCORE)))
    return [res.results[c]["yz"] for c in range(NCORE)]


def run_F(inp, wb, x3, yz):
    ncf1 = _get("F1", build_F1)
    wout_arr = np.ascontiguousarray(
        wb["wout"].reshape(32, P, KC, P).transpose(2, 1, 0, 3).reshape(D, DP))
    in_maps = []
    for c in range(NCORE):
        b, h = c // 2, c % 2
        yzf = np.concatenate([yz[2 * b], yz[2 * b + 1]], axis=0)[:, h * 2048:(h + 1) * 2048]
        in_maps.append({"x3": x3[c], "yz": np.ascontiguousarray(yzf), "wout_arr": wout_arr})
    res = run_bass_kernel_spmd(ncf1, in_maps, core_ids=list(range(NCORE)))
    x4 = [res.results[c]["x4"] for c in range(NCORE)]

    ncf2 = _get("C2", build_C2)
    NI = INTER // P
    g_arr = np.ascontiguousarray(
        wb["qgT"].reshape(KC, P, NI, P).transpose(2, 1, 0, 3).reshape(INTER, D))
    u_arr = np.ascontiguousarray(
        wb["quT"].reshape(KC, P, NI, P).transpose(2, 1, 0, 3).reshape(INTER, D))
    d_arr = np.ascontiguousarray(
        wb["qdT"].reshape(NI, P, KC, P).transpose(2, 1, 0, 3).reshape(D, INTER))
    gs = np.ascontiguousarray(inp["L1_ffn_s_gate"].reshape(NI, P).T.astype(np.float32))
    us = np.ascontiguousarray(inp["L1_ffn_s_up"].reshape(NI, P).T.astype(np.float32))
    ds = np.ascontiguousarray(inp["L1_ffn_s_down"].reshape(KC, P).T.astype(np.float32))
    in_maps = [{"x2": x4[c], "g_arr": g_arr, "u_arr": u_arr, "d_arr": d_arr,
                "gs_c": gs, "us_c": us, "ds_c": ds} for c in range(NCORE)]
    res = run_bass_kernel_spmd(ncf2, in_maps, core_ids=list(range(NCORE)))
    return [res.results[c]["x3"] for c in range(NCORE)]


def kernel(**inp):
    xcore = _shard_x(np.asarray(inp["L0_attn_x"], dtype=np.float32))
    xhat1, wb = run_A(inp, xcore)
    q, kvp = run_B(inp, wb, xhat1)
    x2, x3 = run_C(inp, wb, xcore, q, kvp)
    xp, sz, xc, dt, bc = run_D(inp, wb, x3)
    yz = run_E(inp, xc, dt, sz, bc)
    out_c = run_F(inp, wb, x3, yz)
    out = np.empty((4, NSEQ, D), np.float32)
    for c in range(NCORE):
        b, h = c // 2, c % 2
        out[b, h * TLOC:(h + 1) * TLOC, :] = out_c[c].T
    return out


# version 02
# speedup vs baseline: 1.0611x; 1.0611x over previous
import numpy as np
import ml_dtypes
import concourse.bass as bass
import concourse.mybir as mybir
from concourse.bass_utils import run_bass_kernel_spmd

F32 = mybir.dt.float32
BF16 = mybir.dt.bfloat16
I32 = mybir.dt.int32
AF = mybir.ActivationFunctionType
ALU = mybir.AluOpType

NCORE = 8
P = 128
D = 2048
TLOC = 2048
KC = D // P
NT = 512
NTT = TLOC // NT
EPS_LN = 1e-5
EPS_ATT = 1e-6
NSEQ = 4096
DP = 4096
INTER = 5632
NH = 16
DK = 128

BF = ml_dtypes.bfloat16

WCONV = [
    ("wqT", D, D, "f32"), ("wkT", D, D, "f32"), ("wvT", D, D, "f32"),
    ("woT", D, D, "f32"),
    ("g0T", D, INTER, "f32"), ("u0T", D, INTER, "f32"), ("d0T", INTER, D, "f32"),
    ("win", D, 2 * DP, "f32"), ("wx", DP, 160, "f32"), ("wdt", P, DP, "f32"),
    ("wout", DP, D, "f32"),
    ("qgT", D, INTER, "i32"), ("quT", D, INTER, "i32"), ("qdT", INTER, D, "i32"),
]


class Prog:
    ENGINES = ("sync", "tensor", "vector", "scalar", "gpsimd")

    def __init__(self, nc, ctx):
        self.nc = nc
        self.ctx = ctx
        self.sems = {}
        self.ops = {e: [] for e in self.ENGINES}
        self.sem_count = {}
        self.stream_idx = {}
        self._uniq = 0
        self._last_ev = {}

    def _sem(self, key):
        if key not in self.sems:
            self.sems[key] = self.ctx.enter_context(self.nc.semaphore(key))
            self.sem_count[key] = 0
        return key

    def op(self, eng, fn, deps=(), sem=None, inc=0):
        deps = [d for d in deps if d is not None]
        if eng in ("vector", "scalar", "gpsimd") and eng in self._last_ev:
            deps.append(self._last_ev[eng])
        ev = None
        if sem is not None:
            self._sem(sem)
            self.sem_count[sem] += inc
            ev = (sem, self.sem_count[sem])
        self.ops[eng].append((fn, tuple(deps), sem, inc))
        if eng in ("vector", "scalar", "gpsimd") and ev is not None:
            self._last_ev[eng] = ev
        return ev

    def dma(self, fn, deps=(), stream=None, eng="sync", n=1):
        if stream is None:
            self._uniq += 1
            key = f"dU{self._uniq}"
        else:
            i = self.stream_idx.get(stream, 0)
            self.stream_idx[stream] = i + 1
            key = f"d_{stream}_{i % 2}"
        self._sem(key)
        if n == 1:
            return self.op(eng, fn, deps, key, 16)
        h = self.sems[key]
        return self.op(eng, lambda e, fn=fn, h=h: fn(e, h), deps, key, 16 * n)

    def barrier_ev(self, stream):
        out = []
        for par in (0, 1):
            key = f"d_{stream}_{par}"
            if key in self.sem_count and self.sem_count[key] > 0:
                out.append((key, self.sem_count[key]))
        return out

    def emit(self):
        sems = self.sems

        def body(eng_name):
            def f(eng):
                waited = {}
                for (fn, deps, sem, inc) in self.ops[eng_name]:
                    need = {}
                    for (dsem, dval) in deps:
                        if dval > need.get(dsem, -1):
                            need[dsem] = dval
                    for dsem, dval in need.items():
                        if waited.get(dsem, 0) >= dval:
                            continue
                        eng.wait_ge(sems[dsem], dval)
                        waited[dsem] = dval
                    inst = fn(eng)
                    if sem is not None and inst is not None:
                        inst.then_inc(sems[sem], inc)
            return f

        with self.nc.Block() as block:
            for e in self.ENGINES:
                if self.ops[e]:
                    getattr(block, e)(body(e))


def emit_ln(p, nc, ctx, tag, x_hbm, ncols, psums, out_fn, deps_start=()):
    sb = lambda n, sh, dt: ctx.enter_context(nc.sbuf_tensor(tag + n, sh, dt))
    ntt = ncols // NT if ncols % NT == 0 else None
    assert ntt is not None
    xt = sb("xt", [P, ncols], F32)
    sq = sb("sq", [P, ncols], F32)
    ones_col = sb("ones", [P, 1], F32)
    mrow = sb("mrow", [1, ncols], F32)
    vrow = sb("vrow", [1, ncols], F32)
    mbc = sb("mbc", [P, ncols], F32)
    rbc = sb("rbc", [P, ncols], F32)
    t1 = sb("t1", [P, ncols], F32)
    m2row = t1[0:1, 0:ncols]
    rrow = vrow
    srow = nc.dram_tensor(tag + "srow", [2, ncols], F32, kind="Internal").ap()

    ones_ev = p.op("vector", lambda v: v.memset(ones_col[:], 1.0), sem="VE", inc=1)
    xld, sqd, mm_done = [], [], []
    for k in range(KC):
        deps = list(deps_start)
        if k >= 1:
            deps += [mm_done[k - 1], sqd[k - 1]]
        b = 0
        xld.append(p.dma(lambda s, k=k, b=b: s.dma_start(
            out=xt[:, b:b + ncols], in_=x_hbm[k * P:(k + 1) * P, :]),
            deps=deps, stream=tag + "x"))
        sqd.append(p.op("scalar", lambda s, b=b: s.activation(
            sq[:, b:b + ncols], xt[:, b:b + ncols], AF.Square),
            deps=[xld[k]], sem="SE", inc=1))

        def mk_mm(k=k, b=b):
            def f(t):
                last = None
                for tt in range(ntt):
                    t.matmul(psums[tt][0:1, :], ones_col[:, 0:1],
                             xt[:, b + tt * NT:b + (tt + 1) * NT],
                             start=(k == 0), stop=(k == KC - 1))
                    last = t.matmul(psums[4 + tt][0:1, :], ones_col[:, 0:1],
                                    sq[:, b + tt * NT:b + (tt + 1) * NT],
                                    start=(k == 0), stop=(k == KC - 1))
                return last
            return f
        mm_done.append(p.op("tensor", mk_mm(), deps=[xld[k], sqd[k], ones_ev],
                            sem="TE", inc=1))

    def evac_stats(v):
        for tt in range(ntt):
            v.tensor_copy(mrow[0:1, tt * NT:(tt + 1) * NT], psums[tt][0:1, :])
        for tt in range(ntt - 1):
            v.tensor_copy(vrow[0:1, tt * NT:(tt + 1) * NT], psums[4 + tt][0:1, :])
        return v.tensor_copy(vrow[0:1, (ntt - 1) * NT:ntt * NT], psums[4 + ntt - 1][0:1, :])
    ev_ev = p.op("vector", evac_stats, deps=[mm_done[KC - 1]], sem="VE", inc=1)
    ev_mr = p.op("vector", lambda v: v.tensor_scalar_mul(mrow[:], mrow[:], 1.0 / D),
                 deps=[ev_ev], sem="VE", inc=1)
    ev_st = p.op("vector", lambda v: v.tensor_scalar_mul(vrow[:], vrow[:], 1.0 / D),
                 deps=[ev_ev], sem="VE", inc=1)
    ev_m2 = p.op("scalar", lambda s: s.activation(m2row, mrow[:], AF.Square),
                 deps=[ev_mr], sem="SE", inc=1)
    ev_v1 = p.op("vector", lambda v: v.tensor_sub(vrow[:], vrow[:], m2row),
                 deps=[ev_m2, ev_st], sem="VE", inc=1)
    ev_var = p.op("vector", lambda v: v.tensor_scalar_add(vrow[:], vrow[:], EPS_LN),
                  deps=[ev_v1], sem="VE", inc=1)
    ev_ln = p.op("scalar", lambda s: s.activation(vrow[:], vrow[:], AF.Ln),
                 deps=[ev_var], sem="SE", inc=1)
    ev_lh = p.op("vector", lambda v: v.tensor_scalar_mul(vrow[:], vrow[:], -0.5),
                 deps=[ev_ln], sem="VE", inc=1)
    ev_rstd = p.op("scalar", lambda s: s.activation(vrow[:], vrow[:], AF.Exp),
                   deps=[ev_lh], sem="SE", inc=1)
    ev_sr0 = p.dma(lambda s: s.dma_start(out=srow[0:1, :], in_=mrow[:]), deps=[ev_mr])
    ev_sr1 = p.dma(lambda s: s.dma_start(out=srow[1:2, :], in_=vrow[:]), deps=[ev_rstd])
    ev_mbc = p.dma(lambda s: s.dma_start(out=mbc[:], in_=srow[0:1, :].partition_broadcast(P)),
                   deps=[ev_sr0])
    ev_rbc = p.dma(lambda s: s.dma_start(out=rbc[:], in_=srow[1:2, :].partition_broadcast(P)),
                   deps=[ev_sr1])

    xh_evs, xwr = [], []
    for k in range(KC):
        b = 0
        deps = [mm_done[KC - 1], sqd[KC - 1]]
        if k >= 1:
            deps.append(ev_t1)
        ld2 = p.dma(lambda s, k=k, b=b: s.dma_start(
            out=xt[:, b:b + ncols], in_=x_hbm[k * P:(k + 1) * P, :]),
            deps=deps, stream=tag + "x2")
        out_sb, out_hbm = out_fn(k)

        deps2 = [ld2, ev_mbc]
        if k >= 1:
            deps2.append(ev_x)
        else:
            deps2.append(ev_v1)
        ev_t1 = p.op("vector", lambda v, b=b: v.tensor_sub(t1[:], xt[:, b:b + ncols], mbc[:]),
                     deps=deps2, sem="VE", inc=1)
        deps3 = [ev_t1, ev_rbc]
        if out_hbm is not None and k >= 2:
            deps3.append(xwr[k - 2])
        ev_x = p.op("vector", lambda v, out_sb=out_sb: v.tensor_mul(out_sb, t1[:], rbc[:]),
                    deps=deps3, sem="VE", inc=1)
        if out_hbm is not None:
            xwr.append(p.dma(lambda s, out_hbm=out_hbm, out_sb=out_sb: s.dma_start(
                out=out_hbm, in_=out_sb), deps=[ev_x], stream=tag + "xw"))
            xh_evs.append(xwr[-1])
        else:
            xh_evs.append(ev_x)
    return xh_evs


def build_A():
    nc = bass.Bass("TRN2", target_bir_lowering=False)
    x = nc.dram_tensor("x", [D, TLOC], F32, kind="ExternalInput").ap()
    xhat = nc.dram_tensor("xhat1", [D, TLOC], BF16, kind="ExternalOutput").ap()

    wc_in, wc_out = {}, {}
    for (nm, rows, cols, kd) in WCONV:
        rs = rows // NCORE
        wc_in[nm] = nc.dram_tensor(nm + "_s", [rs, cols], I32 if kd == "i32" else F32,
                                   kind="ExternalInput").ap()
        wc_out[nm] = nc.dram_tensor(nm + "_b", [rs, cols], BF16, kind="ExternalOutput").ap()
    CT = 2048
    conv_tiles = []
    for (nm, rows, cols, kd) in WCONV:
        rs = rows // NCORE
        for r0 in range(0, rs, P):
            nr = min(P, rs - r0)
            for c0 in range(0, cols, CT):
                conv_tiles.append((nm, r0, nr, c0, min(CT, cols - c0), kd))

    import contextlib
    with contextlib.ExitStack() as ctx:
        sb = lambda n, sh, dt: ctx.enter_context(nc.sbuf_tensor(n, sh, dt))
        xh = sb("xh", [P, 2 * TLOC], BF16)
        cvf = sb("cvf", [P, 2 * CT], F32)
        cvi = sb("cvi", [P, 2 * CT], I32)
        cvo = sb("cvo", [P, 2 * CT], BF16)
        psums = [ctx.enter_context(nc.psum_tensor(f"ps{i}", [P, NT], F32)) for i in range(8)]
        p = Prog(nc, ctx)
        wr_evs = []

        def out_fn(k):
            b = (k % 2) * TLOC
            return xh[:, b:b + TLOC], xhat[k * P:(k + 1) * P, :]
        xh_evs = emit_ln(p, nc, ctx, "l1", x, TLOC, psums, out_fn)
        wr_evs += xh_evs

        cvd, cwr = [], []
        for i, (nm, r0, nr, c0, ncol, kd) in enumerate(conv_tiles):
            b = (i % 2) * CT
            deps = []
            if i >= 2:
                deps.append(cvd[i - 2])
            src_ap = wc_in[nm][r0:r0 + nr, c0:c0 + ncol]
            dst = (cvi if kd == "i32" else cvf)[0:nr, b:b + ncol]
            ld = p.dma(lambda s, src_ap=src_ap, dst=dst: s.dma_start(out=dst, in_=src_ap),
                       deps=deps, stream="cv")

            def mk_cv(nm=nm, nr=nr, ncol=ncol, b=b, kd=kd):
                def f(s):
                    if kd == "i32":
                        return s.activation(cvo[0:nr, b:b + ncol], cvi[0:nr, b:b + ncol],
                                            AF.Copy, bias=-128.0)
                    return s.activation(cvo[0:nr, b:b + ncol], cvf[0:nr, b:b + ncol], AF.Copy)
                return f
            deps_c = [ld]
            if i >= 2:
                deps_c.append(cwr[i - 2])
            cvd.append(p.op("scalar", mk_cv(), deps=deps_c, sem="SE", inc=1))
            cwr.append(p.dma(lambda s, nm=nm, r0=r0, nr=nr, c0=c0, ncol=ncol, b=b: s.dma_start(
                out=wc_out[nm][r0:r0 + nr, c0:c0 + ncol], in_=cvo[0:nr, b:b + ncol]),
                deps=[cvd[i]], stream="cw"))
        wr_evs += cwr

        p.op("sync", lambda s: None, deps=wr_evs)
        p.emit()
    return nc


_NC_CACHE = {}


def _get(name, builder):
    if name not in _NC_CACHE:
        _NC_CACHE[name] = builder()
    return _NC_CACHE[name]


def _shard_x(xfull):
    out = []
    for c in range(NCORE):
        b, h = c // 2, c % 2
        sl = xfull[b, h * TLOC:(h + 1) * TLOC, :]
        out.append(np.ascontiguousarray(sl.T, dtype=np.float32))
    return out


def run_A(inp, xcore):
    nc = _get("A", build_A)
    wq = np.ascontiguousarray(inp["L0_attn_Wq"].T)
    wk = np.ascontiguousarray(inp["L0_attn_Wk"].T)
    wv = np.ascontiguousarray(inp["L0_attn_Wv"].T)
    wo = np.ascontiguousarray(inp["L0_attn_Wo"].T)
    g0 = np.ascontiguousarray(inp["L0_ffn_gate_w"].T)
    u0 = np.ascontiguousarray(inp["L0_ffn_up_w"].T)
    d0 = np.ascontiguousarray(inp["L0_ffn_down_w"].T)
    win = np.ascontiguousarray(inp["L1_attn_W_in"])
    wx = np.ascontiguousarray(inp["L1_attn_W_x"])
    wdt = np.ascontiguousarray(inp["L1_attn_W_dt"])
    wout = np.ascontiguousarray(inp["L1_attn_W_out"])
    qg = np.ascontiguousarray(inp["L1_ffn_wq_gate"].T)
    qu = np.ascontiguousarray(inp["L1_ffn_wq_up"].T)
    qd = np.ascontiguousarray(inp["L1_ffn_wq_down"].T)
    full = {"wqT": wq, "wkT": wk, "wvT": wv, "woT": wo, "g0T": g0, "u0T": u0,
            "d0T": d0, "win": win, "wx": wx, "wdt": wdt, "wout": wout,
            "qgT": qg, "quT": qu, "qdT": qd}
    in_maps = []
    for c in range(NCORE):
        m = {"x": xcore[c]}
        for (nm, rows, cols, kd) in WCONV:
            rs = rows // NCORE
            m[nm + "_s"] = np.ascontiguousarray(full[nm][c * rs:(c + 1) * rs])
        in_maps.append(m)
    res = run_bass_kernel_spmd(nc, in_maps, core_ids=list(range(NCORE)))
    wb = {}
    for (nm, rows, cols, kd) in WCONV:
        wb[nm] = np.concatenate([res.results[c][nm + "_b"] for c in range(NCORE)], axis=0)
    xhat1 = [res.results[c]["xhat1"] for c in range(NCORE)]
    return xhat1, wb


def kernel(**inp):
    xcore = _shard_x(np.asarray(inp["L0_attn_x"], dtype=np.float32))
    xhat1, wb = run_A(inp, xcore)
    raise NotImplementedError("launches B..F pending")


def build_B():
    nc = bass.Bass("TRN2", target_bir_lowering=False)
    xhat = nc.dram_tensor("xhat1", [D, TLOC], BF16, kind="ExternalInput").ap()
    wq_arr = nc.dram_tensor("wq_arr", [D, D], BF16, kind="ExternalInput").ap()
    wkT = nc.dram_tensor("wkT", [D, D], BF16, kind="ExternalInput").ap()
    wvT = nc.dram_tensor("wvT", [D, D], BF16, kind="ExternalInput").ap()
    bq = nc.dram_tensor("bq", [1, D], F32, kind="ExternalInput").ap()
    bk = nc.dram_tensor("bk", [1, D], F32, kind="ExternalInput").ap()
    bv = nc.dram_tensor("bv", [1, D], F32, kind="ExternalInput").ap()
    pos_row = nc.dram_tensor("pos_row", [1, TLOC], F32, kind="ExternalInput").ap()
    pos_col = nc.dram_tensor("pos_col", [P, KC], F32, kind="ExternalInput").ap()
    qout = nc.dram_tensor("q", [D, TLOC], BF16, kind="ExternalOutput").ap()
    kvp = nc.dram_tensor("kvp", [P, NH * 129], F32, kind="ExternalOutput").ap()
    drow_h = nc.dram_tensor("drow_h", [1, TLOC], F32, kind="Internal").ap()

    import contextlib
    with contextlib.ExitStack() as ctx:
        sbt = lambda n, sh, dt: ctx.enter_context(nc.sbuf_tensor(n, sh, dt))
        arena = sbt("arena", [P, 2 * KC * TLOC], BF16)
        wkb = arena[:, 0:KC * TLOC]
        wvb = arena[:, KC * TLOC:2 * KC * TLOC]
        xh_tc = sbt("xh_tc", [P, 2 * D], BF16)
        ktile = sbt("ktile", [P, D], BF16)
        VW = NH * 129
        vtile = sbt("vtile", [P, VW], BF16)
        kvsb = sbt("kvsb", [P, NH * 129], F32)
        wq_st = sbt("wq_st", [P, 2 * D], BF16)
        qtile = sbt("qtile", [P, TLOC], BF16)
        brow = sbt("brow", [1, 3 * D], BF16)
        brow_f = sbt("brow_f", [1, D], F32)
        ones_row = sbt("ones_row", [1, NT], BF16)
        posc = sbt("posc", [P, KC], F32)
        drow = sbt("drow", [1, TLOC], F32)
        dcols = sbt("dcols", [P, KC], F32)
        dbc = sbt("dbc", [P, TLOC], F32)
        ps = [ctx.enter_context(nc.psum_tensor(f"ps{i}", [P, NT], F32)) for i in range(8)]
        p = Prog(nc, ctx)
        wr_evs = []

        ev_ones = p.op("vector", lambda v: v.memset(ones_row[:], 1.0), sem="VE", inc=1)
        ev_vones = p.op("vector", lambda v: v.memset(vtile[:], 1.0), sem="VE", inc=1)

        ev_bb = None
        for i, bsrc in enumerate((bq, bk, bv)):
            ld_bi = p.dma(lambda s, bsrc=bsrc: s.dma_start(out=brow_f[:], in_=bsrc[:]),
                          deps=[ev_bb], stream="bia")
            ev_bb = p.op("scalar", lambda s, i=i: s.activation(
                brow[0:1, i * D:(i + 1) * D], brow_f[:], AF.Copy),
                deps=[ld_bi], sem="SE", inc=1)

        ld_pr = p.dma(lambda s: s.dma_start(out=drow[:], in_=pos_row[:]))
        ld_pc = p.dma(lambda s: s.dma_start(out=posc[:], in_=pos_col[:]))
        ev_dr0 = p.op("vector", lambda v: v.tensor_scalar_mul(drow[:], drow[:], -1.0 / NSEQ),
                      deps=[ld_pr], sem="VE", inc=1)
        ev_dr = p.op("scalar", lambda s: s.activation(drow[:], drow[:], AF.Exp),
                     deps=[ev_dr0], sem="SE", inc=1)
        ev_dc0 = p.op("vector", lambda v: v.tensor_scalar_mul(dcols[:], posc[:], -1.0 / NSEQ),
                      deps=[ld_pc], sem="VE", inc=1)
        ev_dc = p.op("scalar", lambda s: s.activation(dcols[:], dcols[:], AF.Exp),
                     deps=[ev_dc0], sem="SE", inc=1)
        ev_dwr = p.dma(lambda s: s.dma_start(out=drow_h[:], in_=drow[:]), deps=[ev_dr])
        ev_dbc = p.dma(lambda s: s.dma_start(
            out=dbc[:], in_=drow_h.partition_broadcast(P)), deps=[ev_dwr])

        wkv_evs = []
        for k in range(KC):
            wkv_evs.append(p.dma(lambda s, k=k: s.dma_start(
                out=arena[:, k * TLOC:(k + 1) * TLOC], in_=wkT[k * P:(k + 1) * P, :]),
                stream="wkv"))
            wkv_evs.append(p.dma(lambda s, k=k: s.dma_start(
                out=arena[:, KC * TLOC + k * TLOC:KC * TLOC + (k + 1) * TLOC],
                in_=wvT[k * P:(k + 1) * P, :]), stream="wkv"))

        NB = 129
        kv_mm, xld = [], []
        ev_evK, ev_evV, kv_add = {}, {}, {}
        for tc in range(KC):
            bx = (tc % 2) * D
            deps = [] if tc < 2 else [kv_mm[tc - 2]]

            def mk_ldx(tc=tc, bx=bx):
                def f(s, h):
                    for k in range(KC):
                        s.dma_start(out=xh_tc[:, bx + k * P:bx + (k + 1) * P],
                                    in_=xhat[k * P:(k + 1) * P, tc * P:(tc + 1) * P]
                                    ).then_inc(h, 16)
                    return None
                return f
            xld.append(p.dma(mk_ldx(), deps=deps, stream="xh", n=KC))

            for dtile in range(4):
                def mk_kv(tc=tc, bx=bx, dtile=dtile):
                    def f(t):
                        t.matmul(ps[0][:, :], ones_row[0:1, 0:P],
                                 brow[0:1, D + dtile * NT:D + (dtile + 1) * NT],
                                 start=True, stop=False)
                        t.matmul(ps[1][:, :], ones_row[0:1, 0:P],
                                 brow[0:1, 2 * D + dtile * NT:2 * D + (dtile + 1) * NT],
                                 start=True, stop=False)
                        last = None
                        for k in range(KC):
                            lhs = xh_tc[:, bx + k * P:bx + (k + 1) * P]
                            t.matmul(ps[0][:, :], lhs,
                                     wkb[:, k * TLOC + dtile * NT:k * TLOC + (dtile + 1) * NT],
                                     start=False, stop=(k == KC - 1))
                            last = t.matmul(ps[1][:, :], lhs,
                                            wvb[:, k * TLOC + dtile * NT:k * TLOC + (dtile + 1) * NT],
                                            start=False, stop=(k == KC - 1))
                        return last
                    return f
                deps_mm = [xld[tc], ev_bb, ev_ones] + wkv_evs
                if dtile > 0:
                    deps_mm += [ev_evK[(tc, dtile - 1)], ev_evV[(tc, dtile - 1)]]
                elif tc > 0:
                    deps_mm += [ev_evK[(tc - 1, 3)], ev_evV[(tc - 1, 3)]]
                if tc == 0 and dtile == 0:
                    deps_mm.append(ev_vones)
                ev_mm = p.op("tensor", mk_kv(), deps=deps_mm, sem="TE", inc=1)

                def mk_evK(tc=tc, dtile=dtile):
                    def f(v):
                        return v.tensor_scalar(
                            ktile[:, dtile * NT:(dtile + 1) * NT], ps[0][:, :],
                            0.0, dcols[:, tc:tc + 1], ALU.max, ALU.mult)
                    return f
                deps_ek = [ev_mm, ev_dc]
                if tc > 0:
                    deps_ek.append(kv_mm[tc - 1])
                ev_evK[(tc, dtile)] = p.op("vector", mk_evK(), deps=deps_ek,
                                           sem="VE", inc=1)

                def mk_evV(tc=tc, dtile=dtile):
                    def f(v):
                        h0 = dtile * 4
                        out = bass.AP(vtile, h0 * NB, [[VW, P], [NB, 4], [1, P]])
                        return v.tensor_copy(out, ps[1][:, :])
                    return f
                deps_ev = [ev_mm]
                if tc > 0:
                    deps_ev.append(kv_mm[tc - 1])
                ev_evV[(tc, dtile)] = p.op("vector", mk_evV(), deps=deps_ev,
                                           sem="VE", inc=1)

            for h in range(NH):
                pb = 2 + (h % 2)

                def mk_kvh(tc=tc, h=h, pb=pb):
                    def f(t):
                        return t.matmul(ps[pb][:, 0:NB], ktile[:, h * P:(h + 1) * P],
                                        vtile[:, h * NB:(h + 1) * NB],
                                        start=True, stop=True)
                    return f
                deps_h = [ev_evK[(tc, 3)], ev_evV[(tc, 3)]]
                if h >= 2:
                    deps_h.append(kv_add[(tc, h - 2)])
                elif tc > 0:
                    deps_h.append(kv_add[(tc - 1, NH - 2 + h)])
                kv_h = p.op("tensor", mk_kvh(), deps=deps_h, sem="TE", inc=1)

                def mk_add(tc=tc, h=h, pb=pb):
                    def f(v):
                        if tc == 0:
                            return v.tensor_copy(kvsb[:, h * NB:(h + 1) * NB], ps[pb][:, 0:NB])
                        return v.tensor_add(kvsb[:, h * NB:(h + 1) * NB],
                                            kvsb[:, h * NB:(h + 1) * NB], ps[pb][:, 0:NB])
                    return f
                kv_add[(tc, h)] = p.op("vector", mk_add(), deps=[kv_h], sem="VE", inc=1)
            kv_mm.append(kv_add[(tc, NH - 1)])

        ev_evkv = kv_add[(KC - 1, NH - 1)]
        kv_wrs = [p.dma(lambda s: s.dma_start(out=kvp[:], in_=kvsb[:]),
                        deps=[ev_evkv], stream="kvw")]
        wr_evs += kv_wrs

        xres_evs = []
        for k in range(KC):
            xres_evs.append(p.dma(lambda s, k=k: s.dma_start(
                out=arena[:, k * TLOC:(k + 1) * TLOC], in_=xhat[k * P:(k + 1) * P, :]),
                deps=[kv_mm[KC - 1]], stream="xr"))
        q_wr, qmm, ev_evq = [], [], []
        for g in range(KC):
            bs = (g % 2) * D
            bq_t = 0
            deps_st = [] if g < 2 else [qmm[g - 2]]
            st_ld = p.dma(lambda s, g=g, bs=bs: s.dma_start(
                out=wq_st[:, bs:bs + D], in_=wq_arr[g * P:(g + 1) * P, :]),
                deps=deps_st, stream="qst")

            def mk_qmm(g=g, bs=bs):
                def f(t):
                    last = None
                    for tt in range(NTT):
                        t.matmul(ps[tt][:, :], brow[0:1, g * P:(g + 1) * P],
                                 ones_row[0:1, 0:NT], start=True, stop=False)
                    for k in range(KC):
                        for tt in range(NTT):
                            last = t.matmul(ps[tt][:, :], wq_st[:, bs + k * P:bs + (k + 1) * P],
                                            arena[:, k * TLOC + tt * NT:k * TLOC + (tt + 1) * NT],
                                            start=False, stop=(k == KC - 1))
                    return last
                return f
            deps_q = [st_ld, ev_bb] + xres_evs
            if g >= 1:
                deps_q.append(ev_evq[g - 1])
            if g == 0:
                deps_q.append(ev_evkv)
            qmm.append(p.op("tensor", mk_qmm(), deps=deps_q, sem="TE", inc=1))

            def mk_evq(g=g, bq_t=bq_t):
                def f(v):
                    last = None
                    for tt in range(NTT):
                        last = v.scalar_tensor_tensor(
                            qtile[:, bq_t + tt * NT:bq_t + (tt + 1) * NT],
                            ps[tt][:, :], 0.0, dbc[:, tt * NT:(tt + 1) * NT],
                            ALU.max, ALU.mult)
                    return last
                return f
            deps_eq = [qmm[g], ev_dbc]
            if g >= 1:
                deps_eq.append(q_wr[g - 1])
            ev_evq.append(p.op("vector", mk_evq(), deps=deps_eq, sem="VE", inc=1))
            q_wr.append(p.dma(lambda s, g=g, bq_t=bq_t: s.dma_start(
                out=qout[g * P:(g + 1) * P, :], in_=qtile[:, bq_t:bq_t + TLOC]),
                deps=[ev_evq[g]], stream="qw"))
        wr_evs += q_wr

        p.op("sync", lambda s: None, deps=wr_evs)
        p.emit()
    return nc


def build_C1():
    nc = bass.Bass("TRN2", target_bir_lowering=False)
    x = nc.dram_tensor("x", [D, TLOC], F32, kind="ExternalInput").ap()
    q = nc.dram_tensor("q", [D, TLOC], BF16, kind="ExternalInput").ap()
    kvp_s = nc.dram_tensor("kvp_s", [P, NH * 129], F32, kind="ExternalInput").ap()
    kvp_p = nc.dram_tensor("kvp_p", [P, NH * 129], F32, kind="ExternalInput").ap()
    wo_arr = nc.dram_tensor("wo_arr", [D, D], BF16, kind="ExternalInput").ap()
    bo = nc.dram_tensor("bo", [1, D], F32, kind="ExternalInput").ap()
    x2o = nc.dram_tensor("x2", [D, TLOC], F32, kind="ExternalOutput").ap()
    zh = nc.dram_tensor("zh", [NH, TLOC], F32, kind="Internal").ap()
    NB = 129

    import contextlib
    with contextlib.ExitStack() as ctx:
        sbt = lambda n, sh, dt: ctx.enter_context(nc.sbuf_tensor(n, sh, dt))
        arena = sbt("arena", [P, KC * TLOC], BF16)
        kv_f = sbt("kv_f", [P, NH * NB], F32)
        kv2 = sbt("kv2", [P, NH * NB], F32)
        kvb = sbt("kvb", [P, NH * NB], BF16)
        zb = sbt("zb", [P, 2 * TLOC], F32)
        dnrow = sbt("dnrow", [1, TLOC], F32)
        wst = sbt("wst", [P, 2 * D], BF16)
        xrt = sbt("xrt", [P, 2 * TLOC], F32)
        x2t = sbt("x2t", [P, 2 * NT], F32)
        brow_f = sbt("brow_f", [1, D], F32)
        bo_b = sbt("bo_b", [1, D], BF16)
        ones_row = sbt("ones_row", [1, NT], BF16)
        ps = [ctx.enter_context(nc.psum_tensor(f"ps{i}", [P, NT], F32)) for i in range(6)]
        p = Prog(nc, ctx)
        wr_evs = []

        ev_ones = p.op("vector", lambda v: v.memset(ones_row[:], 1.0), sem="VE", inc=1)
        ld_bo = p.dma(lambda s: s.dma_start(out=brow_f[:], in_=bo[:]))
        ev_bo = p.op("scalar", lambda s: s.activation(bo_b[:], brow_f[:], AF.Copy),
                     deps=[ld_bo], sem="SE", inc=1)

        ld_kv1 = p.dma(lambda s: s.dma_start(out=kv_f[:], in_=kvp_s[:]))
        ld_kv2 = p.dma(lambda s: s.dma_start(out=kv2[:], in_=kvp_p[:]))
        ev_kvs = p.op("vector", lambda v: v.tensor_add(kv_f[:], kv_f[:], kv2[:]),
                      deps=[ld_kv1, ld_kv2], sem="VE", inc=1)
        ev_kvb = p.op("scalar", lambda s: s.activation(kvb[:], kv_f[:], AF.Copy),
                      deps=[ev_kvs], sem="SE", inc=1)

        q_evs = [p.dma(lambda s, k=k: s.dma_start(
            out=arena[:, k * TLOC:(k + 1) * TLOC], in_=q[k * P:(k + 1) * P, :]),
            stream="q") for k in range(KC)]

        z_wr = []
        ev_dn_prev = None
        dn_ev_evac = {}
        for h in range(NH):
            ev_tt_mm = []
            for tt in range(NTT):
                pb = 4 + (tt % 2)

                def mk_dn1(h=h, tt=tt, pb=pb):
                    def f(t):
                        return t.matmul(ps[pb][0:1, :],
                                        kvb[:, h * NB + 128:h * NB + NB],
                                        arena[:, h * TLOC + tt * NT:h * TLOC + (tt + 1) * NT],
                                        start=True, stop=True)
                    return f
                deps_d = [ev_kvb] + q_evs
                key_prev = (h, tt - 2) if tt >= 2 else ((h - 1, tt + 2) if h > 0 else None)
                if key_prev is not None and key_prev in dn_ev_evac:
                    deps_d.append(dn_ev_evac[key_prev])
                ev_mm = p.op("tensor", mk_dn1(), deps=deps_d, sem="TE", inc=1)
                ev_tt_mm.append(ev_mm)
                deps_ev = [ev_mm]
                if h > 0:
                    deps_ev.append(z_wr[h - 1])
                dn_ev_evac[(h, tt)] = p.op("vector", lambda v, tt=tt, pb=pb: v.tensor_copy(
                    dnrow[0:1, tt * NT:(tt + 1) * NT], ps[pb][0:1, :]),
                    deps=deps_ev, sem="VE", inc=1)

            deps_z0 = [dn_ev_evac[(h, tt)] for tt in range(NTT)]
            if h > 0:
                deps_z0.append(z_wr[h - 1])
            ev_z0 = p.op("vector", lambda v: v.tensor_scalar_add(dnrow[:], dnrow[:], EPS_ATT),
                         deps=deps_z0, sem="VE", inc=1)
            ev_z = p.op("vector", lambda v: v.reciprocal(dnrow[:], dnrow[:]),
                        deps=[ev_z0], sem="VE", inc=1)
            z_wr.append(p.dma(lambda s, h=h: s.dma_start(
                out=zh[h:h + 1, :], in_=dnrow[:]), deps=[ev_z], stream="zw"))
            ev_dn_prev = ev_z
        ev_dn_all = ev_dn_prev

        att_evs = []
        zld = []
        for h in range(NH):
            bz = (h % 2) * TLOC
            deps_z = [z_wr[h]]
            if h >= 2:
                deps_z.append(att_evs[h - 2])
            zld.append(p.dma(lambda s, h=h, bz=bz: s.dma_start(
                out=zb[:, bz:bz + TLOC], in_=zh[h:h + 1, :].partition_broadcast(P)),
                deps=deps_z, stream="zb"))
            ev_prev_tt = None
            for tt in range(NTT):
                pb = tt % 2

                def mk_am(h=h, tt=tt, pb=pb):
                    def f(t):
                        return t.matmul(ps[pb][:, :], kvb[:, h * NB:h * NB + 128],
                                        arena[:, h * TLOC + tt * NT:h * TLOC + (tt + 1) * NT],
                                        start=True, stop=True)
                    return f
                deps_am = [ev_kvb, ev_dn_all]
                if ev_prev_tt is not None:
                    deps_am.append(ev_prev_tt)
                if tt >= 2:
                    deps_am.append(ev_ev_tt[tt - 2])
                elif h > 0:
                    deps_am.append(prev_ev_ev[tt + 2])
                ev_am = p.op("tensor", mk_am(), deps=[d for d in deps_am if d],
                             sem="TE", inc=1)

                def mk_ea(h=h, tt=tt, pb=pb, bz=bz):
                    def f(v):
                        return v.tensor_mul(
                            arena[:, h * TLOC + tt * NT:h * TLOC + (tt + 1) * NT],
                            ps[pb][:, :], zb[:, bz + tt * NT:bz + (tt + 1) * NT])
                    return f
                if tt == 0:
                    ev_ev_tt = [None] * NTT
                ev_ev_tt[tt] = p.op("vector", mk_ea(), deps=[ev_am, zld[h]],
                                    sem="VE", inc=1)
                ev_prev_tt = ev_am
            prev_ev_ev = list(ev_ev_tt)
            att_evs.append(ev_ev_tt[NTT - 1])
        att_all = [e for e in prev_ev_ev] + att_evs

        x2_wr, womm, ev_evo = [], [], []
        xr_ld = []
        for g in range(KC):
            bs = (g % 2) * D
            bx = (g % 2) * TLOC
            deps_st = [] if g < 2 else [womm[g - 2]]
            st_ld = p.dma(lambda s, g=g, bs=bs: s.dma_start(
                out=wst[:, bs:bs + D], in_=wo_arr[g * P:(g + 1) * P, :]),
                deps=deps_st, stream="wst")
            deps_xr = [] if g < 2 else [ev_evo[g - 2]]
            xr_ld.append(p.dma(lambda s, g=g, bx=bx: s.dma_start(
                out=xrt[:, bx:bx + TLOC], in_=x[g * P:(g + 1) * P, :]),
                deps=deps_xr, stream="xr"))

            ev_o_tts = []
            for tt in range(NTT):
                pb = 2 + (tt % 2)

                def mk_wog(g=g, bs=bs, tt=tt, pb=pb):
                    def f(t):
                        t.matmul(ps[pb][:, :], bo_b[0:1, g * P:(g + 1) * P],
                                 ones_row[0:1, 0:NT], start=True, stop=False)
                        last = None
                        for k in range(KC):
                            last = t.matmul(ps[pb][:, :], wst[:, bs + k * P:bs + (k + 1) * P],
                                            arena[:, k * TLOC + tt * NT:k * TLOC + (tt + 1) * NT],
                                            start=False, stop=(k == KC - 1))
                        return last
                    return f
                deps_wo = [st_ld, ev_bo, ev_ones] + att_all
                if tt >= 2:
                    deps_wo.append(ev_o_tts[tt - 2])
                elif g > 0:
                    deps_wo.append(prev_o_tts[tt])
                mm_ev = p.op("tensor", mk_wog(), deps=deps_wo, sem="TE", inc=1)
                if tt == NTT - 1:
                    womm.append(mm_ev)

                def mk_eo(g=g, tt=tt, pb=pb, bx=bx):
                    def f(v):
                        bt2 = (tt % 2) * NT
                        return v.tensor_add(x2t[:, bt2:bt2 + NT], ps[pb][:, :],
                                            xrt[:, bx + tt * NT:bx + (tt + 1) * NT])
                    return f
                deps_eo = [mm_ev, xr_ld[g]]
                if len(x2_wr) >= 2:
                    deps_eo.append(x2_wr[-2])
                ev_e = p.op("vector", mk_eo(), deps=deps_eo, sem="VE", inc=1)
                ev_o_tts.append(ev_e)
                x2_wr.append(p.dma(lambda s, g=g, tt=tt: s.dma_start(
                    out=x2o[g * P:(g + 1) * P, tt * NT:(tt + 1) * NT],
                    in_=x2t[:, (tt % 2) * NT:(tt % 2) * NT + NT]),
                    deps=[ev_e], stream="x2w"))
            prev_o_tts = ev_o_tts
            ev_evo.append(ev_o_tts[NTT - 1])
        wr_evs += x2_wr

        p.op("sync", lambda s: None, deps=wr_evs)
        p.emit()
    return nc


def build_C2():
    nc = bass.Bass("TRN2", target_bir_lowering=False)
    x2 = nc.dram_tensor("x2", [D, TLOC], F32, kind="ExternalInput").ap()
    g_arr = nc.dram_tensor("g_arr", [INTER, D], BF16, kind="ExternalInput").ap()
    u_arr = nc.dram_tensor("u_arr", [INTER, D], BF16, kind="ExternalInput").ap()
    d_arr = nc.dram_tensor("d_arr", [D, INTER], BF16, kind="ExternalInput").ap()
    gs_c = nc.dram_tensor("gs_c", [P, 44], F32, kind="ExternalInput").ap()
    us_c = nc.dram_tensor("us_c", [P, 44], F32, kind="ExternalInput").ap()
    ds_c = nc.dram_tensor("ds_c", [P, KC], F32, kind="ExternalInput").ap()
    x3o = nc.dram_tensor("x3", [D, TLOC], F32, kind="ExternalOutput").ap()
    NI = INTER // P

    import contextlib
    with contextlib.ExitStack() as ctx:
        sbt = lambda n, sh, dt: ctx.enter_context(nc.sbuf_tensor(n, sh, dt))
        arena = sbt("arena", [P, KC * TLOC], BF16)
        hdn = sbt("hdn", [P, NI * NT], BF16)
        wpool = sbt("wpool", [P, 2 * INTER], BF16)
        gst = wpool[:, 0:2 * D]
        ust = wpool[:, 2 * D:4 * D]
        dst = wpool[:, 0:2 * INTER]
        gt = sbt("gt", [P, NT], BF16)
        ut = sbt("ut", [P, NT], BF16)
        x2r = sbt("x2r", [P, 2 * NT], F32)
        x3t = sbt("x3t", [P, 2 * NT], F32)
        gsc = sbt("gsc", [P, NI], F32)
        usc = sbt("usc", [P, NI], F32)
        dsc = sbt("dsc", [P, KC], F32)
        ps = [ctx.enter_context(nc.psum_tensor(f"ps{i}", [P, NT], F32)) for i in range(8)]
        p = Prog(nc, ctx)
        wr_evs = []

        ld_gs = p.dma(lambda s: s.dma_start(out=gsc[:], in_=gs_c[:]))
        ld_us = p.dma(lambda s: s.dma_start(out=usc[:], in_=us_c[:]))
        ld_ds = p.dma(lambda s: s.dma_start(out=dsc[:], in_=ds_c[:]))

        def out_fn(k):
            return arena[:, k * TLOC:(k + 1) * TLOC], None
        xh_evs = emit_ln(p, nc, ctx, "l2", x2, TLOC, ps, out_fn)

        x3_wr = []
        last_down_of_tt = {}
        hd_evs = {}
        for tt in range(NTT):
            for i in range(NI):
                bgs = (i % 2) * D

                def mk_ldg(i=i, bgs=bgs):
                    def f(s):
                        return s.dma_start(out=gst[:, bgs:bgs + D],
                                           in_=g_arr[i * P:(i + 1) * P, :])
                    return f

                def mk_ldu(i=i, bgs=bgs):
                    def f(s):
                        return s.dma_start(out=ust[:, bgs:bgs + D],
                                           in_=u_arr[i * P:(i + 1) * P, :])
                    return f
                idx = tt * NI + i
                deps_l = [] if idx < 2 else [gmm_evs[idx - 2]]
                if tt > 0 and i < 2:
                    deps_l = deps_l + [dmm_evs[tt * KC - 1]]
                ld_g = p.dma(mk_ldg(), deps=deps_l, stream="gst")
                ld_u = p.dma(mk_ldu(), deps=deps_l, stream="ust")

                def mk_gu(i=i, tt=tt, bgs=bgs):
                    def f(t):
                        last = None
                        for k in range(KC):
                            t.matmul(ps[0][:, :], gst[:, bgs + k * P:bgs + (k + 1) * P],
                                     arena[:, k * TLOC + tt * NT:k * TLOC + (tt + 1) * NT],
                                     start=(k == 0), stop=(k == KC - 1))
                            last = t.matmul(ps[1][:, :], ust[:, bgs + k * P:bgs + (k + 1) * P],
                                            arena[:, k * TLOC + tt * NT:k * TLOC + (tt + 1) * NT],
                                            start=(k == 0), stop=(k == KC - 1))
                        return last
                    return f
                deps_g = [ld_g, ld_u] + xh_evs
                if idx == 0:
                    gmm_evs = []
                if idx >= 1:
                    deps_g.append(ev_hd)
                gmm_evs.append(p.op("tensor", mk_gu(), deps=deps_g, sem="TE", inc=1))

                ev_g = p.op("scalar", lambda s, i=i: s.activation(
                    gt[:], ps[0][:, :], AF.Silu, scale=gsc[:, i:i + 1]),
                    deps=[gmm_evs[idx], ld_gs], sem="SE", inc=1)
                ev_u = p.op("vector", lambda v, i=i: v.tensor_scalar_mul(
                    ut[:], ps[1][:, :], usc[:, i:i + 1]),
                    deps=[gmm_evs[idx], ld_us], sem="VE", inc=1)

                def mk_hd(i=i, tt=tt):
                    def f(v):
                        return v.tensor_mul(hdn[:, i * NT:(i + 1) * NT], gt[:], ut[:])
                    return f
                deps_h = [ev_g, ev_u]
                if tt > 0:
                    deps_h.append(last_down_of_tt[tt - 1])
                ev_hd = p.op("vector", mk_hd(), deps=deps_h, sem="VE", inc=1)
                hd_evs[(tt, i)] = ev_hd

            for g in range(KC):
                bds = (g % 2) * INTER
                idx2 = tt * KC + g
                deps_ld = [] if idx2 < 2 else [dmm_evs[idx2 - 2]]
                if g < 2:
                    deps_ld = deps_ld + [gmm_evs[tt * NI + NI - 1],
                                         hd_evs[(tt, NI - 1)], hd_evs[(tt, NI - 2)]]
                ld_d = p.dma(lambda s, g=g, bds=bds: s.dma_start(
                    out=dst[:, bds:bds + INTER], in_=d_arr[g * P:(g + 1) * P, :]),
                    deps=deps_ld, stream="dst")
                deps_x2 = [] if idx2 < 2 else [x3_wr[idx2 - 2]]
                ld_x2 = p.dma(lambda s, g=g, tt=tt, bds=bds: s.dma_start(
                    out=x2r[:, (g % 2) * NT:(g % 2) * NT + NT],
                    in_=x2[g * P:(g + 1) * P, tt * NT:(tt + 1) * NT]),
                    deps=deps_x2, stream="x2r")

                def mk_dn(g=g, tt=tt, bds=bds):
                    def f(t):
                        pb = 2 + (g % 2)
                        last = None
                        for i in range(NI):
                            last = t.matmul(ps[pb][:, :], dst[:, bds + i * P:bds + (i + 1) * P],
                                            hdn[:, i * NT:(i + 1) * NT],
                                            start=(i == 0), stop=(i == NI - 1))
                        return last
                    return f
                deps_d = [ld_d] + [hd_evs[(tt, i)] for i in range(NI)]
                if idx2 == 0:
                    dmm_evs = []
                if idx2 >= 2:
                    deps_d.append(ev_x3s[idx2 - 2])
                dmm_evs.append(p.op("tensor", mk_dn(), deps=deps_d, sem="TE", inc=1))

                def mk_x3(g=g, tt=tt):
                    def f(v):
                        pb = 2 + (g % 2)
                        bo2 = (g % 2) * NT
                        return v.scalar_tensor_tensor(
                            x3t[:, bo2:bo2 + NT], ps[pb][:, :], dsc[:, g:g + 1],
                            x2r[:, bo2:bo2 + NT], ALU.mult, ALU.add)
                    return f
                if idx2 == 0:
                    ev_x3s = []
                ev_x3 = p.op("vector", mk_x3(), deps=[dmm_evs[idx2], ld_x2, ld_ds],
                             sem="VE", inc=1)
                ev_x3s.append(ev_x3)
                x3_wr.append(p.dma(lambda s, g=g, tt=tt: s.dma_start(
                    out=x3o[g * P:(g + 1) * P, tt * NT:(tt + 1) * NT],
                    in_=x3t[:, (g % 2) * NT:(g % 2) * NT + NT]),
                    deps=[ev_x3], stream="x3w"))
                last_down_of_tt[tt] = dmm_evs[idx2]
        wr_evs += x3_wr

        p.op("sync", lambda s: None, deps=wr_evs)
        p.emit()
    return nc


def build_D1():
    nc = bass.Bass("TRN2", target_bir_lowering=False)
    x3 = nc.dram_tensor("x3", [D, TLOC], F32, kind="ExternalInput").ap()
    win_arr = nc.dram_tensor("win_arr", [2 * DP, D], BF16, kind="ExternalInput").ap()
    xpo = nc.dram_tensor("xp", [DP, TLOC], F32, kind="ExternalOutput").ap()
    szo = nc.dram_tensor("sz", [DP, TLOC], BF16, kind="ExternalOutput").ap()
    NG = 2 * DP // P

    import contextlib
    with contextlib.ExitStack() as ctx:
        sbt = lambda n, sh, dt: ctx.enter_context(nc.sbuf_tensor(n, sh, dt))
        arena = sbt("arena", [P, KC * TLOC], BF16)
        wst = sbt("wst", [P, 2 * D], BF16)
        xpt = sbt("xpt", [P, 2 * NT], F32)
        szt = sbt("szt", [P, 2 * NT], BF16)
        ps = [ctx.enter_context(nc.psum_tensor(f"ps{i}", [P, NT], F32)) for i in range(8)]
        p = Prog(nc, ctx)
        wr_evs = []

        def out_fn(k):
            return arena[:, k * TLOC:(k + 1) * TLOC], None
        xh_evs = emit_ln(p, nc, ctx, "l3", x3, TLOC, ps, out_fn)

        out_wr = []
        for g in range(NG):
            bs = (g % 2) * D
            idx0 = g * NTT
            deps_st = [] if g < 2 else [mm_evs[(g - 2) * NTT + NTT - 1]]
            st_ld = p.dma(lambda s, g=g, bs=bs: s.dma_start(
                out=wst[:, bs:bs + D], in_=win_arr[g * P:(g + 1) * P, :]),
                deps=deps_st, stream="wst")
            for tt in range(NTT):
                idx = idx0 + tt
                pb = idx % 8

                def mk_mm(g=g, bs=bs, tt=tt, pb=pb):
                    def f(t):
                        last = None
                        for k in range(KC):
                            last = t.matmul(ps[pb][:, :], wst[:, bs + k * P:bs + (k + 1) * P],
                                            arena[:, k * TLOC + tt * NT:k * TLOC + (tt + 1) * NT],
                                            start=(k == 0), stop=(k == KC - 1))
                        return last
                    return f
                deps_m = [st_ld] + xh_evs
                if idx == 0:
                    mm_evs, ev_evs = [], []
                if idx >= 8:
                    deps_m.append(ev_evs[idx - 8])
                mm_evs.append(p.op("tensor", mk_mm(), deps=deps_m, sem="TE", inc=1))

                if g < NG // 2:
                    def mk_ev(g=g, tt=tt, pb=pb):
                        def f(v):
                            bo = (idx % 2) * NT if False else (tt % 2) * NT
                            return v.tensor_copy(xpt[:, (tt % 2) * NT:(tt % 2) * NT + NT],
                                                 ps[pb][:, :])
                        return f
                    deps_e = [mm_evs[idx]]
                    if len(out_wr) >= 2:
                        deps_e.append(out_wr[-2])
                    ev_e = p.op("vector", mk_ev(), deps=deps_e, sem="VE", inc=1)
                    out_wr.append(p.dma(lambda s, g=g, tt=tt: s.dma_start(
                        out=xpo[g * P:(g + 1) * P, tt * NT:(tt + 1) * NT],
                        in_=xpt[:, (tt % 2) * NT:(tt % 2) * NT + NT]),
                        deps=[ev_e], stream="xw"))
                else:
                    def mk_ev(g=g, tt=tt, pb=pb):
                        def f(s):
                            return s.activation(szt[:, (tt % 2) * NT:(tt % 2) * NT + NT],
                                                ps[pb][:, :], AF.Silu)
                        return f
                    deps_e = [mm_evs[idx]]
                    if len(out_wr) >= 2:
                        deps_e.append(out_wr[-2])
                    ev_e = p.op("scalar", mk_ev(), deps=deps_e, sem="SE", inc=1)
                    g2 = g - NG // 2
                    out_wr.append(p.dma(lambda s, g2=g2, tt=tt: s.dma_start(
                        out=szo[g2 * P:(g2 + 1) * P, tt * NT:(tt + 1) * NT],
                        in_=szt[:, (tt % 2) * NT:(tt % 2) * NT + NT]),
                        deps=[ev_e], stream="zw"))
                ev_evs.append(ev_e)
        wr_evs += out_wr

        p.op("sync", lambda s: None, deps=wr_evs)
        p.emit()
    return nc


D2_STAGE = 3
def build_D2():
    nc = bass.Bass("TRN2", target_bir_lowering=False)
    xpe = nc.dram_tensor("xpe", [DP, TLOC + 2], F32, kind="ExternalInput").ap()
    cw_c = nc.dram_tensor("cw_c", [P, 3 * 32], F32, kind="ExternalInput").ap()
    cb_c = nc.dram_tensor("cb_c", [P, 32], F32, kind="ExternalInput").ap()
    wx_b = nc.dram_tensor("wx_b", [DP, 160], BF16, kind="ExternalInput").ap()
    wdt_b = nc.dram_tensor("wdt_b", [P, DP], BF16, kind="ExternalInput").ap()
    bdt_c = nc.dram_tensor("bdt_c", [P, 32], F32, kind="ExternalInput").ap()
    xco = nc.dram_tensor("xc", [DP, TLOC], F32, kind="ExternalOutput").ap()
    dto = nc.dram_tensor("dt", [DP, TLOC], F32, kind="ExternalOutput").ap()
    bco = nc.dram_tensor("bc", [32, TLOC], F32, kind="ExternalOutput").ap()
    bcbo = nc.dram_tensor("bcb", [32, TLOC], BF16, kind="ExternalOutput").ap()
    NG = DP // P
    TE = TLOC + 2

    import contextlib
    with contextlib.ExitStack() as ctx:
        sbt = lambda n, sh, dt: ctx.enter_context(nc.sbuf_tensor(n, sh, dt))
        xpt = sbt("xpt", [P, 2 * TE], F32)
        cwc = sbt("cwc", [P, 3 * 32], F32)
        cbc = sbt("cbc", [P, 32], F32)
        bdt = sbt("bdt", [P, 32], F32)
        c1 = sbt("c1", [P, TLOC], F32)
        c2t = sbt("c2t", [P, TLOC], F32)
        xct = sbt("xct", [P, 2 * TLOC], F32)
        xcb = sbt("xcb", [P, 2 * TLOC], BF16)
        wxt = sbt("wxt", [P, 2 * 160], BF16)
        xdd = sbt("xdd", [P, TLOC], F32)
        xdb = sbt("xdb", [32, TLOC], F32)
        xdbb = sbt("xdbb", [32, TLOC], BF16)
        xdbB = sbt("xdbB", [P, TLOC], BF16)
        wdt_t = sbt("wdt_t", [P, DP], BF16)
        mt = sbt("mt", [P, 2 * NT], F32)
        ut = sbt("ut", [P, 2 * NT], F32)
        et = sbt("et", [P, 2 * NT], F32)
        dtt = sbt("dtt", [P, 2 * NT], F32)
        ps = [ctx.enter_context(nc.psum_tensor(f"ps{i}", [P, NT], F32)) for i in range(8)]
        p = Prog(nc, ctx)
        wr_evs = []

        ld_cw = p.dma(lambda s: s.dma_start(out=cwc[:], in_=cw_c[:]))
        ld_cb = p.dma(lambda s: s.dma_start(out=cbc[:], in_=cb_c[:]))
        ld_bdt = p.dma(lambda s: s.dma_start(out=bdt[:], in_=bdt_c[:]))
        ld_wdt = p.dma(lambda s: s.dma_start(out=wdt_t[:], in_=wdt_b[:]))

        xc_wr = []
        xdbl_done = []
        for g in range(NG):
            bx = (g % 2) * TE
            bc_ = (g % 2) * TLOC
            deps_x = [] if g < 2 else [xcb_ev[g - 2]]
            ld_x = p.dma(lambda s, g=g, bx=bx: s.dma_start(
                out=xpt[:, bx:bx + TE], in_=xpe[g * P:(g + 1) * P, :]),
                deps=deps_x, stream="xp")
            deps_w = [] if g < 2 else [xdbl_done[g - 2]]
            ld_w = p.dma(lambda s, g=g: s.dma_start(
                out=wxt[:, (g % 2) * 160:(g % 2) * 160 + 160],
                in_=wx_b[g * P:(g + 1) * P, :]),
                deps=deps_w, stream="wx")

            deps_c = [ld_x, ld_cw]
            if g >= 1:
                deps_c.append(ev_c2[0] if False else ev_c3)
            ev_c1 = p.op("vector", lambda v, bx=bx, g=g: v.tensor_scalar_mul(
                c1[:], xpt[:, bx:bx + TLOC], cwc[:, g:g + 1]),
                deps=[ld_x, ld_cw] + ([ev_c3] if g >= 1 else []), sem="VE", inc=1)
            ev_c2 = p.op("vector", lambda v, bx=bx, g=g: v.scalar_tensor_tensor(
                c2t[:], xpt[:, bx + 1:bx + 1 + TLOC], cwc[:, 32 + g:32 + g + 1], c1[:],
                ALU.mult, ALU.add), deps=[ev_c1], sem="VE", inc=1)
            ev_c3 = p.op("vector", lambda v, bx=bx, g=g, bc_=bc_: v.scalar_tensor_tensor(
                xct[:, bc_:bc_ + TLOC], xpt[:, bx + 2:bx + 2 + TLOC],
                cwc[:, 64 + g:64 + g + 1], c2t[:], ALU.mult, ALU.add),
                deps=[ev_c2] + ([xc_wr[g - 2]] if g >= 2 else []), sem="VE", inc=1)
            ev_xc = p.op("scalar", lambda s, g=g, bc_=bc_: s.activation(
                xct[:, bc_:bc_ + TLOC], xct[:, bc_:bc_ + TLOC], AF.Silu,
                bias=cbc[:, g:g + 1]), deps=[ev_c3, ld_cb], sem="SE", inc=1)
            xc_wr.append(p.dma(lambda s, g=g, bc_=bc_: s.dma_start(
                out=xco[g * P:(g + 1) * P, :], in_=xct[:, bc_:bc_ + TLOC]),
                deps=[ev_xc], stream="xcw"))
            if g == 0:
                xcb_ev = []
            deps_cb = [ev_xc]
            if g >= 2:
                deps_cb.append(xdbl_done[g - 2])
            xcb_ev.append(p.op("vector", lambda v, g=g, bc_=bc_: v.tensor_copy(
                xcb[:, bc_:bc_ + TLOC], xct[:, bc_:bc_ + TLOC]),
                deps=deps_cb, sem="VE", inc=1))

            def mk_xd(g=g, bc_=bc_):
                def f(t):
                    last = None
                    for tt in range(NTT):
                        t.matmul(ps[tt][:, :],
                                 wxt[:, (g % 2) * 160:(g % 2) * 160 + 128],
                                 xcb[:, bc_ + tt * NT:bc_ + (tt + 1) * NT],
                                 start=(g == 0), stop=(g == NG - 1))
                        last = t.matmul(ps[4 + tt][0:32, :],
                                        wxt[:, (g % 2) * 160 + 128:(g % 2) * 160 + 160],
                                        xcb[:, bc_ + tt * NT:bc_ + (tt + 1) * NT],
                                        start=(g == 0), stop=(g == NG - 1))
                    return last
                return f
            xdbl_done.append(p.op("tensor", mk_xd(), deps=[xcb_ev[g], ld_w],
                                  sem="TE", inc=1))

        if D2_STAGE == 1:
            wr_evs += xc_wr
            p.op("sync", lambda s: None, deps=wr_evs)
            p.emit()
            return nc
        def mk_exd(v):
            for tt in range(NTT):
                v.tensor_copy(xdd[:, tt * NT:(tt + 1) * NT], ps[tt][:, :])
            for tt in range(NTT - 1):
                v.tensor_copy(xdb[0:32, tt * NT:(tt + 1) * NT], ps[4 + tt][0:32, :])
            return v.tensor_copy(xdb[0:32, (NTT - 1) * NT:NTT * NT], ps[7][0:32, :])
        ev_exd = p.op("vector", mk_exd, deps=[xdbl_done[NG - 1]], sem="VE", inc=1)
        wr_evs.append(p.dma(lambda s: s.dma_start(out=bco[:], in_=xdb[0:32, :]),
                            deps=[ev_exd]))
        ev_bcb = p.op("scalar", lambda s: s.activation(xdbb[:], xdb[0:32, :], AF.Copy),
                      deps=[ev_exd], sem="SE", inc=1)
        wr_evs.append(p.dma(lambda s: s.dma_start(out=bcbo[:], in_=xdbb[:]),
                            deps=[ev_bcb]))
        ev_xdbB = p.op("scalar", lambda s: s.activation(xdbB[:], xdd[:], AF.Copy),
                       deps=[ev_exd], sem="SE", inc=1)

        if D2_STAGE == 2:
            wr_evs += xc_wr
            p.op("sync", lambda s: None, deps=wr_evs + [ev_xdbB])
            p.emit()
            return nc
        dt_wr = []
        for j in range(NG):
            for tt in range(NTT):
                idx = j * NTT + tt
                pb = idx % 2
                bo = pb * NT

                def mk_dm(j=j, tt=tt, pb=pb):
                    def f(t):
                        return t.matmul(ps[pb][:, :], wdt_t[:, j * P:(j + 1) * P],
                                        xdbB[:, tt * NT:(tt + 1) * NT],
                                        start=True, stop=True)
                    return f
                deps_dm = [ev_xdbB, ld_wdt, ev_exd]
                if idx >= 2:
                    deps_dm.append(ev_m[idx - 2])
                    if ev_u[idx - 2] is not None:
                        deps_dm.append(ev_u[idx - 2])
                mm = p.op("tensor", mk_dm(), deps=deps_dm, sem="TE", inc=1)
                if idx == 0:
                    ev_m, ev_u, ev_dt = [], [], []
                ev_m.append(p.op("vector", lambda v, j=j, pb=pb, bo=bo: v.tensor_scalar(
                    mt[:, bo:bo + NT], ps[pb][:, :], bdt[:, j:j + 1], 0.0,
                    ALU.add, ALU.max), deps=[mm, ld_bdt] +
                    ([ev_dt[idx - 2]] if idx >= 2 else []), sem="VE", inc=1))
                if D2_STAGE == 4:
                    ev_u.append(None)
                    ev_dt.append(p.op("vector", lambda v, bo=bo: v.tensor_copy(
                        dtt[:, bo:bo + NT], mt[:, bo:bo + NT]),
                        deps=[ev_m[idx]] +
                        ([dt_wr[idx - 2]] if idx >= 2 else []), sem="VE", inc=1))
                else:
                    ev_u.append(p.op("scalar", lambda s, j=j, pb=pb, bo=bo: s.activation(
                        ut[:, bo:bo + NT], ps[pb][:, :], AF.Abs, bias=bdt[:, j:j + 1]),
                        deps=[mm, ld_bdt, ev_m[idx]] +
                        ([ev_dt[idx - 2]] if idx >= 2 else []),
                        sem="SE", inc=1))
                    ev_e = p.op("scalar", lambda s, bo=bo: s.activation(
                        et[:, bo:bo + NT], ut[:, bo:bo + NT], AF.Exp, scale=-1.0),
                        deps=[ev_u[idx]], sem="SE", inc=1)
                    ev_l = p.op("scalar", lambda s, bo=bo: s.activation(
                        et[:, bo:bo + NT], et[:, bo:bo + NT], AF.Ln, bias=1.0),
                        deps=[ev_e], sem="SE", inc=1)
                    ev_dt.append(p.op("vector", lambda v, bo=bo: v.tensor_add(
                        dtt[:, bo:bo + NT], mt[:, bo:bo + NT], et[:, bo:bo + NT]),
                        deps=[ev_l, ev_m[idx]] +
                        ([dt_wr[idx - 2]] if idx >= 2 else []), sem="VE", inc=1))
                dt_wr.append(p.dma(lambda s, j=j, tt=tt, bo=bo: s.dma_start(
                    out=dto[j * P:(j + 1) * P, tt * NT:(tt + 1) * NT],
                    in_=dtt[:, bo:bo + NT]), deps=[ev_dt[idx]], stream="dtw"))
        wr_evs += dt_wr + xc_wr

        p.op("sync", lambda s: None, deps=wr_evs)
        p.emit()
    return nc


def build_E():
    nc = bass.Bass("TRN2", target_bir_lowering=False)
    CH = 2048
    NGC = CH // P
    NS = 16
    NTE = 2048
    NTAU = NSEQ // NTE
    xc = nc.dram_tensor("xc", [CH, NSEQ], F32, kind="ExternalInput").ap()
    dt = nc.dram_tensor("dt", [CH, NSEQ], F32, kind="ExternalInput").ap()
    sz = nc.dram_tensor("sz", [CH, NSEQ], BF16, kind="ExternalInput").ap()
    bcb = nc.dram_tensor("bcb", [32, NSEQ], BF16, kind="ExternalInput").ap()
    A_arr = nc.dram_tensor("A_arr", [P, NGC * NS], F32, kind="ExternalInput").ap()
    D_c = nc.dram_tensor("D_c", [P, NGC], F32, kind="ExternalInput").ap()
    yzo = nc.dram_tensor("yz", [CH, NSEQ], BF16, kind="ExternalOutput").ap()

    import contextlib
    with contextlib.ExitStack() as ctx:
        sbt = lambda n, sh, dt_: ctx.enter_context(nc.sbuf_tensor(n, sh, dt_))
        xct = sbt("xct", [P, 2 * NTE], F32)
        dtt = sbt("dtt", [P, 2 * NTE], F32)
        szt = sbt("szt", [P, NTE], BF16)
        dut = sbt("dut", [P, NTE], F32)
        dub = sbt("dub", [P, NTE], BF16)
        dat = sbt("dat", [P, NTE], F32)
        wt = sbt("wt", [P, NTE], BF16)
        ht = sbt("ht", [P, NTE], BF16)
        yab = sbt("yab", [P, NTE], BF16)
        yft = dut
        yzt = sbt("yzt", [P, 2 * NTE], BF16)
        Bbc = sbt("Bbc", [P, NS * NTE], BF16)
        Cbc = sbt("Cbc", [P, NS * NTE], BF16)
        Aar = sbt("Aar", [P, NGC * NS], F32)
        Am = sbt("Am", [P, NGC * NS], F32)
        Dc = sbt("Dc", [P, NGC], F32)
        hst = sbt("hst", [P, NGC * NS], BF16)
        p = Prog(nc, ctx)

        ld_A = p.dma(lambda s: s.dma_start(out=Aar[:], in_=A_arr[:]))
        ld_D = p.dma(lambda s: s.dma_start(out=Dc[:], in_=D_c[:]))
        ev_A1 = p.op("scalar", lambda s: s.activation(Am[:], Aar[:], AF.Exp),
                     deps=[ld_A], sem="SE", inc=1)
        ev_A = p.op("vector", lambda v: v.tensor_scalar_mul(Am[:], Am[:], -1.0),
                    deps=[ev_A1], sem="VE", inc=1)

        yz_wr = []
        prev_tau_done = {}
        g_last_scan = None
        g_last_hup = None
        g_last_yp = None
        g_last_yadd = None
        for tau in range(NTAU):
            t0 = tau * NTE
            bb_evs = []
            for s_ in range(NS):
                deps_bb = []
                if tau > 0:
                    deps_bb = [prev_tau_done[tau - 1]]
                bb_evs.append(p.dma(lambda s, s_=s_, t0=t0: s.dma_start(
                    out=Bbc[:, s_ * NTE:(s_ + 1) * NTE],
                    in_=bcb[s_:s_ + 1, t0:t0 + NTE].partition_broadcast(P)),
                    deps=deps_bb, stream=f"bb{s_}"))
                bb_evs.append(p.dma(lambda s, s_=s_, t0=t0: s.dma_start(
                    out=Cbc[:, s_ * NTE:(s_ + 1) * NTE],
                    in_=bcb[16 + s_:16 + s_ + 1, t0:t0 + NTE].partition_broadcast(P)),
                    deps=deps_bb, stream=f"cb{s_}"))
            for g in range(NGC):
                bx = (g % 2) * NTE
                idx = tau * NGC + g
                deps_ld = []
                if idx >= 2:
                    deps_ld.append(lastread_evs[idx - 2])
                ld_xc = p.dma(lambda s, g=g, t0=t0, bx=bx: s.dma_start(
                    out=xct[:, bx:bx + NTE], in_=xc[g * P:(g + 1) * P, t0:t0 + NTE]),
                    deps=deps_ld, stream="xc")
                ld_dt = p.dma(lambda s, g=g, t0=t0, bx=bx: s.dma_start(
                    out=dtt[:, bx:bx + NTE], in_=dt[g * P:(g + 1) * P, t0:t0 + NTE]),
                    deps=deps_ld, stream="dt")
                deps_ldz = []
                if idx >= 1:
                    deps_ldz.append(yz_evs[idx - 1])
                ld_sz = p.dma(lambda s, g=g, t0=t0: s.dma_start(
                    out=szt[:], in_=sz[g * P:(g + 1) * P, t0:t0 + NTE]),
                    deps=deps_ldz, stream="sz")

                if idx == 0:
                    lastread_evs, yz_evs = [], []
                ev_du = p.op("vector", lambda v, bx=bx: v.tensor_mul(
                    dut[:], dtt[:, bx:bx + NTE], xct[:, bx:bx + NTE]),
                    deps=[ld_xc, ld_dt], sem="VE", inc=1)
                ev_dub = p.op("vector", lambda v: v.tensor_copy(dub[:], dut[:]),
                              deps=[ev_du], sem="VE", inc=1)

                prev_scan = g_last_scan
                for s_ in range(NS):
                    col = g * NS + s_
                    deps_da = [ld_dt, ev_A]
                    if prev_scan is not None:
                        deps_da.append(prev_scan)
                    da_ev = p.op("scalar", lambda s, bx=bx, col=col: s.activation(
                        dat[:], dtt[:, bx:bx + NTE], AF.Exp, scale=Am[:, col:col + 1]),
                        deps=deps_da, sem="SE", inc=1)
                    deps_w = [ev_dub, bb_evs[2 * s_]]
                    if prev_scan is not None:
                        deps_w.append(prev_scan)
                    w_ev = p.op("vector", lambda v, s_=s_: v.tensor_mul(
                        wt[:], dub[:], Bbc[:, s_ * NTE:(s_ + 1) * NTE]),
                        deps=deps_w, sem="VE", inc=1)
                    init = 0.0 if tau == 0 else hst[:, col:col + 1]

                    def mk_scan(init=init):
                        def f(v):
                            return v.tensor_tensor_scan(ht[:], dat[:], wt[:], init,
                                                        ALU.mult, ALU.add)
                        return f
                    deps_sc = [da_ev, w_ev]
                    if tau > 0:
                        deps_sc.append(hup_evs[(tau - 1, col)])
                    deps_sc += [g_last_hup, g_last_yp]
                    scan_ev = p.op("vector", mk_scan(), deps=deps_sc, sem="VE", inc=1)
                    if tau == 0 and g == 0 and s_ == 0:
                        hup_evs = {}
                    deps_h = [scan_ev]
                    if tau > 0:
                        deps_h.append(hup_evs[(tau - 1, col)])
                    hup_evs[(tau, col)] = p.op("vector", lambda v, col=col: v.tensor_copy(
                        hst[:, col:col + 1], ht[:, NTE - 1:NTE]),
                        deps=deps_h, sem="VE", inc=1)
                    if s_ == 0:
                        yp_ev = p.op("vector", lambda v, s_=s_: v.tensor_mul(
                            yab[:], ht[:], Cbc[:, s_ * NTE:(s_ + 1) * NTE]),
                            deps=[scan_ev, bb_evs[2 * s_ + 1]], sem="VE", inc=1)
                        ya_ev = yp_ev
                    else:
                        yp_ev = p.op("vector", lambda v, s_=s_: v.tensor_mul(
                            wt[:], ht[:], Cbc[:, s_ * NTE:(s_ + 1) * NTE]),
                            deps=[scan_ev, bb_evs[2 * s_ + 1]], sem="VE", inc=1)
                        ya_ev = p.op("vector", lambda v: v.tensor_add(
                            yab[:], yab[:], wt[:]), deps=[yp_ev], sem="VE", inc=1)
                    g_last_hup = hup_evs[(tau, col)]
                    g_last_yp = yp_ev
                    prev_scan = scan_ev
                g_last_scan = prev_scan
                g_last_yadd = ya_ev
                ev_y2 = p.op("vector", lambda v, g=g, bx=bx: v.scalar_tensor_tensor(
                    yft[:], xct[:, bx:bx + NTE], Dc[:, g:g + 1], yab[:],
                    ALU.mult, ALU.add), deps=[g_last_yadd, ld_D], sem="VE", inc=1)
                yz_evs.append(p.op("vector", lambda v, bx=bx: v.tensor_mul(
                    yzt[:, bx:bx + NTE], yft[:], szt[:]),
                    deps=[ev_y2, ld_sz] + ([yz_wr[idx - 2]] if idx >= 2 else []),
                    sem="VE", inc=1))
                lastread_evs.append(yz_evs[idx])
                yz_wr.append(p.dma(lambda s, g=g, t0=t0, bx=bx: s.dma_start(
                    out=yzo[g * P:(g + 1) * P, t0:t0 + NTE], in_=yzt[:, bx:bx + NTE]),
                    deps=[yz_evs[idx]], stream="yzw"))
            prev_tau_done[tau] = yz_evs[tau * NGC + NGC - 1]

        p.op("sync", lambda s: None, deps=yz_wr)
        p.emit()
    return nc


def build_F1():
    nc = bass.Bass("TRN2", target_bir_lowering=False)
    x3 = nc.dram_tensor("x3", [D, TLOC], F32, kind="ExternalInput").ap()
    yz = nc.dram_tensor("yz", [DP, TLOC], BF16, kind="ExternalInput").ap()
    wout_arr = nc.dram_tensor("wout_arr", [D, DP], BF16, kind="ExternalInput").ap()
    x4o = nc.dram_tensor("x4", [D, TLOC], F32, kind="ExternalOutput").ap()
    KD = DP // P

    import contextlib
    with contextlib.ExitStack() as ctx:
        sbt = lambda n, sh, dt: ctx.enter_context(nc.sbuf_tensor(n, sh, dt))
        yzb = sbt("yzb", [P, KD * NT], BF16)
        wst = sbt("wst", [P, 2 * DP], BF16)
        xrt = sbt("xrt", [P, 2 * NT], F32)
        x4t = sbt("x4t", [P, 2 * NT], F32)
        ps = [ctx.enter_context(nc.psum_tensor(f"ps{i}", [P, NT], F32)) for i in range(4)]
        p = Prog(nc, ctx)
        wr_evs = []

        x4_wr = []
        for tt in range(NTT):
            def mk_ldyz(tt=tt):
                def f(s, h):
                    for k in range(KD):
                        s.dma_start(out=yzb[:, k * NT:(k + 1) * NT],
                                    in_=yz[k * P:(k + 1) * P, tt * NT:(tt + 1) * NT]
                                    ).then_inc(h, 16)
                    return None
                return f
            deps_yz = [] if tt == 0 else [mm_last]
            yz_ev = p.dma(mk_ldyz(), deps=deps_yz, stream="yzl", n=KD)
            for g in range(KC):
                idx = tt * KC + g
                pb = idx % 4
                deps_st = [] if idx < 2 else [mm_evs2[idx - 2]]
                st_ld = p.dma(lambda s, g=g, idx=idx: s.dma_start(
                    out=wst[:, (idx % 2) * DP:(idx % 2) * DP + DP],
                    in_=wout_arr[g * P:(g + 1) * P, :]), deps=deps_st, stream="wst")
                deps_xr = [] if idx < 2 else [ev_x4[idx - 2]]
                xr_ld = p.dma(lambda s, g=g, tt=tt, idx=idx: s.dma_start(
                    out=xrt[:, (idx % 2) * NT:(idx % 2) * NT + NT],
                    in_=x3[g * P:(g + 1) * P, tt * NT:(tt + 1) * NT]),
                    deps=deps_xr, stream="xr")

                def mk_mm(g=g, tt=tt, idx=idx, pb=pb):
                    def f(t):
                        last = None
                        for k in range(KD):
                            last = t.matmul(ps[pb][:, :],
                                            wst[:, (idx % 2) * DP + k * P:(idx % 2) * DP + (k + 1) * P],
                                            yzb[:, k * NT:(k + 1) * NT],
                                            start=(k == 0), stop=(k == KD - 1))
                        return last
                    return f
                deps_m = [st_ld, yz_ev]
                if idx == 0:
                    mm_evs2, ev_x4 = [], []
                if idx >= 4:
                    deps_m.append(ev_x4[idx - 4])
                mm_evs2.append(p.op("tensor", mk_mm(), deps=deps_m, sem="TE", inc=1))

                def mk_e(idx=idx, pb=pb):
                    def f(v):
                        return v.tensor_add(x4t[:, (idx % 2) * NT:(idx % 2) * NT + NT],
                                            ps[pb][:, :],
                                            xrt[:, (idx % 2) * NT:(idx % 2) * NT + NT])
                    return f
                deps_e = [mm_evs2[idx], xr_ld]
                if idx >= 2:
                    deps_e.append(x4_wr[idx - 2])
                ev_x4.append(p.op("vector", mk_e(), deps=deps_e, sem="VE", inc=1))
                x4_wr.append(p.dma(lambda s, g=g, tt=tt, idx=idx: s.dma_start(
                    out=x4o[g * P:(g + 1) * P, tt * NT:(tt + 1) * NT],
                    in_=x4t[:, (idx % 2) * NT:(idx % 2) * NT + NT]),
                    deps=[ev_x4[idx]], stream="x4w"))
            mm_last = mm_evs2[tt * KC + KC - 1]
        wr_evs += x4_wr

        p.op("sync", lambda s: None, deps=wr_evs)
        p.emit()
    return nc


def run_B(inp, wb, xhat1):
    nc = _get("B", build_B)
    wq_arr = np.ascontiguousarray(
        wb["wqT"].reshape(KC, P, KC, P).transpose(2, 1, 0, 3).reshape(D, D))
    in_maps = []
    for c in range(NCORE):
        n0 = (c % 2) * TLOC
        pos = n0 + np.arange(TLOC, dtype=np.float32)
        pos_col = (n0 + np.arange(TLOC, dtype=np.float32)).reshape(KC, P).T
        in_maps.append({
            "xhat1": xhat1[c], "wq_arr": wq_arr, "wkT": wb["wkT"], "wvT": wb["wvT"],
            "bq": inp["L0_attn_bq"].reshape(1, D).astype(np.float32),
            "bk": inp["L0_attn_bk"].reshape(1, D).astype(np.float32),
            "bv": inp["L0_attn_bv"].reshape(1, D).astype(np.float32),
            "pos_row": pos.reshape(1, TLOC),
            "pos_col": np.ascontiguousarray(pos_col),
        })
    res = run_bass_kernel_spmd(nc, in_maps, core_ids=list(range(NCORE)))
    q = [res.results[c]["q"] for c in range(NCORE)]
    kvp = [res.results[c]["kvp"] for c in range(NCORE)]
    return q, kvp


def run_C(inp, wb, xcore, q, kvp):
    ncc1 = _get("C1", build_C1)
    wo_arr = np.ascontiguousarray(
        wb["woT"].reshape(KC, P, KC, P).transpose(2, 1, 0, 3).reshape(D, D))
    in_maps = []
    for c in range(NCORE):
        in_maps.append({
            "x": xcore[c], "q": q[c], "kvp_s": kvp[c], "kvp_p": kvp[c ^ 1],
            "wo_arr": wo_arr,
            "bo": inp["L0_attn_bo"].reshape(1, D).astype(np.float32),
        })
    res = run_bass_kernel_spmd(ncc1, in_maps, core_ids=list(range(NCORE)))
    x2 = [res.results[c]["x2"] for c in range(NCORE)]

    ncc2 = _get("C2", build_C2)
    NI = INTER // P
    g_arr = np.ascontiguousarray(
        wb["g0T"].reshape(KC, P, NI, P).transpose(2, 1, 0, 3).reshape(INTER, D))
    u_arr = np.ascontiguousarray(
        wb["u0T"].reshape(KC, P, NI, P).transpose(2, 1, 0, 3).reshape(INTER, D))
    d_arr = np.ascontiguousarray(
        wb["d0T"].reshape(NI, P, KC, P).transpose(2, 1, 0, 3).reshape(D, INTER))
    gs = np.ascontiguousarray(inp["L0_ffn_gate_s"].reshape(NI, P).T.astype(np.float32))
    us = np.ascontiguousarray(inp["L0_ffn_up_s"].reshape(NI, P).T.astype(np.float32))
    ds = np.ascontiguousarray(inp["L0_ffn_down_s"].reshape(KC, P).T.astype(np.float32))
    in_maps = []
    for c in range(NCORE):
        in_maps.append({"x2": x2[c], "g_arr": g_arr, "u_arr": u_arr, "d_arr": d_arr,
                        "gs_c": gs, "us_c": us, "ds_c": ds})
    res = run_bass_kernel_spmd(ncc2, in_maps, core_ids=list(range(NCORE)))
    x3 = [res.results[c]["x3"] for c in range(NCORE)]
    return x2, x3


def run_D(inp, wb, x3):
    ncd1 = _get("D1", build_D1)
    win_arr = np.ascontiguousarray(
        wb["win"].reshape(KC, P, 64, P).transpose(2, 1, 0, 3).reshape(2 * DP, D))
    in_maps = [{"x3": x3[c], "win_arr": win_arr} for c in range(NCORE)]
    res = run_bass_kernel_spmd(ncd1, in_maps, core_ids=list(range(NCORE)))
    xp = [res.results[c]["xp"] for c in range(NCORE)]
    sz = [res.results[c]["sz"] for c in range(NCORE)]

    ncd2 = _get("D2", build_D2)
    cw = np.asarray(inp["L1_attn_conv_w"], dtype=np.float32).reshape(DP, 3)
    cw_c = np.concatenate([cw[:, k].reshape(32, P).T for k in range(3)], axis=1)
    cb_c = np.asarray(inp["L1_attn_conv_b"], dtype=np.float32).reshape(32, P).T
    bdt_c = np.asarray(inp["L1_attn_b_dt"], dtype=np.float32).reshape(32, P).T
    zcol = np.zeros((DP, 1), np.float32)
    in_maps = []
    for c in range(NCORE):
        b, h = c // 2, c % 2
        left = zcol if h == 0 else xp[2 * b][:, TLOC - 1:TLOC]
        right = zcol if h == 1 else xp[2 * b + 1][:, 0:1]
        xpe = np.ascontiguousarray(np.concatenate([left, xp[c], right], axis=1))
        in_maps.append({"xpe": xpe, "cw_c": np.ascontiguousarray(cw_c),
                        "cb_c": np.ascontiguousarray(cb_c),
                        "wx_b": wb["wx"], "wdt_b": wb["wdt"],
                        "bdt_c": np.ascontiguousarray(bdt_c)})
    res = run_bass_kernel_spmd(ncd2, in_maps, core_ids=list(range(NCORE)))
    xc = [res.results[c]["xc"] for c in range(NCORE)]
    dt = [res.results[c]["dt"] for c in range(NCORE)]
    bc = [res.results[c]["bcb"] for c in range(NCORE)]
    return xp, sz, xc, dt, bc


def run_E(inp, xc, dt, sz, bc):
    nce = _get("E", build_E)
    A_log = np.asarray(inp["L1_attn_A_log"], dtype=np.float32)
    Dv = np.asarray(inp["L1_attn_D"], dtype=np.float32)
    in_maps = []
    for c in range(NCORE):
        b, h = c // 2, c % 2
        xcf = np.concatenate([xc[2 * b], xc[2 * b + 1]], axis=1)[h * 2048:(h + 1) * 2048]
        dtf = np.concatenate([dt[2 * b], dt[2 * b + 1]], axis=1)[h * 2048:(h + 1) * 2048]
        szf = np.concatenate([sz[2 * b], sz[2 * b + 1]], axis=1)[h * 2048:(h + 1) * 2048]
        bcf = np.concatenate([bc[2 * b], bc[2 * b + 1]], axis=1)
        Asl = A_log[h * 2048:(h + 1) * 2048].reshape(16, P, 16).transpose(1, 0, 2).reshape(P, 256)
        Dsl = Dv[h * 2048:(h + 1) * 2048].reshape(16, P).T
        in_maps.append({"xc": np.ascontiguousarray(xcf), "dt": np.ascontiguousarray(dtf),
                        "sz": np.ascontiguousarray(szf), "bcb": np.ascontiguousarray(bcf),
                        "A_arr": np.ascontiguousarray(Asl), "D_c": np.ascontiguousarray(Dsl)})
    res = run_bass_kernel_spmd(nce, in_maps, core_ids=list(range(NCORE)))
    return [res.results[c]["yz"] for c in range(NCORE)]


def run_F(inp, wb, x3, yz):
    ncf1 = _get("F1", build_F1)
    wout_arr = np.ascontiguousarray(
        wb["wout"].reshape(32, P, KC, P).transpose(2, 1, 0, 3).reshape(D, DP))
    in_maps = []
    for c in range(NCORE):
        b, h = c // 2, c % 2
        yzf = np.concatenate([yz[2 * b], yz[2 * b + 1]], axis=0)[:, h * 2048:(h + 1) * 2048]
        in_maps.append({"x3": x3[c], "yz": np.ascontiguousarray(yzf), "wout_arr": wout_arr})
    res = run_bass_kernel_spmd(ncf1, in_maps, core_ids=list(range(NCORE)))
    x4 = [res.results[c]["x4"] for c in range(NCORE)]

    ncf2 = _get("C2", build_C2)
    NI = INTER // P
    g_arr = np.ascontiguousarray(
        wb["qgT"].reshape(KC, P, NI, P).transpose(2, 1, 0, 3).reshape(INTER, D))
    u_arr = np.ascontiguousarray(
        wb["quT"].reshape(KC, P, NI, P).transpose(2, 1, 0, 3).reshape(INTER, D))
    d_arr = np.ascontiguousarray(
        wb["qdT"].reshape(NI, P, KC, P).transpose(2, 1, 0, 3).reshape(D, INTER))
    gs = np.ascontiguousarray(inp["L1_ffn_s_gate"].reshape(NI, P).T.astype(np.float32))
    us = np.ascontiguousarray(inp["L1_ffn_s_up"].reshape(NI, P).T.astype(np.float32))
    ds = np.ascontiguousarray(inp["L1_ffn_s_down"].reshape(KC, P).T.astype(np.float32))
    in_maps = [{"x2": x4[c], "g_arr": g_arr, "u_arr": u_arr, "d_arr": d_arr,
                "gs_c": gs, "us_c": us, "ds_c": ds} for c in range(NCORE)]
    res = run_bass_kernel_spmd(ncf2, in_maps, core_ids=list(range(NCORE)))
    return [res.results[c]["x3"] for c in range(NCORE)]


def kernel(**inp):
    xcore = _shard_x(np.asarray(inp["L0_attn_x"], dtype=np.float32))
    xhat1, wb = run_A(inp, xcore)
    q, kvp = run_B(inp, wb, xhat1)
    x2, x3 = run_C(inp, wb, xcore, q, kvp)
    xp, sz, xc, dt, bc = run_D(inp, wb, x3)
    yz = run_E(inp, xc, dt, sz, bc)
    out_c = run_F(inp, wb, x3, yz)
    out = np.empty((4, NSEQ, D), np.float32)
    for c in range(NCORE):
        b, h = c // 2, c % 2
        out[b, h * TLOC:(h + 1) * TLOC, :] = out_c[c].T
    return out


# version 03 (final)
# speedup vs baseline: 1.2852x; 1.2112x over previous
import numpy as np
import ml_dtypes
import concourse.bass as bass
import concourse.mybir as mybir
from concourse.bass_utils import run_bass_kernel_spmd

F32 = mybir.dt.float32
BF16 = mybir.dt.bfloat16
I32 = mybir.dt.int32
AF = mybir.ActivationFunctionType
ALU = mybir.AluOpType

NCORE = 8
P = 128
D = 2048
TLOC = 2048
KC = D // P
NT = 512
NTT = TLOC // NT
EPS_LN = 1e-5
EPS_ATT = 1e-6
NSEQ = 4096
DP = 4096
INTER = 5632
NH = 16
DK = 128

BF = ml_dtypes.bfloat16

WCONV = [
    ("wqT", D, D, "f32"), ("wkT", D, D, "f32"), ("wvT", D, D, "f32"),
    ("woT", D, D, "f32"),
    ("g0T", D, INTER, "f32"), ("u0T", D, INTER, "f32"), ("d0T", INTER, D, "f32"),
    ("win", D, 2 * DP, "f32"), ("wx", DP, 160, "f32"), ("wdt", P, DP, "f32"),
    ("wout", DP, D, "f32"),
    ("qgT", D, INTER, "i32"), ("quT", D, INTER, "i32"), ("qdT", INTER, D, "i32"),
]


class Prog:
    ENGINES = ("sync", "tensor", "vector", "scalar", "gpsimd")

    def __init__(self, nc, ctx):
        self.nc = nc
        self.ctx = ctx
        self.sems = {}
        self.ops = {e: [] for e in self.ENGINES}
        self.sem_count = {}
        self.stream_idx = {}
        self._uniq = 0
        self._last_ev = {}

    def _sem(self, key):
        if key not in self.sems:
            self.sems[key] = self.ctx.enter_context(self.nc.semaphore(key))
            self.sem_count[key] = 0
        return key

    def op(self, eng, fn, deps=(), sem=None, inc=0):
        deps = [d for d in deps if d is not None]
        if eng in ("vector", "scalar", "gpsimd") and eng in self._last_ev:
            deps.append(self._last_ev[eng])
        ev = None
        if sem is not None:
            self._sem(sem)
            self.sem_count[sem] += inc
            ev = (sem, self.sem_count[sem])
        self.ops[eng].append((fn, tuple(deps), sem, inc))
        if eng in ("vector", "scalar", "gpsimd") and ev is not None:
            self._last_ev[eng] = ev
        return ev

    def dma(self, fn, deps=(), stream=None, eng="sync", n=1):
        if stream is None:
            self._uniq += 1
            key = f"dU{self._uniq}"
        else:
            i = self.stream_idx.get(stream, 0)
            self.stream_idx[stream] = i + 1
            key = f"d_{stream}_{i % 2}"
        self._sem(key)
        if n == 1:
            return self.op(eng, fn, deps, key, 16)
        h = self.sems[key]
        return self.op(eng, lambda e, fn=fn, h=h: fn(e, h), deps, key, 16 * n)

    def barrier_ev(self, stream):
        out = []
        for par in (0, 1):
            key = f"d_{stream}_{par}"
            if key in self.sem_count and self.sem_count[key] > 0:
                out.append((key, self.sem_count[key]))
        return out

    def emit(self):
        sems = self.sems

        def body(eng_name):
            def f(eng):
                waited = {}
                for (fn, deps, sem, inc) in self.ops[eng_name]:
                    need = {}
                    for (dsem, dval) in deps:
                        if dval > need.get(dsem, -1):
                            need[dsem] = dval
                    for dsem, dval in need.items():
                        if waited.get(dsem, 0) >= dval:
                            continue
                        eng.wait_ge(sems[dsem], dval)
                        waited[dsem] = dval
                    inst = fn(eng)
                    if sem is not None and inst is not None:
                        inst.then_inc(sems[sem], inc)
            return f

        with self.nc.Block() as block:
            for e in self.ENGINES:
                if self.ops[e]:
                    getattr(block, e)(body(e))


def emit_ln(p, nc, ctx, tag, x_hbm, ncols, psums, out_fn, deps_start=()):
    sb = lambda n, sh, dt: ctx.enter_context(nc.sbuf_tensor(tag + n, sh, dt))
    ntt = ncols // NT if ncols % NT == 0 else None
    assert ntt is not None
    xt = sb("xt", [P, ncols], F32)
    sq = sb("sq", [P, ncols], F32)
    ones_col = sb("ones", [P, 1], F32)
    mrow = sb("mrow", [1, ncols], F32)
    vrow = sb("vrow", [1, ncols], F32)
    mbc = sb("mbc", [P, ncols], F32)
    rbc = sb("rbc", [P, ncols], F32)
    t1 = sb("t1", [P, ncols], F32)
    m2row = t1[0:1, 0:ncols]
    rrow = vrow
    srow = nc.dram_tensor(tag + "srow", [2, ncols], F32, kind="Internal").ap()

    ones_ev = p.op("vector", lambda v: v.memset(ones_col[:], 1.0), sem="VE", inc=1)
    xld, sqd, mm_done = [], [], []
    for k in range(KC):
        deps = list(deps_start)
        if k >= 1:
            deps += [mm_done[k - 1], sqd[k - 1]]
        b = 0
        xld.append(p.dma(lambda s, k=k, b=b: s.dma_start(
            out=xt[:, b:b + ncols], in_=x_hbm[k * P:(k + 1) * P, :]),
            deps=deps, stream=tag + "x"))
        sqd.append(p.op("scalar", lambda s, b=b: s.activation(
            sq[:, b:b + ncols], xt[:, b:b + ncols], AF.Square),
            deps=[xld[k]], sem="SE", inc=1))

        def mk_mm(k=k, b=b):
            def f(t):
                last = None
                for tt in range(ntt):
                    t.matmul(psums[tt][0:1, :], ones_col[:, 0:1],
                             xt[:, b + tt * NT:b + (tt + 1) * NT],
                             start=(k == 0), stop=(k == KC - 1))
                    last = t.matmul(psums[4 + tt][0:1, :], ones_col[:, 0:1],
                                    sq[:, b + tt * NT:b + (tt + 1) * NT],
                                    start=(k == 0), stop=(k == KC - 1))
                return last
            return f
        mm_done.append(p.op("tensor", mk_mm(), deps=[xld[k], sqd[k], ones_ev],
                            sem="TE", inc=1))

    def evac_stats(v):
        for tt in range(ntt):
            v.tensor_copy(mrow[0:1, tt * NT:(tt + 1) * NT], psums[tt][0:1, :])
        for tt in range(ntt - 1):
            v.tensor_copy(vrow[0:1, tt * NT:(tt + 1) * NT], psums[4 + tt][0:1, :])
        return v.tensor_copy(vrow[0:1, (ntt - 1) * NT:ntt * NT], psums[4 + ntt - 1][0:1, :])
    ev_ev = p.op("vector", evac_stats, deps=[mm_done[KC - 1]], sem="VE", inc=1)
    ev_mr = p.op("vector", lambda v: v.tensor_scalar_mul(mrow[:], mrow[:], 1.0 / D),
                 deps=[ev_ev], sem="VE", inc=1)
    ev_st = p.op("vector", lambda v: v.tensor_scalar_mul(vrow[:], vrow[:], 1.0 / D),
                 deps=[ev_ev], sem="VE", inc=1)
    ev_m2 = p.op("scalar", lambda s: s.activation(m2row, mrow[:], AF.Square),
                 deps=[ev_mr], sem="SE", inc=1)
    ev_v1 = p.op("vector", lambda v: v.tensor_sub(vrow[:], vrow[:], m2row),
                 deps=[ev_m2, ev_st], sem="VE", inc=1)
    ev_var = p.op("vector", lambda v: v.tensor_scalar_add(vrow[:], vrow[:], EPS_LN),
                  deps=[ev_v1], sem="VE", inc=1)
    ev_ln = p.op("scalar", lambda s: s.activation(vrow[:], vrow[:], AF.Ln),
                 deps=[ev_var], sem="SE", inc=1)
    ev_lh = p.op("vector", lambda v: v.tensor_scalar_mul(vrow[:], vrow[:], -0.5),
                 deps=[ev_ln], sem="VE", inc=1)
    ev_rstd = p.op("scalar", lambda s: s.activation(vrow[:], vrow[:], AF.Exp),
                   deps=[ev_lh], sem="SE", inc=1)
    ev_sr0 = p.dma(lambda s: s.dma_start(out=srow[0:1, :], in_=mrow[:]), deps=[ev_mr])
    ev_sr1 = p.dma(lambda s: s.dma_start(out=srow[1:2, :], in_=vrow[:]), deps=[ev_rstd])
    ev_mbc = p.dma(lambda s: s.dma_start(out=mbc[:], in_=srow[0:1, :].partition_broadcast(P)),
                   deps=[ev_sr0])
    ev_rbc = p.dma(lambda s: s.dma_start(out=rbc[:], in_=srow[1:2, :].partition_broadcast(P)),
                   deps=[ev_sr1])

    xh_evs, xwr = [], []
    for k in range(KC):
        b = 0
        deps = [mm_done[KC - 1], sqd[KC - 1]]
        if k >= 1:
            deps.append(ev_t1)
        ld2 = p.dma(lambda s, k=k, b=b: s.dma_start(
            out=xt[:, b:b + ncols], in_=x_hbm[k * P:(k + 1) * P, :]),
            deps=deps, stream=tag + "x2")
        out_sb, out_hbm = out_fn(k)

        deps2 = [ld2, ev_mbc]
        if k >= 1:
            deps2.append(ev_x)
        else:
            deps2.append(ev_v1)
        ev_t1 = p.op("vector", lambda v, b=b: v.tensor_sub(t1[:], xt[:, b:b + ncols], mbc[:]),
                     deps=deps2, sem="VE", inc=1)
        deps3 = [ev_t1, ev_rbc]
        if out_hbm is not None and k >= 2:
            deps3.append(xwr[k - 2])
        ev_x = p.op("vector", lambda v, out_sb=out_sb: v.tensor_mul(out_sb, t1[:], rbc[:]),
                    deps=deps3, sem="VE", inc=1)
        if out_hbm is not None:
            xwr.append(p.dma(lambda s, out_hbm=out_hbm, out_sb=out_sb: s.dma_start(
                out=out_hbm, in_=out_sb), deps=[ev_x], stream=tag + "xw"))
            xh_evs.append(xwr[-1])
        else:
            xh_evs.append(ev_x)
    return xh_evs


def build_A():
    nc = bass.Bass("TRN2", target_bir_lowering=False)
    x = nc.dram_tensor("x", [D, TLOC], F32, kind="ExternalInput").ap()
    xhat = nc.dram_tensor("xhat1", [D, TLOC], BF16, kind="ExternalOutput").ap()

    wc_in, wc_out = {}, {}
    for (nm, rows, cols, kd) in WCONV:
        rs = rows // NCORE
        wc_in[nm] = nc.dram_tensor(nm + "_s", [rs, cols], I32 if kd == "i32" else F32,
                                   kind="ExternalInput").ap()
        wc_out[nm] = nc.dram_tensor(nm + "_b", [rs, cols], BF16, kind="ExternalOutput").ap()
    CT = 2048
    conv_tiles = []
    for (nm, rows, cols, kd) in WCONV:
        rs = rows // NCORE
        for r0 in range(0, rs, P):
            nr = min(P, rs - r0)
            for c0 in range(0, cols, CT):
                conv_tiles.append((nm, r0, nr, c0, min(CT, cols - c0), kd))

    import contextlib
    with contextlib.ExitStack() as ctx:
        sb = lambda n, sh, dt: ctx.enter_context(nc.sbuf_tensor(n, sh, dt))
        xh = sb("xh", [P, 2 * TLOC], BF16)
        cvf = sb("cvf", [P, 2 * CT], F32)
        cvi = sb("cvi", [P, 2 * CT], I32)
        cvo = sb("cvo", [P, 2 * CT], BF16)
        psums = [ctx.enter_context(nc.psum_tensor(f"ps{i}", [P, NT], F32)) for i in range(8)]
        p = Prog(nc, ctx)
        wr_evs = []

        def out_fn(k):
            b = (k % 2) * TLOC
            return xh[:, b:b + TLOC], xhat[k * P:(k + 1) * P, :]
        xh_evs = emit_ln(p, nc, ctx, "l1", x, TLOC, psums, out_fn)
        wr_evs += xh_evs

        cvd, cwr = [], []
        for i, (nm, r0, nr, c0, ncol, kd) in enumerate(conv_tiles):
            b = (i % 2) * CT
            deps = []
            if i >= 2:
                deps.append(cvd[i - 2])
            src_ap = wc_in[nm][r0:r0 + nr, c0:c0 + ncol]
            dst = (cvi if kd == "i32" else cvf)[0:nr, b:b + ncol]
            ld = p.dma(lambda s, src_ap=src_ap, dst=dst: s.dma_start(out=dst, in_=src_ap),
                       deps=deps, stream="cv")

            def mk_cv(nm=nm, nr=nr, ncol=ncol, b=b, kd=kd):
                def f(s):
                    if kd == "i32":
                        return s.activation(cvo[0:nr, b:b + ncol], cvi[0:nr, b:b + ncol],
                                            AF.Copy, bias=-128.0)
                    return s.activation(cvo[0:nr, b:b + ncol], cvf[0:nr, b:b + ncol], AF.Copy)
                return f
            deps_c = [ld]
            if i >= 2:
                deps_c.append(cwr[i - 2])
            cvd.append(p.op("scalar", mk_cv(), deps=deps_c, sem="SE", inc=1))
            cwr.append(p.dma(lambda s, nm=nm, r0=r0, nr=nr, c0=c0, ncol=ncol, b=b: s.dma_start(
                out=wc_out[nm][r0:r0 + nr, c0:c0 + ncol], in_=cvo[0:nr, b:b + ncol]),
                deps=[cvd[i]], stream="cw"))
        wr_evs += cwr

        p.op("sync", lambda s: None, deps=wr_evs)
        p.emit()
    return nc


_NC_CACHE = {}


def _get(name, builder):
    if name not in _NC_CACHE:
        _NC_CACHE[name] = builder()
    return _NC_CACHE[name]


def _shard_x(xfull):
    out = []
    for c in range(NCORE):
        b, h = c // 2, c % 2
        sl = xfull[b, h * TLOC:(h + 1) * TLOC, :]
        out.append(np.ascontiguousarray(sl.T, dtype=np.float32))
    return out


def run_A(inp, xcore):
    nc = _get("A", build_A)
    wq = np.ascontiguousarray(inp["L0_attn_Wq"].T)
    wk = np.ascontiguousarray(inp["L0_attn_Wk"].T)
    wv = np.ascontiguousarray(inp["L0_attn_Wv"].T)
    wo = np.ascontiguousarray(inp["L0_attn_Wo"].T)
    g0 = np.ascontiguousarray(inp["L0_ffn_gate_w"].T)
    u0 = np.ascontiguousarray(inp["L0_ffn_up_w"].T)
    d0 = np.ascontiguousarray(inp["L0_ffn_down_w"].T)
    win = np.ascontiguousarray(inp["L1_attn_W_in"])
    wx = np.ascontiguousarray(inp["L1_attn_W_x"])
    wdt = np.ascontiguousarray(inp["L1_attn_W_dt"])
    wout = np.ascontiguousarray(inp["L1_attn_W_out"])
    qg = np.ascontiguousarray(inp["L1_ffn_wq_gate"].T)
    qu = np.ascontiguousarray(inp["L1_ffn_wq_up"].T)
    qd = np.ascontiguousarray(inp["L1_ffn_wq_down"].T)
    full = {"wqT": wq, "wkT": wk, "wvT": wv, "woT": wo, "g0T": g0, "u0T": u0,
            "d0T": d0, "win": win, "wx": wx, "wdt": wdt, "wout": wout,
            "qgT": qg, "quT": qu, "qdT": qd}
    in_maps = []
    for c in range(NCORE):
        m = {"x": xcore[c]}
        for (nm, rows, cols, kd) in WCONV:
            rs = rows // NCORE
            m[nm + "_s"] = np.ascontiguousarray(full[nm][c * rs:(c + 1) * rs])
        in_maps.append(m)
    res = run_bass_kernel_spmd(nc, in_maps, core_ids=list(range(NCORE)))
    wb = {}
    for (nm, rows, cols, kd) in WCONV:
        wb[nm] = np.concatenate([res.results[c][nm + "_b"] for c in range(NCORE)], axis=0)
    xhat1 = [res.results[c]["xhat1"] for c in range(NCORE)]
    return xhat1, wb


def kernel(**inp):
    xcore = _shard_x(np.asarray(inp["L0_attn_x"], dtype=np.float32))
    xhat1, wb = run_A(inp, xcore)
    raise NotImplementedError("launches B..F pending")


def build_B():
    nc = bass.Bass("TRN2", target_bir_lowering=False)
    xhat = nc.dram_tensor("xhat1", [D, TLOC], BF16, kind="ExternalInput").ap()
    wq_arr = nc.dram_tensor("wq_arr", [D, D], BF16, kind="ExternalInput").ap()
    wkT = nc.dram_tensor("wkT", [D, D], BF16, kind="ExternalInput").ap()
    wvT = nc.dram_tensor("wvT", [D, D], BF16, kind="ExternalInput").ap()
    bq = nc.dram_tensor("bq", [1, D], F32, kind="ExternalInput").ap()
    bk = nc.dram_tensor("bk", [1, D], F32, kind="ExternalInput").ap()
    bv = nc.dram_tensor("bv", [1, D], F32, kind="ExternalInput").ap()
    pos_row = nc.dram_tensor("pos_row", [1, TLOC], F32, kind="ExternalInput").ap()
    pos_col = nc.dram_tensor("pos_col", [P, KC], F32, kind="ExternalInput").ap()
    qout = nc.dram_tensor("q", [D, TLOC], BF16, kind="ExternalOutput").ap()
    kvp = nc.dram_tensor("kvp", [P, NH * 129], F32, kind="ExternalOutput").ap()
    drow_h = nc.dram_tensor("drow_h", [1, TLOC], F32, kind="Internal").ap()

    import contextlib
    with contextlib.ExitStack() as ctx:
        sbt = lambda n, sh, dt: ctx.enter_context(nc.sbuf_tensor(n, sh, dt))
        arena = sbt("arena", [P, 2 * KC * TLOC], BF16)
        wkb = arena[:, 0:KC * TLOC]
        wvb = arena[:, KC * TLOC:2 * KC * TLOC]
        xh_tc = sbt("xh_tc", [P, 2 * D], BF16)
        ktile = sbt("ktile", [P, D], BF16)
        VW = NH * 129
        vtile = sbt("vtile", [P, VW], BF16)
        kvsb = sbt("kvsb", [P, NH * 129], F32)
        wq_st = sbt("wq_st", [P, 2 * D], BF16)
        qtile = sbt("qtile", [P, TLOC], BF16)
        brow = sbt("brow", [1, 3 * D], BF16)
        brow_f = sbt("brow_f", [1, D], F32)
        ones_row = sbt("ones_row", [1, NT], BF16)
        posc = sbt("posc", [P, KC], F32)
        drow = sbt("drow", [1, TLOC], F32)
        dcols = sbt("dcols", [P, KC], F32)
        dbc = sbt("dbc", [P, TLOC], F32)
        ps = [ctx.enter_context(nc.psum_tensor(f"ps{i}", [P, NT], F32)) for i in range(8)]
        p = Prog(nc, ctx)
        wr_evs = []

        ev_ones = p.op("vector", lambda v: v.memset(ones_row[:], 1.0), sem="VE", inc=1)
        ev_vones = p.op("vector", lambda v: v.memset(vtile[:], 1.0), sem="VE", inc=1)

        ev_bb = None
        for i, bsrc in enumerate((bq, bk, bv)):
            ld_bi = p.dma(lambda s, bsrc=bsrc: s.dma_start(out=brow_f[:], in_=bsrc[:]),
                          deps=[ev_bb], stream="bia")
            ev_bb = p.op("scalar", lambda s, i=i: s.activation(
                brow[0:1, i * D:(i + 1) * D], brow_f[:], AF.Copy),
                deps=[ld_bi], sem="SE", inc=1)

        ld_pr = p.dma(lambda s: s.dma_start(out=drow[:], in_=pos_row[:]))
        ld_pc = p.dma(lambda s: s.dma_start(out=posc[:], in_=pos_col[:]))
        ev_dr0 = p.op("vector", lambda v: v.tensor_scalar_mul(drow[:], drow[:], -1.0 / NSEQ),
                      deps=[ld_pr], sem="VE", inc=1)
        ev_dr = p.op("scalar", lambda s: s.activation(drow[:], drow[:], AF.Exp),
                     deps=[ev_dr0], sem="SE", inc=1)
        ev_dc0 = p.op("vector", lambda v: v.tensor_scalar_mul(dcols[:], posc[:], -1.0 / NSEQ),
                      deps=[ld_pc], sem="VE", inc=1)
        ev_dc = p.op("scalar", lambda s: s.activation(dcols[:], dcols[:], AF.Exp),
                     deps=[ev_dc0], sem="SE", inc=1)
        ev_dwr = p.dma(lambda s: s.dma_start(out=drow_h[:], in_=drow[:]), deps=[ev_dr])
        ev_dbc = p.dma(lambda s: s.dma_start(
            out=dbc[:], in_=drow_h.partition_broadcast(P)), deps=[ev_dwr])

        wkv_evs = []
        for k in range(KC):
            wkv_evs.append(p.dma(lambda s, k=k: s.dma_start(
                out=arena[:, k * TLOC:(k + 1) * TLOC], in_=wkT[k * P:(k + 1) * P, :]),
                stream="wkv"))
            wkv_evs.append(p.dma(lambda s, k=k: s.dma_start(
                out=arena[:, KC * TLOC + k * TLOC:KC * TLOC + (k + 1) * TLOC],
                in_=wvT[k * P:(k + 1) * P, :]), stream="wkv"))

        NB = 129
        kv_mm, xld = [], []
        ev_evK, ev_evV, kv_add = {}, {}, {}
        for tc in range(KC):
            bx = (tc % 2) * D
            deps = [] if tc < 2 else [kv_mm[tc - 2]]

            def mk_ldx(tc=tc, bx=bx):
                def f(s, h):
                    for k in range(KC):
                        s.dma_start(out=xh_tc[:, bx + k * P:bx + (k + 1) * P],
                                    in_=xhat[k * P:(k + 1) * P, tc * P:(tc + 1) * P]
                                    ).then_inc(h, 16)
                    return None
                return f
            xld.append(p.dma(mk_ldx(), deps=deps, stream="xh", n=KC))

            for dtile in range(4):
                def mk_kv(tc=tc, bx=bx, dtile=dtile):
                    def f(t):
                        t.matmul(ps[0][:, :], ones_row[0:1, 0:P],
                                 brow[0:1, D + dtile * NT:D + (dtile + 1) * NT],
                                 start=True, stop=False)
                        t.matmul(ps[1][:, :], ones_row[0:1, 0:P],
                                 brow[0:1, 2 * D + dtile * NT:2 * D + (dtile + 1) * NT],
                                 start=True, stop=False)
                        last = None
                        for k in range(KC):
                            lhs = xh_tc[:, bx + k * P:bx + (k + 1) * P]
                            t.matmul(ps[0][:, :], lhs,
                                     wkb[:, k * TLOC + dtile * NT:k * TLOC + (dtile + 1) * NT],
                                     start=False, stop=(k == KC - 1))
                            last = t.matmul(ps[1][:, :], lhs,
                                            wvb[:, k * TLOC + dtile * NT:k * TLOC + (dtile + 1) * NT],
                                            start=False, stop=(k == KC - 1))
                        return last
                    return f
                deps_mm = [xld[tc], ev_bb, ev_ones] + wkv_evs
                if dtile > 0:
                    deps_mm += [ev_evK[(tc, dtile - 1)], ev_evV[(tc, dtile - 1)]]
                elif tc > 0:
                    deps_mm += [ev_evK[(tc - 1, 3)], ev_evV[(tc - 1, 3)]]
                if tc == 0 and dtile == 0:
                    deps_mm.append(ev_vones)
                ev_mm = p.op("tensor", mk_kv(), deps=deps_mm, sem="TE", inc=1)

                def mk_evK(tc=tc, dtile=dtile):
                    def f(v):
                        return v.tensor_scalar(
                            ktile[:, dtile * NT:(dtile + 1) * NT], ps[0][:, :],
                            0.0, dcols[:, tc:tc + 1], ALU.max, ALU.mult)
                    return f
                deps_ek = [ev_mm, ev_dc]
                if tc > 0:
                    deps_ek.append(kv_mm[tc - 1])
                ev_evK[(tc, dtile)] = p.op("vector", mk_evK(), deps=deps_ek,
                                           sem="VE", inc=1)

                def mk_evV(tc=tc, dtile=dtile):
                    def f(v):
                        h0 = dtile * 4
                        out = bass.AP(vtile, h0 * NB, [[VW, P], [NB, 4], [1, P]])
                        return v.tensor_copy(out, ps[1][:, :])
                    return f
                deps_ev = [ev_mm]
                if tc > 0:
                    deps_ev.append(kv_mm[tc - 1])
                ev_evV[(tc, dtile)] = p.op("vector", mk_evV(), deps=deps_ev,
                                           sem="VE", inc=1)

            for h in range(NH):
                pb = 2 + (h % 2)

                def mk_kvh(tc=tc, h=h, pb=pb):
                    def f(t):
                        return t.matmul(ps[pb][:, 0:NB], ktile[:, h * P:(h + 1) * P],
                                        vtile[:, h * NB:(h + 1) * NB],
                                        start=True, stop=True)
                    return f
                deps_h = [ev_evK[(tc, 3)], ev_evV[(tc, 3)]]
                if h >= 2:
                    deps_h.append(kv_add[(tc, h - 2)])
                elif tc > 0:
                    deps_h.append(kv_add[(tc - 1, NH - 2 + h)])
                kv_h = p.op("tensor", mk_kvh(), deps=deps_h, sem="TE", inc=1)

                def mk_add(tc=tc, h=h, pb=pb):
                    def f(v):
                        if tc == 0:
                            return v.tensor_copy(kvsb[:, h * NB:(h + 1) * NB], ps[pb][:, 0:NB])
                        return v.tensor_add(kvsb[:, h * NB:(h + 1) * NB],
                                            kvsb[:, h * NB:(h + 1) * NB], ps[pb][:, 0:NB])
                    return f
                kv_add[(tc, h)] = p.op("vector", mk_add(), deps=[kv_h], sem="VE", inc=1)
            kv_mm.append(kv_add[(tc, NH - 1)])

        ev_evkv = kv_add[(KC - 1, NH - 1)]
        kv_wrs = [p.dma(lambda s: s.dma_start(out=kvp[:], in_=kvsb[:]),
                        deps=[ev_evkv], stream="kvw")]
        wr_evs += kv_wrs

        xres_evs = []
        for k in range(KC):
            xres_evs.append(p.dma(lambda s, k=k: s.dma_start(
                out=arena[:, k * TLOC:(k + 1) * TLOC], in_=xhat[k * P:(k + 1) * P, :]),
                deps=[kv_mm[KC - 1]], stream="xr"))
        q_wr, qmm, ev_evq = [], [], []
        for g in range(KC):
            bs = (g % 2) * D
            bq_t = 0
            deps_st = [] if g < 2 else [qmm[g - 2]]
            st_ld = p.dma(lambda s, g=g, bs=bs: s.dma_start(
                out=wq_st[:, bs:bs + D], in_=wq_arr[g * P:(g + 1) * P, :]),
                deps=deps_st, stream="qst")

            def mk_qmm(g=g, bs=bs):
                def f(t):
                    last = None
                    for tt in range(NTT):
                        t.matmul(ps[tt][:, :], brow[0:1, g * P:(g + 1) * P],
                                 ones_row[0:1, 0:NT], start=True, stop=False)
                    for k in range(KC):
                        for tt in range(NTT):
                            last = t.matmul(ps[tt][:, :], wq_st[:, bs + k * P:bs + (k + 1) * P],
                                            arena[:, k * TLOC + tt * NT:k * TLOC + (tt + 1) * NT],
                                            start=False, stop=(k == KC - 1))
                    return last
                return f
            deps_q = [st_ld, ev_bb] + xres_evs
            if g >= 1:
                deps_q.append(ev_evq[g - 1])
            if g == 0:
                deps_q.append(ev_evkv)
            qmm.append(p.op("tensor", mk_qmm(), deps=deps_q, sem="TE", inc=1))

            def mk_evq(g=g, bq_t=bq_t):
                def f(v):
                    last = None
                    for tt in range(NTT):
                        last = v.scalar_tensor_tensor(
                            qtile[:, bq_t + tt * NT:bq_t + (tt + 1) * NT],
                            ps[tt][:, :], 0.0, dbc[:, tt * NT:(tt + 1) * NT],
                            ALU.max, ALU.mult)
                    return last
                return f
            deps_eq = [qmm[g], ev_dbc]
            if g >= 1:
                deps_eq.append(q_wr[g - 1])
            ev_evq.append(p.op("vector", mk_evq(), deps=deps_eq, sem="VE", inc=1))
            q_wr.append(p.dma(lambda s, g=g, bq_t=bq_t: s.dma_start(
                out=qout[g * P:(g + 1) * P, :], in_=qtile[:, bq_t:bq_t + TLOC]),
                deps=[ev_evq[g]], stream="qw"))
        wr_evs += q_wr

        p.op("sync", lambda s: None, deps=wr_evs)
        p.emit()
    return nc


def build_C1():
    nc = bass.Bass("TRN2", target_bir_lowering=False)
    x = nc.dram_tensor("x", [D, TLOC], F32, kind="ExternalInput").ap()
    q = nc.dram_tensor("q", [D, TLOC], BF16, kind="ExternalInput").ap()
    kvp_s = nc.dram_tensor("kvp_s", [P, NH * 129], F32, kind="ExternalInput").ap()
    kvp_p = nc.dram_tensor("kvp_p", [P, NH * 129], F32, kind="ExternalInput").ap()
    wo_arr = nc.dram_tensor("wo_arr", [D, D], BF16, kind="ExternalInput").ap()
    bo = nc.dram_tensor("bo", [1, D], F32, kind="ExternalInput").ap()
    x2o = nc.dram_tensor("x2", [D, TLOC], F32, kind="ExternalOutput").ap()
    zh = nc.dram_tensor("zh", [NH, TLOC], F32, kind="Internal").ap()
    NB = 129

    import contextlib
    with contextlib.ExitStack() as ctx:
        sbt = lambda n, sh, dt: ctx.enter_context(nc.sbuf_tensor(n, sh, dt))
        arena = sbt("arena", [P, KC * TLOC], BF16)
        kv_f = sbt("kv_f", [P, NH * NB], F32)
        kv2 = sbt("kv2", [P, NH * NB], F32)
        kvb = sbt("kvb", [P, NH * NB], BF16)
        zb = sbt("zb", [P, 2 * TLOC], F32)
        dnrow = sbt("dnrow", [1, TLOC], F32)
        wst = sbt("wst", [P, 2 * D], BF16)
        xrt = sbt("xrt", [P, 2 * TLOC], F32)
        x2t = sbt("x2t", [P, 2 * NT], F32)
        brow_f = sbt("brow_f", [1, D], F32)
        bo_b = sbt("bo_b", [1, D], BF16)
        ones_row = sbt("ones_row", [1, NT], BF16)
        ps = [ctx.enter_context(nc.psum_tensor(f"ps{i}", [P, NT], F32)) for i in range(6)]
        p = Prog(nc, ctx)
        wr_evs = []

        ev_ones = p.op("vector", lambda v: v.memset(ones_row[:], 1.0), sem="VE", inc=1)
        ld_bo = p.dma(lambda s: s.dma_start(out=brow_f[:], in_=bo[:]))
        ev_bo = p.op("scalar", lambda s: s.activation(bo_b[:], brow_f[:], AF.Copy),
                     deps=[ld_bo], sem="SE", inc=1)

        ld_kv1 = p.dma(lambda s: s.dma_start(out=kv_f[:], in_=kvp_s[:]))
        ld_kv2 = p.dma(lambda s: s.dma_start(out=kv2[:], in_=kvp_p[:]))
        ev_kvs = p.op("vector", lambda v: v.tensor_add(kv_f[:], kv_f[:], kv2[:]),
                      deps=[ld_kv1, ld_kv2], sem="VE", inc=1)
        ev_kvb = p.op("scalar", lambda s: s.activation(kvb[:], kv_f[:], AF.Copy),
                      deps=[ev_kvs], sem="SE", inc=1)

        q_evs = [p.dma(lambda s, k=k: s.dma_start(
            out=arena[:, k * TLOC:(k + 1) * TLOC], in_=q[k * P:(k + 1) * P, :]),
            stream="q") for k in range(KC)]

        z_wr = []
        ev_dn_prev = None
        dn_ev_evac = {}
        for h in range(NH):
            ev_tt_mm = []
            for tt in range(NTT):
                pb = 4 + (tt % 2)

                def mk_dn1(h=h, tt=tt, pb=pb):
                    def f(t):
                        return t.matmul(ps[pb][0:1, :],
                                        kvb[:, h * NB + 128:h * NB + NB],
                                        arena[:, h * TLOC + tt * NT:h * TLOC + (tt + 1) * NT],
                                        start=True, stop=True)
                    return f
                deps_d = [ev_kvb] + q_evs
                key_prev = (h, tt - 2) if tt >= 2 else ((h - 1, tt + 2) if h > 0 else None)
                if key_prev is not None and key_prev in dn_ev_evac:
                    deps_d.append(dn_ev_evac[key_prev])
                ev_mm = p.op("tensor", mk_dn1(), deps=deps_d, sem="TE", inc=1)
                ev_tt_mm.append(ev_mm)
                deps_ev = [ev_mm]
                if h > 0:
                    deps_ev.append(z_wr[h - 1])
                dn_ev_evac[(h, tt)] = p.op("vector", lambda v, tt=tt, pb=pb: v.tensor_copy(
                    dnrow[0:1, tt * NT:(tt + 1) * NT], ps[pb][0:1, :]),
                    deps=deps_ev, sem="VE", inc=1)

            deps_z0 = [dn_ev_evac[(h, tt)] for tt in range(NTT)]
            if h > 0:
                deps_z0.append(z_wr[h - 1])
            ev_z0 = p.op("vector", lambda v: v.tensor_scalar_add(dnrow[:], dnrow[:], EPS_ATT),
                         deps=deps_z0, sem="VE", inc=1)
            ev_z = p.op("vector", lambda v: v.reciprocal(dnrow[:], dnrow[:]),
                        deps=[ev_z0], sem="VE", inc=1)
            z_wr.append(p.dma(lambda s, h=h: s.dma_start(
                out=zh[h:h + 1, :], in_=dnrow[:]), deps=[ev_z], stream="zw"))
            ev_dn_prev = ev_z
        ev_dn_all = ev_dn_prev

        att_evs = []
        zld = []
        for h in range(NH):
            bz = (h % 2) * TLOC
            deps_z = [z_wr[h]]
            if h >= 2:
                deps_z.append(att_evs[h - 2])
            zld.append(p.dma(lambda s, h=h, bz=bz: s.dma_start(
                out=zb[:, bz:bz + TLOC], in_=zh[h:h + 1, :].partition_broadcast(P)),
                deps=deps_z, stream="zb"))
            ev_prev_tt = None
            for tt in range(NTT):
                pb = tt % 2

                def mk_am(h=h, tt=tt, pb=pb):
                    def f(t):
                        return t.matmul(ps[pb][:, :], kvb[:, h * NB:h * NB + 128],
                                        arena[:, h * TLOC + tt * NT:h * TLOC + (tt + 1) * NT],
                                        start=True, stop=True)
                    return f
                deps_am = [ev_kvb, ev_dn_all]
                if ev_prev_tt is not None:
                    deps_am.append(ev_prev_tt)
                if tt >= 2:
                    deps_am.append(ev_ev_tt[tt - 2])
                elif h > 0:
                    deps_am.append(prev_ev_ev[tt + 2])
                ev_am = p.op("tensor", mk_am(), deps=[d for d in deps_am if d],
                             sem="TE", inc=1)

                def mk_ea(h=h, tt=tt, pb=pb, bz=bz):
                    def f(v):
                        return v.tensor_mul(
                            arena[:, h * TLOC + tt * NT:h * TLOC + (tt + 1) * NT],
                            ps[pb][:, :], zb[:, bz + tt * NT:bz + (tt + 1) * NT])
                    return f
                if tt == 0:
                    ev_ev_tt = [None] * NTT
                ev_ev_tt[tt] = p.op("vector", mk_ea(), deps=[ev_am, zld[h]],
                                    sem="VE", inc=1)
                ev_prev_tt = ev_am
            prev_ev_ev = list(ev_ev_tt)
            att_evs.append(ev_ev_tt[NTT - 1])
        att_all = [e for e in prev_ev_ev] + att_evs

        x2_wr, womm, ev_evo = [], [], []
        xr_ld = []
        for g in range(KC):
            bs = (g % 2) * D
            bx = (g % 2) * TLOC
            deps_st = [] if g < 2 else [womm[g - 2]]
            st_ld = p.dma(lambda s, g=g, bs=bs: s.dma_start(
                out=wst[:, bs:bs + D], in_=wo_arr[g * P:(g + 1) * P, :]),
                deps=deps_st, stream="wst")
            deps_xr = [] if g < 2 else [ev_evo[g - 2]]
            xr_ld.append(p.dma(lambda s, g=g, bx=bx: s.dma_start(
                out=xrt[:, bx:bx + TLOC], in_=x[g * P:(g + 1) * P, :]),
                deps=deps_xr, stream="xr"))

            ev_o_tts = []
            for tt in range(NTT):
                pb = 2 + (tt % 2)

                def mk_wog(g=g, bs=bs, tt=tt, pb=pb):
                    def f(t):
                        t.matmul(ps[pb][:, :], bo_b[0:1, g * P:(g + 1) * P],
                                 ones_row[0:1, 0:NT], start=True, stop=False)
                        last = None
                        for k in range(KC):
                            last = t.matmul(ps[pb][:, :], wst[:, bs + k * P:bs + (k + 1) * P],
                                            arena[:, k * TLOC + tt * NT:k * TLOC + (tt + 1) * NT],
                                            start=False, stop=(k == KC - 1))
                        return last
                    return f
                deps_wo = [st_ld, ev_bo, ev_ones] + att_all
                if tt >= 2:
                    deps_wo.append(ev_o_tts[tt - 2])
                elif g > 0:
                    deps_wo.append(prev_o_tts[tt])
                mm_ev = p.op("tensor", mk_wog(), deps=deps_wo, sem="TE", inc=1)
                if tt == NTT - 1:
                    womm.append(mm_ev)

                def mk_eo(g=g, tt=tt, pb=pb, bx=bx):
                    def f(v):
                        bt2 = (tt % 2) * NT
                        return v.tensor_add(x2t[:, bt2:bt2 + NT], ps[pb][:, :],
                                            xrt[:, bx + tt * NT:bx + (tt + 1) * NT])
                    return f
                deps_eo = [mm_ev, xr_ld[g]]
                if len(x2_wr) >= 2:
                    deps_eo.append(x2_wr[-2])
                ev_e = p.op("vector", mk_eo(), deps=deps_eo, sem="VE", inc=1)
                ev_o_tts.append(ev_e)
                x2_wr.append(p.dma(lambda s, g=g, tt=tt: s.dma_start(
                    out=x2o[g * P:(g + 1) * P, tt * NT:(tt + 1) * NT],
                    in_=x2t[:, (tt % 2) * NT:(tt % 2) * NT + NT]),
                    deps=[ev_e], stream="x2w"))
            prev_o_tts = ev_o_tts
            ev_evo.append(ev_o_tts[NTT - 1])
        wr_evs += x2_wr

        p.op("sync", lambda s: None, deps=wr_evs)
        p.emit()
    return nc


def build_C2():
    nc = bass.Bass("TRN2", target_bir_lowering=False)
    x2 = nc.dram_tensor("x2", [D, TLOC], F32, kind="ExternalInput").ap()
    g_arr = nc.dram_tensor("g_arr", [INTER, D], BF16, kind="ExternalInput").ap()
    u_arr = nc.dram_tensor("u_arr", [INTER, D], BF16, kind="ExternalInput").ap()
    d_arr = nc.dram_tensor("d_arr", [D, INTER], BF16, kind="ExternalInput").ap()
    gs_c = nc.dram_tensor("gs_c", [P, 44], F32, kind="ExternalInput").ap()
    us_c = nc.dram_tensor("us_c", [P, 44], F32, kind="ExternalInput").ap()
    ds_c = nc.dram_tensor("ds_c", [P, KC], F32, kind="ExternalInput").ap()
    x3o = nc.dram_tensor("x3", [D, TLOC], F32, kind="ExternalOutput").ap()
    NI = INTER // P

    import contextlib
    with contextlib.ExitStack() as ctx:
        sbt = lambda n, sh, dt: ctx.enter_context(nc.sbuf_tensor(n, sh, dt))
        arena = sbt("arena", [P, KC * TLOC], BF16)
        hdn = sbt("hdn", [P, NI * NT], BF16)
        wpool = sbt("wpool", [P, 2 * INTER], BF16)
        gst = wpool[:, 0:2 * D]
        ust = wpool[:, 2 * D:4 * D]
        dst = wpool[:, 0:2 * INTER]
        gt = sbt("gt", [P, 2 * NT], BF16)
        ut = sbt("ut", [P, 2 * NT], BF16)
        x2r = sbt("x2r", [P, 2 * NT], F32)
        x3t = sbt("x3t", [P, 2 * NT], F32)
        gsc = sbt("gsc", [P, NI], F32)
        usc = sbt("usc", [P, NI], F32)
        dsc = sbt("dsc", [P, KC], F32)
        ps = [ctx.enter_context(nc.psum_tensor(f"ps{i}", [P, NT], F32)) for i in range(8)]
        p = Prog(nc, ctx)
        wr_evs = []

        ld_gs = p.dma(lambda s: s.dma_start(out=gsc[:], in_=gs_c[:]))
        ld_us = p.dma(lambda s: s.dma_start(out=usc[:], in_=us_c[:]))
        ld_ds = p.dma(lambda s: s.dma_start(out=dsc[:], in_=ds_c[:]))

        def out_fn(k):
            return arena[:, k * TLOC:(k + 1) * TLOC], None
        xh_evs = emit_ln(p, nc, ctx, "l2", x2, TLOC, ps, out_fn)

        x3_wr = []
        last_down_of_tt = {}
        hd_evs = {}
        for tt in range(NTT):
            for i in range(NI):
                bgs = (i % 2) * D

                def mk_ldg(i=i, bgs=bgs):
                    def f(s):
                        return s.dma_start(out=gst[:, bgs:bgs + D],
                                           in_=g_arr[i * P:(i + 1) * P, :])
                    return f

                def mk_ldu(i=i, bgs=bgs):
                    def f(s):
                        return s.dma_start(out=ust[:, bgs:bgs + D],
                                           in_=u_arr[i * P:(i + 1) * P, :])
                    return f
                idx = tt * NI + i
                deps_l = [] if idx < 2 else [gmm_evs[idx - 2]]
                if tt > 0 and i < 2:
                    deps_l = deps_l + [dmm_evs[tt * KC - 1]]
                ld_g = p.dma(mk_ldg(), deps=deps_l, stream="gst")
                ld_u = p.dma(mk_ldu(), deps=deps_l, stream="ust")

                pg, pu = (0, 1) if idx % 2 == 0 else (2, 3)
                bgu = (idx % 2) * NT

                def mk_gu(i=i, tt=tt, bgs=bgs, pg=pg, pu=pu):
                    def f(t):
                        last = None
                        for k in range(KC):
                            t.matmul(ps[pg][:, :], gst[:, bgs + k * P:bgs + (k + 1) * P],
                                     arena[:, k * TLOC + tt * NT:k * TLOC + (tt + 1) * NT],
                                     start=(k == 0), stop=(k == KC - 1))
                            last = t.matmul(ps[pu][:, :], ust[:, bgs + k * P:bgs + (k + 1) * P],
                                            arena[:, k * TLOC + tt * NT:k * TLOC + (tt + 1) * NT],
                                            start=(k == 0), stop=(k == KC - 1))
                        return last
                    return f
                deps_g = [ld_g, ld_u] + xh_evs
                if idx == 0:
                    gmm_evs = []
                    hd_list = []
                if idx >= 2:
                    deps_g.append(hd_list[idx - 2])
                gmm_evs.append(p.op("tensor", mk_gu(), deps=deps_g, sem="TE", inc=1))

                ev_g = p.op("scalar", lambda s, i=i, pg=pg, bgu=bgu: s.activation(
                    gt[:, bgu:bgu + NT], ps[pg][:, :], AF.Silu, scale=gsc[:, i:i + 1]),
                    deps=[gmm_evs[idx], ld_gs], sem="SE", inc=1)
                ev_u = p.op("vector", lambda v, i=i, pu=pu, bgu=bgu: v.tensor_scalar_mul(
                    ut[:, bgu:bgu + NT], ps[pu][:, :], usc[:, i:i + 1]),
                    deps=[gmm_evs[idx], ld_us], sem="VE", inc=1)

                def mk_hd(i=i, tt=tt, bgu=bgu):
                    def f(v):
                        return v.tensor_mul(hdn[:, i * NT:(i + 1) * NT],
                                            gt[:, bgu:bgu + NT], ut[:, bgu:bgu + NT])
                    return f
                deps_h = [ev_g, ev_u]
                if tt > 0:
                    deps_h.append(last_down_of_tt[tt - 1])
                ev_hd = p.op("vector", mk_hd(), deps=deps_h, sem="VE", inc=1)
                hd_list.append(ev_hd)
                hd_evs[(tt, i)] = ev_hd

            for g in range(KC):
                bds = (g % 2) * INTER
                idx2 = tt * KC + g
                deps_ld = [] if idx2 < 2 else [dmm_evs[idx2 - 2]]
                if g < 2:
                    deps_ld = deps_ld + [gmm_evs[tt * NI + NI - 1],
                                         hd_evs[(tt, NI - 1)], hd_evs[(tt, NI - 2)]]
                ld_d = p.dma(lambda s, g=g, bds=bds: s.dma_start(
                    out=dst[:, bds:bds + INTER], in_=d_arr[g * P:(g + 1) * P, :]),
                    deps=deps_ld, stream="dst")
                deps_x2 = [] if idx2 < 2 else [x3_wr[idx2 - 2]]
                ld_x2 = p.dma(lambda s, g=g, tt=tt, bds=bds: s.dma_start(
                    out=x2r[:, (g % 2) * NT:(g % 2) * NT + NT],
                    in_=x2[g * P:(g + 1) * P, tt * NT:(tt + 1) * NT]),
                    deps=deps_x2, stream="x2r")

                def mk_dn(g=g, tt=tt, bds=bds):
                    def f(t):
                        pb = 4 + (g % 2)
                        last = None
                        for i in range(NI):
                            last = t.matmul(ps[pb][:, :], dst[:, bds + i * P:bds + (i + 1) * P],
                                            hdn[:, i * NT:(i + 1) * NT],
                                            start=(i == 0), stop=(i == NI - 1))
                        return last
                    return f
                deps_d = [ld_d] + [hd_evs[(tt, i)] for i in range(NI)]
                if idx2 == 0:
                    dmm_evs = []
                if idx2 >= 2:
                    deps_d.append(ev_x3s[idx2 - 2])
                dmm_evs.append(p.op("tensor", mk_dn(), deps=deps_d, sem="TE", inc=1))

                def mk_x3(g=g, tt=tt):
                    def f(v):
                        pb = 4 + (g % 2)
                        bo2 = (g % 2) * NT
                        return v.scalar_tensor_tensor(
                            x3t[:, bo2:bo2 + NT], ps[pb][:, :], dsc[:, g:g + 1],
                            x2r[:, bo2:bo2 + NT], ALU.mult, ALU.add)
                    return f
                if idx2 == 0:
                    ev_x3s = []
                ev_x3 = p.op("vector", mk_x3(), deps=[dmm_evs[idx2], ld_x2, ld_ds],
                             sem="VE", inc=1)
                ev_x3s.append(ev_x3)
                x3_wr.append(p.dma(lambda s, g=g, tt=tt: s.dma_start(
                    out=x3o[g * P:(g + 1) * P, tt * NT:(tt + 1) * NT],
                    in_=x3t[:, (g % 2) * NT:(g % 2) * NT + NT]),
                    deps=[ev_x3], stream="x3w"))
                last_down_of_tt[tt] = dmm_evs[idx2]
        wr_evs += x3_wr

        p.op("sync", lambda s: None, deps=wr_evs)
        p.emit()
    return nc


def build_D1():
    nc = bass.Bass("TRN2", target_bir_lowering=False)
    x3 = nc.dram_tensor("x3", [D, TLOC], F32, kind="ExternalInput").ap()
    win_arr = nc.dram_tensor("win_arr", [2 * DP, D], BF16, kind="ExternalInput").ap()
    xpo = nc.dram_tensor("xp", [DP, TLOC], F32, kind="ExternalOutput").ap()
    szo = nc.dram_tensor("sz", [DP, TLOC], BF16, kind="ExternalOutput").ap()
    NG = 2 * DP // P

    import contextlib
    with contextlib.ExitStack() as ctx:
        sbt = lambda n, sh, dt: ctx.enter_context(nc.sbuf_tensor(n, sh, dt))
        arena = sbt("arena", [P, KC * TLOC], BF16)
        wst = sbt("wst", [P, 2 * D], BF16)
        xpt = sbt("xpt", [P, 2 * NT], F32)
        szt = sbt("szt", [P, 2 * NT], BF16)
        ps = [ctx.enter_context(nc.psum_tensor(f"ps{i}", [P, NT], F32)) for i in range(8)]
        p = Prog(nc, ctx)
        wr_evs = []

        def out_fn(k):
            return arena[:, k * TLOC:(k + 1) * TLOC], None
        xh_evs = emit_ln(p, nc, ctx, "l3", x3, TLOC, ps, out_fn)

        out_wr = []
        for g in range(NG):
            bs = (g % 2) * D
            idx0 = g * NTT
            deps_st = [] if g < 2 else [mm_evs[(g - 2) * NTT + NTT - 1]]
            st_ld = p.dma(lambda s, g=g, bs=bs: s.dma_start(
                out=wst[:, bs:bs + D], in_=win_arr[g * P:(g + 1) * P, :]),
                deps=deps_st, stream="wst")
            for tt in range(NTT):
                idx = idx0 + tt
                pb = idx % 8

                def mk_mm(g=g, bs=bs, tt=tt, pb=pb):
                    def f(t):
                        last = None
                        for k in range(KC):
                            last = t.matmul(ps[pb][:, :], wst[:, bs + k * P:bs + (k + 1) * P],
                                            arena[:, k * TLOC + tt * NT:k * TLOC + (tt + 1) * NT],
                                            start=(k == 0), stop=(k == KC - 1))
                        return last
                    return f
                deps_m = [st_ld] + xh_evs
                if idx == 0:
                    mm_evs, ev_evs = [], []
                if idx >= 8:
                    deps_m.append(ev_evs[idx - 8])
                mm_evs.append(p.op("tensor", mk_mm(), deps=deps_m, sem="TE", inc=1))

                if g < NG // 2:
                    def mk_ev(g=g, tt=tt, pb=pb):
                        def f(v):
                            bo = (idx % 2) * NT if False else (tt % 2) * NT
                            return v.tensor_copy(xpt[:, (tt % 2) * NT:(tt % 2) * NT + NT],
                                                 ps[pb][:, :])
                        return f
                    deps_e = [mm_evs[idx]]
                    if len(out_wr) >= 2:
                        deps_e.append(out_wr[-2])
                    ev_e = p.op("vector", mk_ev(), deps=deps_e, sem="VE", inc=1)
                    out_wr.append(p.dma(lambda s, g=g, tt=tt: s.dma_start(
                        out=xpo[g * P:(g + 1) * P, tt * NT:(tt + 1) * NT],
                        in_=xpt[:, (tt % 2) * NT:(tt % 2) * NT + NT]),
                        deps=[ev_e], stream="xw"))
                else:
                    def mk_ev(g=g, tt=tt, pb=pb):
                        def f(s):
                            return s.activation(szt[:, (tt % 2) * NT:(tt % 2) * NT + NT],
                                                ps[pb][:, :], AF.Silu)
                        return f
                    deps_e = [mm_evs[idx]]
                    if len(out_wr) >= 2:
                        deps_e.append(out_wr[-2])
                    ev_e = p.op("scalar", mk_ev(), deps=deps_e, sem="SE", inc=1)
                    g2 = g - NG // 2
                    out_wr.append(p.dma(lambda s, g2=g2, tt=tt: s.dma_start(
                        out=szo[g2 * P:(g2 + 1) * P, tt * NT:(tt + 1) * NT],
                        in_=szt[:, (tt % 2) * NT:(tt % 2) * NT + NT]),
                        deps=[ev_e], stream="zw"))
                ev_evs.append(ev_e)
        wr_evs += out_wr

        p.op("sync", lambda s: None, deps=wr_evs)
        p.emit()
    return nc


D2_STAGE = 3
def build_D2():
    nc = bass.Bass("TRN2", target_bir_lowering=False)
    xpe = nc.dram_tensor("xpe", [DP, TLOC + 2], F32, kind="ExternalInput").ap()
    cw_c = nc.dram_tensor("cw_c", [P, 3 * 32], F32, kind="ExternalInput").ap()
    cb_c = nc.dram_tensor("cb_c", [P, 32], F32, kind="ExternalInput").ap()
    wx_b = nc.dram_tensor("wx_b", [DP, 160], BF16, kind="ExternalInput").ap()
    wdt_b = nc.dram_tensor("wdt_b", [P, DP], BF16, kind="ExternalInput").ap()
    bdt_c = nc.dram_tensor("bdt_c", [P, 32], F32, kind="ExternalInput").ap()
    xco = nc.dram_tensor("xc", [DP, TLOC], F32, kind="ExternalOutput").ap()
    dto = nc.dram_tensor("dt", [DP, TLOC], F32, kind="ExternalOutput").ap()
    bco = nc.dram_tensor("bc", [32, TLOC], F32, kind="ExternalOutput").ap()
    bcbo = nc.dram_tensor("bcb", [32, TLOC], BF16, kind="ExternalOutput").ap()
    NG = DP // P
    TE = TLOC + 2

    import contextlib
    with contextlib.ExitStack() as ctx:
        sbt = lambda n, sh, dt: ctx.enter_context(nc.sbuf_tensor(n, sh, dt))
        xpt = sbt("xpt", [P, 2 * TE], F32)
        cwc = sbt("cwc", [P, 3 * 32], F32)
        cbc = sbt("cbc", [P, 32], F32)
        bdt = sbt("bdt", [P, 32], F32)
        c1 = sbt("c1", [P, TLOC], F32)
        c2t = sbt("c2t", [P, TLOC], F32)
        xct = sbt("xct", [P, 2 * TLOC], F32)
        xcb = sbt("xcb", [P, 2 * TLOC], BF16)
        wxt = sbt("wxt", [P, 2 * 160], BF16)
        xdd = sbt("xdd", [P, TLOC], F32)
        xdb = sbt("xdb", [32, TLOC], F32)
        xdbb = sbt("xdbb", [32, TLOC], BF16)
        xdbB = sbt("xdbB", [P, TLOC], BF16)
        wdt_t = sbt("wdt_t", [P, DP], BF16)
        mt = sbt("mt", [P, 2 * NT], F32)
        ut = sbt("ut", [P, 2 * NT], F32)
        et = sbt("et", [P, 2 * NT], F32)
        dtt = sbt("dtt", [P, 2 * NT], F32)
        ps = [ctx.enter_context(nc.psum_tensor(f"ps{i}", [P, NT], F32)) for i in range(8)]
        p = Prog(nc, ctx)
        wr_evs = []

        ld_cw = p.dma(lambda s: s.dma_start(out=cwc[:], in_=cw_c[:]))
        ld_cb = p.dma(lambda s: s.dma_start(out=cbc[:], in_=cb_c[:]))
        ld_bdt = p.dma(lambda s: s.dma_start(out=bdt[:], in_=bdt_c[:]))
        ld_wdt = p.dma(lambda s: s.dma_start(out=wdt_t[:], in_=wdt_b[:]))

        xc_wr = []
        xdbl_done = []
        for g in range(NG):
            bx = (g % 2) * TE
            bc_ = (g % 2) * TLOC
            deps_x = [] if g < 2 else [xcb_ev[g - 2]]
            ld_x = p.dma(lambda s, g=g, bx=bx: s.dma_start(
                out=xpt[:, bx:bx + TE], in_=xpe[g * P:(g + 1) * P, :]),
                deps=deps_x, stream="xp")
            deps_w = [] if g < 2 else [xdbl_done[g - 2]]
            ld_w = p.dma(lambda s, g=g: s.dma_start(
                out=wxt[:, (g % 2) * 160:(g % 2) * 160 + 160],
                in_=wx_b[g * P:(g + 1) * P, :]),
                deps=deps_w, stream="wx")

            deps_c = [ld_x, ld_cw]
            if g >= 1:
                deps_c.append(ev_c2[0] if False else ev_c3)
            ev_c1 = p.op("vector", lambda v, bx=bx, g=g: v.tensor_scalar_mul(
                c1[:], xpt[:, bx:bx + TLOC], cwc[:, g:g + 1]),
                deps=[ld_x, ld_cw] + ([ev_c3] if g >= 1 else []), sem="VE", inc=1)
            ev_c2 = p.op("vector", lambda v, bx=bx, g=g: v.scalar_tensor_tensor(
                c2t[:], xpt[:, bx + 1:bx + 1 + TLOC], cwc[:, 32 + g:32 + g + 1], c1[:],
                ALU.mult, ALU.add), deps=[ev_c1], sem="VE", inc=1)
            ev_c3 = p.op("vector", lambda v, bx=bx, g=g, bc_=bc_: v.scalar_tensor_tensor(
                xct[:, bc_:bc_ + TLOC], xpt[:, bx + 2:bx + 2 + TLOC],
                cwc[:, 64 + g:64 + g + 1], c2t[:], ALU.mult, ALU.add),
                deps=[ev_c2] + ([xc_wr[g - 2]] if g >= 2 else []), sem="VE", inc=1)
            ev_xc = p.op("scalar", lambda s, g=g, bc_=bc_: s.activation(
                xct[:, bc_:bc_ + TLOC], xct[:, bc_:bc_ + TLOC], AF.Silu,
                bias=cbc[:, g:g + 1]), deps=[ev_c3, ld_cb], sem="SE", inc=1)
            xc_wr.append(p.dma(lambda s, g=g, bc_=bc_: s.dma_start(
                out=xco[g * P:(g + 1) * P, :], in_=xct[:, bc_:bc_ + TLOC]),
                deps=[ev_xc], stream="xcw"))
            if g == 0:
                xcb_ev = []
            deps_cb = [ev_xc]
            if g >= 2:
                deps_cb.append(xdbl_done[g - 2])
            xcb_ev.append(p.op("vector", lambda v, g=g, bc_=bc_: v.tensor_copy(
                xcb[:, bc_:bc_ + TLOC], xct[:, bc_:bc_ + TLOC]),
                deps=deps_cb, sem="VE", inc=1))

            def mk_xd(g=g, bc_=bc_):
                def f(t):
                    last = None
                    for tt in range(NTT):
                        t.matmul(ps[tt][:, :],
                                 wxt[:, (g % 2) * 160:(g % 2) * 160 + 128],
                                 xcb[:, bc_ + tt * NT:bc_ + (tt + 1) * NT],
                                 start=(g == 0), stop=(g == NG - 1))
                        last = t.matmul(ps[4 + tt][0:32, :],
                                        wxt[:, (g % 2) * 160 + 128:(g % 2) * 160 + 160],
                                        xcb[:, bc_ + tt * NT:bc_ + (tt + 1) * NT],
                                        start=(g == 0), stop=(g == NG - 1))
                    return last
                return f
            xdbl_done.append(p.op("tensor", mk_xd(), deps=[xcb_ev[g], ld_w],
                                  sem="TE", inc=1))

        if D2_STAGE == 1:
            wr_evs += xc_wr
            p.op("sync", lambda s: None, deps=wr_evs)
            p.emit()
            return nc
        def mk_exd(v):
            for tt in range(NTT):
                v.tensor_copy(xdd[:, tt * NT:(tt + 1) * NT], ps[tt][:, :])
            for tt in range(NTT - 1):
                v.tensor_copy(xdb[0:32, tt * NT:(tt + 1) * NT], ps[4 + tt][0:32, :])
            return v.tensor_copy(xdb[0:32, (NTT - 1) * NT:NTT * NT], ps[7][0:32, :])
        ev_exd = p.op("vector", mk_exd, deps=[xdbl_done[NG - 1]], sem="VE", inc=1)
        wr_evs.append(p.dma(lambda s: s.dma_start(out=bco[:], in_=xdb[0:32, :]),
                            deps=[ev_exd]))
        ev_bcb = p.op("scalar", lambda s: s.activation(xdbb[:], xdb[0:32, :], AF.Copy),
                      deps=[ev_exd], sem="SE", inc=1)
        wr_evs.append(p.dma(lambda s: s.dma_start(out=bcbo[:], in_=xdbb[:]),
                            deps=[ev_bcb]))
        ev_xdbB = p.op("scalar", lambda s: s.activation(xdbB[:], xdd[:], AF.Copy),
                       deps=[ev_exd], sem="SE", inc=1)

        if D2_STAGE == 2:
            wr_evs += xc_wr
            p.op("sync", lambda s: None, deps=wr_evs + [ev_xdbB])
            p.emit()
            return nc
        dt_wr = []
        for j in range(NG):
            for tt in range(NTT):
                idx = j * NTT + tt
                pb = idx % 2
                bo = pb * NT

                def mk_dm(j=j, tt=tt, pb=pb):
                    def f(t):
                        return t.matmul(ps[pb][:, :], wdt_t[:, j * P:(j + 1) * P],
                                        xdbB[:, tt * NT:(tt + 1) * NT],
                                        start=True, stop=True)
                    return f
                deps_dm = [ev_xdbB, ld_wdt, ev_exd]
                if idx >= 2:
                    deps_dm.append(ev_m[idx - 2])
                    if ev_u[idx - 2] is not None:
                        deps_dm.append(ev_u[idx - 2])
                mm = p.op("tensor", mk_dm(), deps=deps_dm, sem="TE", inc=1)
                if idx == 0:
                    ev_m, ev_u, ev_dt = [], [], []
                ev_m.append(p.op("vector", lambda v, j=j, pb=pb, bo=bo: v.tensor_scalar(
                    mt[:, bo:bo + NT], ps[pb][:, :], bdt[:, j:j + 1], 0.0,
                    ALU.add, ALU.max), deps=[mm, ld_bdt] +
                    ([ev_dt[idx - 2]] if idx >= 2 else []), sem="VE", inc=1))
                if D2_STAGE == 4:
                    ev_u.append(None)
                    ev_dt.append(p.op("vector", lambda v, bo=bo: v.tensor_copy(
                        dtt[:, bo:bo + NT], mt[:, bo:bo + NT]),
                        deps=[ev_m[idx]] +
                        ([dt_wr[idx - 2]] if idx >= 2 else []), sem="VE", inc=1))
                else:
                    ev_u.append(p.op("scalar", lambda s, j=j, pb=pb, bo=bo: s.activation(
                        ut[:, bo:bo + NT], ps[pb][:, :], AF.Abs, bias=bdt[:, j:j + 1]),
                        deps=[mm, ld_bdt, ev_m[idx]] +
                        ([ev_dt[idx - 2]] if idx >= 2 else []),
                        sem="SE", inc=1))
                    ev_e = p.op("scalar", lambda s, bo=bo: s.activation(
                        et[:, bo:bo + NT], ut[:, bo:bo + NT], AF.Exp, scale=-1.0),
                        deps=[ev_u[idx]], sem="SE", inc=1)
                    ev_l = p.op("scalar", lambda s, bo=bo: s.activation(
                        et[:, bo:bo + NT], et[:, bo:bo + NT], AF.Ln, bias=1.0),
                        deps=[ev_e], sem="SE", inc=1)
                    ev_dt.append(p.op("vector", lambda v, bo=bo: v.tensor_add(
                        dtt[:, bo:bo + NT], mt[:, bo:bo + NT], et[:, bo:bo + NT]),
                        deps=[ev_l, ev_m[idx]] +
                        ([dt_wr[idx - 2]] if idx >= 2 else []), sem="VE", inc=1))
                dt_wr.append(p.dma(lambda s, j=j, tt=tt, bo=bo: s.dma_start(
                    out=dto[j * P:(j + 1) * P, tt * NT:(tt + 1) * NT],
                    in_=dtt[:, bo:bo + NT]), deps=[ev_dt[idx]], stream="dtw"))
        wr_evs += dt_wr + xc_wr

        p.op("sync", lambda s: None, deps=wr_evs)
        p.emit()
    return nc


def build_E():
    nc = bass.Bass("TRN2", target_bir_lowering=False)
    CH = 2048
    NGC = CH // P
    NS = 16
    NTE = 2048
    NTAU = NSEQ // NTE
    xc = nc.dram_tensor("xc", [CH, NSEQ], F32, kind="ExternalInput").ap()
    dt = nc.dram_tensor("dt", [CH, NSEQ], F32, kind="ExternalInput").ap()
    sz = nc.dram_tensor("sz", [CH, NSEQ], BF16, kind="ExternalInput").ap()
    bcb = nc.dram_tensor("bcb", [32, NSEQ], BF16, kind="ExternalInput").ap()
    A_arr = nc.dram_tensor("A_arr", [P, NGC * NS], F32, kind="ExternalInput").ap()
    D_c = nc.dram_tensor("D_c", [P, NGC], F32, kind="ExternalInput").ap()
    yzo = nc.dram_tensor("yz", [CH, NSEQ], BF16, kind="ExternalOutput").ap()

    import contextlib
    with contextlib.ExitStack() as ctx:
        sbt = lambda n, sh, dt_: ctx.enter_context(nc.sbuf_tensor(n, sh, dt_))
        xct = sbt("xct", [P, 2 * NTE], F32)
        dtt = sbt("dtt", [P, 2 * NTE], F32)
        szt = sbt("szt", [P, NTE], BF16)
        dut = sbt("dut", [P, NTE], F32)
        dub = sbt("dub", [P, NTE], BF16)
        dat = sbt("dat", [P, NTE], F32)
        wt = sbt("wt", [P, NTE], BF16)
        ht = sbt("ht", [P, NTE], BF16)
        yab = sbt("yab", [P, NTE], BF16)
        yft = dut
        yzt = sbt("yzt", [P, 2 * NTE], BF16)
        Bbc = sbt("Bbc", [P, NS * NTE], BF16)
        Cbc = sbt("Cbc", [P, NS * NTE], BF16)
        Aar = sbt("Aar", [P, NGC * NS], F32)
        Am = sbt("Am", [P, NGC * NS], F32)
        Dc = sbt("Dc", [P, NGC], F32)
        hst = sbt("hst", [P, NGC * NS], BF16)
        p = Prog(nc, ctx)

        ld_A = p.dma(lambda s: s.dma_start(out=Aar[:], in_=A_arr[:]))
        ld_D = p.dma(lambda s: s.dma_start(out=Dc[:], in_=D_c[:]))
        ev_A1 = p.op("scalar", lambda s: s.activation(Am[:], Aar[:], AF.Exp),
                     deps=[ld_A], sem="SE", inc=1)
        ev_A = p.op("vector", lambda v: v.tensor_scalar_mul(Am[:], Am[:], -1.0),
                    deps=[ev_A1], sem="VE", inc=1)

        yz_wr = []
        prev_tau_done = {}
        g_last_scan = None
        g_last_hup = None
        g_last_yp = None
        g_last_yadd = None
        for tau in range(NTAU):
            t0 = tau * NTE
            bb_evs = []
            for s_ in range(NS):
                deps_bb = []
                if tau > 0:
                    deps_bb = [prev_tau_done[tau - 1]]
                bb_evs.append(p.dma(lambda s, s_=s_, t0=t0: s.dma_start(
                    out=Bbc[:, s_ * NTE:(s_ + 1) * NTE],
                    in_=bcb[s_:s_ + 1, t0:t0 + NTE].partition_broadcast(P)),
                    deps=deps_bb, stream=f"bb{s_}"))
                bb_evs.append(p.dma(lambda s, s_=s_, t0=t0: s.dma_start(
                    out=Cbc[:, s_ * NTE:(s_ + 1) * NTE],
                    in_=bcb[16 + s_:16 + s_ + 1, t0:t0 + NTE].partition_broadcast(P)),
                    deps=deps_bb, stream=f"cb{s_}"))
            for g in range(NGC):
                bx = (g % 2) * NTE
                idx = tau * NGC + g
                deps_ld = []
                if idx >= 2:
                    deps_ld.append(lastread_evs[idx - 2])
                ld_xc = p.dma(lambda s, g=g, t0=t0, bx=bx: s.dma_start(
                    out=xct[:, bx:bx + NTE], in_=xc[g * P:(g + 1) * P, t0:t0 + NTE]),
                    deps=deps_ld, stream="xc")
                ld_dt = p.dma(lambda s, g=g, t0=t0, bx=bx: s.dma_start(
                    out=dtt[:, bx:bx + NTE], in_=dt[g * P:(g + 1) * P, t0:t0 + NTE]),
                    deps=deps_ld, stream="dt")
                deps_ldz = []
                if idx >= 1:
                    deps_ldz.append(yz_evs[idx - 1])
                ld_sz = p.dma(lambda s, g=g, t0=t0: s.dma_start(
                    out=szt[:], in_=sz[g * P:(g + 1) * P, t0:t0 + NTE]),
                    deps=deps_ldz, stream="sz")

                if idx == 0:
                    lastread_evs, yz_evs = [], []
                ev_du = p.op("vector", lambda v, bx=bx: v.tensor_mul(
                    dut[:], dtt[:, bx:bx + NTE], xct[:, bx:bx + NTE]),
                    deps=[ld_xc, ld_dt], sem="VE", inc=1)
                ev_dub = p.op("vector", lambda v: v.tensor_copy(dub[:], dut[:]),
                              deps=[ev_du], sem="VE", inc=1)

                prev_scan = g_last_scan
                for s_ in range(NS):
                    col = g * NS + s_
                    deps_da = [ld_dt, ev_A]
                    if prev_scan is not None:
                        deps_da.append(prev_scan)
                    da_ev = p.op("scalar", lambda s, bx=bx, col=col: s.activation(
                        dat[:], dtt[:, bx:bx + NTE], AF.Exp, scale=Am[:, col:col + 1]),
                        deps=deps_da, sem="SE", inc=1)
                    deps_w = [ev_dub, bb_evs[2 * s_]]
                    if prev_scan is not None:
                        deps_w.append(prev_scan)
                    w_ev = p.op("vector", lambda v, s_=s_: v.tensor_mul(
                        wt[:], dub[:], Bbc[:, s_ * NTE:(s_ + 1) * NTE]),
                        deps=deps_w, sem="VE", inc=1)
                    init = 0.0 if tau == 0 else hst[:, col:col + 1]

                    def mk_scan(init=init):
                        def f(v):
                            return v.tensor_tensor_scan(ht[:], dat[:], wt[:], init,
                                                        ALU.mult, ALU.add)
                        return f
                    deps_sc = [da_ev, w_ev]
                    if tau > 0:
                        deps_sc.append(hup_evs[(tau - 1, col)])
                    deps_sc += [g_last_hup, g_last_yp]
                    scan_ev = p.op("vector", mk_scan(), deps=deps_sc, sem="VE", inc=1)
                    if tau == 0 and g == 0 and s_ == 0:
                        hup_evs = {}
                    deps_h = [scan_ev]
                    if tau > 0:
                        deps_h.append(hup_evs[(tau - 1, col)])
                    hup_evs[(tau, col)] = p.op("vector", lambda v, col=col: v.tensor_copy(
                        hst[:, col:col + 1], ht[:, NTE - 1:NTE]),
                        deps=deps_h, sem="VE", inc=1)
                    if s_ == 0:
                        yp_ev = p.op("vector", lambda v, s_=s_: v.tensor_mul(
                            yab[:], ht[:], Cbc[:, s_ * NTE:(s_ + 1) * NTE]),
                            deps=[scan_ev, bb_evs[2 * s_ + 1]], sem="VE", inc=1)
                        ya_ev = yp_ev
                    else:
                        yp_ev = p.op("vector", lambda v, s_=s_: v.tensor_mul(
                            wt[:], ht[:], Cbc[:, s_ * NTE:(s_ + 1) * NTE]),
                            deps=[scan_ev, bb_evs[2 * s_ + 1]], sem="VE", inc=1)
                        ya_ev = p.op("vector", lambda v: v.tensor_add(
                            yab[:], yab[:], wt[:]), deps=[yp_ev], sem="VE", inc=1)
                    g_last_hup = hup_evs[(tau, col)]
                    g_last_yp = yp_ev
                    prev_scan = scan_ev
                g_last_scan = prev_scan
                g_last_yadd = ya_ev
                ev_y2 = p.op("vector", lambda v, g=g, bx=bx: v.scalar_tensor_tensor(
                    yft[:], xct[:, bx:bx + NTE], Dc[:, g:g + 1], yab[:],
                    ALU.mult, ALU.add), deps=[g_last_yadd, ld_D], sem="VE", inc=1)
                yz_evs.append(p.op("vector", lambda v, bx=bx: v.tensor_mul(
                    yzt[:, bx:bx + NTE], yft[:], szt[:]),
                    deps=[ev_y2, ld_sz] + ([yz_wr[idx - 2]] if idx >= 2 else []),
                    sem="VE", inc=1))
                lastread_evs.append(yz_evs[idx])
                yz_wr.append(p.dma(lambda s, g=g, t0=t0, bx=bx: s.dma_start(
                    out=yzo[g * P:(g + 1) * P, t0:t0 + NTE], in_=yzt[:, bx:bx + NTE]),
                    deps=[yz_evs[idx]], stream="yzw"))
            prev_tau_done[tau] = yz_evs[tau * NGC + NGC - 1]

        p.op("sync", lambda s: None, deps=yz_wr)
        p.emit()
    return nc


def build_F1():
    nc = bass.Bass("TRN2", target_bir_lowering=False)
    x3 = nc.dram_tensor("x3", [D, TLOC], F32, kind="ExternalInput").ap()
    yz = nc.dram_tensor("yz", [DP, TLOC], BF16, kind="ExternalInput").ap()
    wout_arr = nc.dram_tensor("wout_arr", [D, DP], BF16, kind="ExternalInput").ap()
    x4o = nc.dram_tensor("x4", [D, TLOC], F32, kind="ExternalOutput").ap()
    KD = DP // P

    import contextlib
    with contextlib.ExitStack() as ctx:
        sbt = lambda n, sh, dt: ctx.enter_context(nc.sbuf_tensor(n, sh, dt))
        yzb = sbt("yzb", [P, KD * NT], BF16)
        wst = sbt("wst", [P, 2 * DP], BF16)
        xrt = sbt("xrt", [P, 2 * NT], F32)
        x4t = sbt("x4t", [P, 2 * NT], F32)
        ps = [ctx.enter_context(nc.psum_tensor(f"ps{i}", [P, NT], F32)) for i in range(4)]
        p = Prog(nc, ctx)
        wr_evs = []

        x4_wr = []
        for tt in range(NTT):
            def mk_ldyz(tt=tt):
                def f(s, h):
                    for k in range(KD):
                        s.dma_start(out=yzb[:, k * NT:(k + 1) * NT],
                                    in_=yz[k * P:(k + 1) * P, tt * NT:(tt + 1) * NT]
                                    ).then_inc(h, 16)
                    return None
                return f
            deps_yz = [] if tt == 0 else [mm_last]
            yz_ev = p.dma(mk_ldyz(), deps=deps_yz, stream="yzl", n=KD)
            for g in range(KC):
                idx = tt * KC + g
                pb = idx % 4
                deps_st = [] if idx < 2 else [mm_evs2[idx - 2]]
                st_ld = p.dma(lambda s, g=g, idx=idx: s.dma_start(
                    out=wst[:, (idx % 2) * DP:(idx % 2) * DP + DP],
                    in_=wout_arr[g * P:(g + 1) * P, :]), deps=deps_st, stream="wst")
                deps_xr = [] if idx < 2 else [ev_x4[idx - 2]]
                xr_ld = p.dma(lambda s, g=g, tt=tt, idx=idx: s.dma_start(
                    out=xrt[:, (idx % 2) * NT:(idx % 2) * NT + NT],
                    in_=x3[g * P:(g + 1) * P, tt * NT:(tt + 1) * NT]),
                    deps=deps_xr, stream="xr")

                def mk_mm(g=g, tt=tt, idx=idx, pb=pb):
                    def f(t):
                        last = None
                        for k in range(KD):
                            last = t.matmul(ps[pb][:, :],
                                            wst[:, (idx % 2) * DP + k * P:(idx % 2) * DP + (k + 1) * P],
                                            yzb[:, k * NT:(k + 1) * NT],
                                            start=(k == 0), stop=(k == KD - 1))
                        return last
                    return f
                deps_m = [st_ld, yz_ev]
                if idx == 0:
                    mm_evs2, ev_x4 = [], []
                if idx >= 4:
                    deps_m.append(ev_x4[idx - 4])
                mm_evs2.append(p.op("tensor", mk_mm(), deps=deps_m, sem="TE", inc=1))

                def mk_e(idx=idx, pb=pb):
                    def f(v):
                        return v.tensor_add(x4t[:, (idx % 2) * NT:(idx % 2) * NT + NT],
                                            ps[pb][:, :],
                                            xrt[:, (idx % 2) * NT:(idx % 2) * NT + NT])
                    return f
                deps_e = [mm_evs2[idx], xr_ld]
                if idx >= 2:
                    deps_e.append(x4_wr[idx - 2])
                ev_x4.append(p.op("vector", mk_e(), deps=deps_e, sem="VE", inc=1))
                x4_wr.append(p.dma(lambda s, g=g, tt=tt, idx=idx: s.dma_start(
                    out=x4o[g * P:(g + 1) * P, tt * NT:(tt + 1) * NT],
                    in_=x4t[:, (idx % 2) * NT:(idx % 2) * NT + NT]),
                    deps=[ev_x4[idx]], stream="x4w"))
            mm_last = mm_evs2[tt * KC + KC - 1]
        wr_evs += x4_wr

        p.op("sync", lambda s: None, deps=wr_evs)
        p.emit()
    return nc


def run_B(inp, wb, xhat1):
    nc = _get("B", build_B)
    wq_arr = np.ascontiguousarray(
        wb["wqT"].reshape(KC, P, KC, P).transpose(2, 1, 0, 3).reshape(D, D))
    in_maps = []
    for c in range(NCORE):
        n0 = (c % 2) * TLOC
        pos = n0 + np.arange(TLOC, dtype=np.float32)
        pos_col = (n0 + np.arange(TLOC, dtype=np.float32)).reshape(KC, P).T
        in_maps.append({
            "xhat1": xhat1[c], "wq_arr": wq_arr, "wkT": wb["wkT"], "wvT": wb["wvT"],
            "bq": inp["L0_attn_bq"].reshape(1, D).astype(np.float32),
            "bk": inp["L0_attn_bk"].reshape(1, D).astype(np.float32),
            "bv": inp["L0_attn_bv"].reshape(1, D).astype(np.float32),
            "pos_row": pos.reshape(1, TLOC),
            "pos_col": np.ascontiguousarray(pos_col),
        })
    res = run_bass_kernel_spmd(nc, in_maps, core_ids=list(range(NCORE)))
    q = [res.results[c]["q"] for c in range(NCORE)]
    kvp = [res.results[c]["kvp"] for c in range(NCORE)]
    return q, kvp


def run_C(inp, wb, xcore, q, kvp):
    ncc1 = _get("C1", build_C1)
    wo_arr = np.ascontiguousarray(
        wb["woT"].reshape(KC, P, KC, P).transpose(2, 1, 0, 3).reshape(D, D))
    in_maps = []
    for c in range(NCORE):
        in_maps.append({
            "x": xcore[c], "q": q[c], "kvp_s": kvp[c], "kvp_p": kvp[c ^ 1],
            "wo_arr": wo_arr,
            "bo": inp["L0_attn_bo"].reshape(1, D).astype(np.float32),
        })
    res = run_bass_kernel_spmd(ncc1, in_maps, core_ids=list(range(NCORE)))
    x2 = [res.results[c]["x2"] for c in range(NCORE)]

    ncc2 = _get("C2", build_C2)
    NI = INTER // P
    g_arr = np.ascontiguousarray(
        wb["g0T"].reshape(KC, P, NI, P).transpose(2, 1, 0, 3).reshape(INTER, D))
    u_arr = np.ascontiguousarray(
        wb["u0T"].reshape(KC, P, NI, P).transpose(2, 1, 0, 3).reshape(INTER, D))
    d_arr = np.ascontiguousarray(
        wb["d0T"].reshape(NI, P, KC, P).transpose(2, 1, 0, 3).reshape(D, INTER))
    gs = np.ascontiguousarray(inp["L0_ffn_gate_s"].reshape(NI, P).T.astype(np.float32))
    us = np.ascontiguousarray(inp["L0_ffn_up_s"].reshape(NI, P).T.astype(np.float32))
    ds = np.ascontiguousarray(inp["L0_ffn_down_s"].reshape(KC, P).T.astype(np.float32))
    in_maps = []
    for c in range(NCORE):
        in_maps.append({"x2": x2[c], "g_arr": g_arr, "u_arr": u_arr, "d_arr": d_arr,
                        "gs_c": gs, "us_c": us, "ds_c": ds})
    res = run_bass_kernel_spmd(ncc2, in_maps, core_ids=list(range(NCORE)))
    x3 = [res.results[c]["x3"] for c in range(NCORE)]
    return x2, x3


def run_D(inp, wb, x3):
    ncd1 = _get("D1", build_D1)
    win_arr = np.ascontiguousarray(
        wb["win"].reshape(KC, P, 64, P).transpose(2, 1, 0, 3).reshape(2 * DP, D))
    in_maps = [{"x3": x3[c], "win_arr": win_arr} for c in range(NCORE)]
    res = run_bass_kernel_spmd(ncd1, in_maps, core_ids=list(range(NCORE)))
    xp = [res.results[c]["xp"] for c in range(NCORE)]
    sz = [res.results[c]["sz"] for c in range(NCORE)]

    ncd2 = _get("D2", build_D2)
    cw = np.asarray(inp["L1_attn_conv_w"], dtype=np.float32).reshape(DP, 3)
    cw_c = np.concatenate([cw[:, k].reshape(32, P).T for k in range(3)], axis=1)
    cb_c = np.asarray(inp["L1_attn_conv_b"], dtype=np.float32).reshape(32, P).T
    bdt_c = np.asarray(inp["L1_attn_b_dt"], dtype=np.float32).reshape(32, P).T
    zcol = np.zeros((DP, 1), np.float32)
    in_maps = []
    for c in range(NCORE):
        b, h = c // 2, c % 2
        left = zcol if h == 0 else xp[2 * b][:, TLOC - 1:TLOC]
        right = zcol if h == 1 else xp[2 * b + 1][:, 0:1]
        xpe = np.ascontiguousarray(np.concatenate([left, xp[c], right], axis=1))
        in_maps.append({"xpe": xpe, "cw_c": np.ascontiguousarray(cw_c),
                        "cb_c": np.ascontiguousarray(cb_c),
                        "wx_b": wb["wx"], "wdt_b": wb["wdt"],
                        "bdt_c": np.ascontiguousarray(bdt_c)})
    res = run_bass_kernel_spmd(ncd2, in_maps, core_ids=list(range(NCORE)))
    xc = [res.results[c]["xc"] for c in range(NCORE)]
    dt = [res.results[c]["dt"] for c in range(NCORE)]
    bc = [res.results[c]["bcb"] for c in range(NCORE)]
    return xp, sz, xc, dt, bc


def run_E(inp, xc, dt, sz, bc):
    nce = _get("E", build_E)
    A_log = np.asarray(inp["L1_attn_A_log"], dtype=np.float32)
    Dv = np.asarray(inp["L1_attn_D"], dtype=np.float32)
    in_maps = []
    for c in range(NCORE):
        b, h = c // 2, c % 2
        xcf = np.concatenate([xc[2 * b], xc[2 * b + 1]], axis=1)[h * 2048:(h + 1) * 2048]
        dtf = np.concatenate([dt[2 * b], dt[2 * b + 1]], axis=1)[h * 2048:(h + 1) * 2048]
        szf = np.concatenate([sz[2 * b], sz[2 * b + 1]], axis=1)[h * 2048:(h + 1) * 2048]
        bcf = np.concatenate([bc[2 * b], bc[2 * b + 1]], axis=1)
        Asl = A_log[h * 2048:(h + 1) * 2048].reshape(16, P, 16).transpose(1, 0, 2).reshape(P, 256)
        Dsl = Dv[h * 2048:(h + 1) * 2048].reshape(16, P).T
        in_maps.append({"xc": np.ascontiguousarray(xcf), "dt": np.ascontiguousarray(dtf),
                        "sz": np.ascontiguousarray(szf), "bcb": np.ascontiguousarray(bcf),
                        "A_arr": np.ascontiguousarray(Asl), "D_c": np.ascontiguousarray(Dsl)})
    res = run_bass_kernel_spmd(nce, in_maps, core_ids=list(range(NCORE)))
    return [res.results[c]["yz"] for c in range(NCORE)]


def run_F(inp, wb, x3, yz):
    ncf1 = _get("F1", build_F1)
    wout_arr = np.ascontiguousarray(
        wb["wout"].reshape(32, P, KC, P).transpose(2, 1, 0, 3).reshape(D, DP))
    in_maps = []
    for c in range(NCORE):
        b, h = c // 2, c % 2
        yzf = np.concatenate([yz[2 * b], yz[2 * b + 1]], axis=0)[:, h * 2048:(h + 1) * 2048]
        in_maps.append({"x3": x3[c], "yz": np.ascontiguousarray(yzf), "wout_arr": wout_arr})
    res = run_bass_kernel_spmd(ncf1, in_maps, core_ids=list(range(NCORE)))
    x4 = [res.results[c]["x4"] for c in range(NCORE)]

    ncf2 = _get("C2", build_C2)
    NI = INTER // P
    g_arr = np.ascontiguousarray(
        wb["qgT"].reshape(KC, P, NI, P).transpose(2, 1, 0, 3).reshape(INTER, D))
    u_arr = np.ascontiguousarray(
        wb["quT"].reshape(KC, P, NI, P).transpose(2, 1, 0, 3).reshape(INTER, D))
    d_arr = np.ascontiguousarray(
        wb["qdT"].reshape(NI, P, KC, P).transpose(2, 1, 0, 3).reshape(D, INTER))
    gs = np.ascontiguousarray(inp["L1_ffn_s_gate"].reshape(NI, P).T.astype(np.float32))
    us = np.ascontiguousarray(inp["L1_ffn_s_up"].reshape(NI, P).T.astype(np.float32))
    ds = np.ascontiguousarray(inp["L1_ffn_s_down"].reshape(KC, P).T.astype(np.float32))
    in_maps = [{"x2": x4[c], "g_arr": g_arr, "u_arr": u_arr, "d_arr": d_arr,
                "gs_c": gs, "us_c": us, "ds_c": ds} for c in range(NCORE)]
    res = run_bass_kernel_spmd(ncf2, in_maps, core_ids=list(range(NCORE)))
    return [res.results[c]["x3"] for c in range(NCORE)]


def kernel(**inp):
    xcore = _shard_x(np.asarray(inp["L0_attn_x"], dtype=np.float32))
    xhat1, wb = run_A(inp, xcore)
    q, kvp = run_B(inp, wb, xhat1)
    x2, x3 = run_C(inp, wb, xcore, q, kvp)
    xp, sz, xc, dt, bc = run_D(inp, wb, x3)
    yz = run_E(inp, xc, dt, sz, bc)
    out_c = run_F(inp, wb, x3, yz)
    out = np.empty((4, NSEQ, D), np.float32)
    for c in range(NCORE):
        b, h = c // 2, c % 2
        out[b, h * TLOC:(h + 1) * TLOC, :] = out_c[c].T
    return out
